# version 07
# baseline (speedup 1.0000x reference)
import numpy as np
from contextlib import ExitStack

import concourse.bass as bass
import concourse.mybir as mybir
from concourse.bass_utils import run_bass_kernel_spmd

F32 = mybir.dt.float32
AF = mybir.ActivationFunctionType
ALU = mybir.AluOpType
AXX = mybir.AxisListType.X

ENGS = ("sync", "tensor", "vector", "scalar", "gpsimd")

MM_DT = F32

NEGM = -1.0e30

D, FC = 2048, 16
QR, QC = 1536, 12
CKV, KC = 512, 4
H = 16
INTER, IC = 8192, 64
NM = 9
TS = NM * 512
TQ = 1024
EPS_LN, EPS_RMS = 1e-5, 1e-6
SCALE = float(1.0 / np.sqrt(192.0))
FS1 = float(1.0 / (64.0 * np.sqrt(2048.0)))
FS2 = float(1.0 / np.sqrt(8192.0))


class Prog:
    def __init__(self):
        self.q = {e: [] for e in ENGS}
        self.count = {}
        self.last_wait = {e: {} for e in ENGS}

    def inc(self, sem, n=1):
        c = self.count.get(sem.name, 0) + n
        self.count[sem.name] = c
        return c

    def cur(self, sem):
        return self.count.get(sem.name, 0)

    def wait(self, eng, sem, val):
        if val is None or val <= 0:
            return
        if self.last_wait[eng].get(sem.name, 0) >= val:
            return
        self.last_wait[eng][sem.name] = val
        self.q[eng].append(lambda e, s=sem, v=val: e.wait_ge(s, v))

    def waitt(self, eng, tok):
        if tok is not None:
            self.wait(eng, tok[0], tok[1])

    def op(self, eng, sem, fn, wait=()):
        for w in wait:
            if w is None:
                continue
            self.wait(eng, w[0], w[1])
        if sem is None:
            self.q[eng].append(fn)
            return None
        v = self.inc(sem)
        self.q[eng].append(lambda e, f=fn, s=sem: f(e).then_inc(s, 1))
        return v

    def dma(self, eng, sem, out, in_, wait=()):
        for w in wait:
            if w is None:
                continue
            self.wait(eng, w[0], w[1])
        v = self.inc(sem, 16)
        self.q[eng].append(
            lambda e, o=out, i=in_, s=sem: e.dma_start(out=o, in_=i).then_inc(s, 16))
        return (sem, v)

    def run(self, eng_name, engine):
        for fn in self.q[eng_name]:
            fn(engine)


class Bank:
    __slots__ = ("t", "free")

    def __init__(self, t):
        self.t = t
        self.free = None


class Builder:
    def __init__(self, nc):
        self.nc = nc
        self.p = Prog()

    def _mm(self, ap):
        return ap if MM_DT is F32 else ap.bitcast(MM_DT)

    def acq(self, bank):
        if bank.free is not None:
            self.p.wait("tensor", *bank.free)

    def barrier(self):
        snap = [(sm, self.p.cur(sm)) for sm in self.all_sems if self.p.cur(sm) > 0]
        for e in ENGS:
            for sm, vv in snap:
                self.p.wait(e, sm, vv)

    def newton_rsqrt(self, r_out, var_ap, eps, t1, t2, wait=()):
        p, S, V = self.p, self.S, self.V
        sv = p.op("scalar", S, lambda e: e.add(t1, var_ap, eps), wait=wait)
        vv = p.op("vector", V, lambda e: e.reciprocal(t2, t1), wait=[(S, sv)])
        sv = p.op("scalar", S, lambda e: e.sqrt(r_out, t2), wait=[(V, vv), (S, sv)])
        last = None
        for it in range(2):
            s2 = p.op("scalar", S, lambda e: e.square(t2, r_out),
                      wait=[(S, sv)] if it == 0 else [(V, last)])
            p.op("vector", V, lambda e: e.tensor_mul(t2, t1, t2), wait=[(S, s2)])
            p.op("vector", V, lambda e: e.tensor_scalar(t2, t2, -0.5, 1.5,
                                                        ALU.mult, ALU.add))
            last = p.op("vector", V, lambda e: e.tensor_mul(r_out, r_out, t2))
        return last

    def ln_stats(self, x_sb, nfc, width, mean_out, rstd_out, sq_scratch, t1, t2,
                 denom, eps, lv):
        p = self.p
        S, T = self.S, self.T
        b_sum, b_sq = self.banks[0], self.banks[1]
        self.acq(b_sum)
        self.acq(b_sq)
        p.waitt("tensor", lv)
        for f in range(nfc):
            p.op("tensor", None, lambda e, f=f: e.matmul(
                b_sum.t[:, :width], self._mm(self.ones[:, :]),
                self._mm(x_sb[:, f * width:(f + 1) * width]),
                start=(f == 0), stop=(f == nfc - 1)))
        p.waitt("scalar", lv)
        sv = None
        for f in range(nfc):
            sv = p.op("scalar", S, lambda e, f=f: e.square(
                sq_scratch[:, f * width:(f + 1) * width],
                x_sb[:, f * width:(f + 1) * width]))
        p.wait("tensor", S, sv)
        tv = None
        for f in range(nfc):
            tv = p.op("tensor", T if f == nfc - 1 else None, lambda e, f=f: e.matmul(
                b_sq.t[:, :width], self._mm(self.ones[:, :]),
                self._mm(sq_scratch[:, f * width:(f + 1) * width]),
                start=(f == 0), stop=(f == nfc - 1)))
        sv = p.op("scalar", S, lambda e: e.mul(mean_out, b_sum.t[:, :width], 1.0 / denom),
                  wait=[(T, tv)])
        sv2 = p.op("scalar", S, lambda e: e.mul(t1, b_sq.t[:, :width], 1.0 / denom))
        b_sum.free = (S, sv)
        b_sq.free = (S, sv2)
        sv3 = p.op("scalar", S, lambda e: e.square(t2, mean_out), wait=[(S, sv)])
        vsub = p.op("vector", self.V, lambda e: e.tensor_sub(t1, t1, t2),
                    wait=[(S, sv3)])
        return self.newton_rsqrt(rstd_out, t1, eps, t2, sq_scratch[:, 0:width],
                                 wait=[(V := self.V, vsub)])

    def rms_stats(self, get_chunk, nfc, width, rstd_out, t1, t2, t3, denom, eps,
                  bank_idx=0):
        p = self.p
        S, T = self.S, self.T
        b_r = self.banks[bank_idx]
        self.acq(b_r)
        tv = None
        for f in range(nfc):
            sqv = p.op("scalar", S, lambda e, f=f: e.square(t3, get_chunk(f)),
                       wait=[(T, tv)] if tv else [])
            p.wait("tensor", S, sqv)
            tv = p.op("tensor", T, lambda e, f=f: e.matmul(
                b_r.t[:, :width], self._mm(self.ones[:, :]), self._mm(t3),
                start=(f == 0), stop=(f == nfc - 1)))
        sv = p.op("scalar", S, lambda e: e.mul(t1, b_r.t[:, :width], 1.0 / denom),
                  wait=[(T, tv)])
        b_r.free = (S, sv)
        return self.newton_rsqrt(rstd_out, t1, eps, t2, t3, wait=[(S, sv)])

    def build(self):
        nc, p = self.nc, self.p

        def din(name, shape):
            return nc.dram_tensor(name, shape, F32, kind="ExternalInput").ap()

        self.xseq = din("xseq", [D, TS])
        self.xown = din("xown", [D, TQ])
        self.cosq = din("cosq", [128, TQ])
        self.sinq = din("sinq", [128, TQ])
        self.cosk = din("cosk", [32, TS])
        self.sink = din("sink", [32, TS])
        self.deadt = din("deadt", [128, 13])
        self.mdiagd = din("mdiagd", [128, 4 * 512])
        self.onesd = din("onesd", [128, 128])
        self.h128d = din("h128d", [128, 128])
        self.gqd = din("gqd", [128, QC])
        self.gkvd = din("gkvd", [128, KC])
        self.suugd = din("suugd", [128, FC])
        self.sudpd = din("sudpd", [128, IC])
        self.scupd = din("scupd", [128, IC])
        self.scgtd = din("scgtd", [128, IC])
        self.scdpd = din("scdpd", [128, FC])
        self.constsd = din("constsd", [128, 4])
        self.wqd = din("wqd", [FC * 128, QC * 128])
        self.wqu = din("wqu", [QC * 128, 24 * 128])
        self.wkd = din("wkd", [FC * 128, 576])
        self.wkuk = din("wkuk", [H * 128, KC * 128])
        self.wkuv = din("wkuv", [KC * 128, 2048])
        self.wod = din("wod", [16 * 128, 16 * 128])
        self.wup = din("wup", [IC * 128, FC * 128])
        self.wgate = din("wgate", [IC * 128, FC * 128])
        self.wdown = din("wdown", [IC * 128, 16 * 128])
        self.outs = nc.dram_tensor("outs", [D, TQ], F32, kind="ExternalOutput").ap()

        def scr(name, shape):
            return nc.dram_tensor(name, shape, F32).ap()

        self.ckvn_s = scr("ckvn_s", [KC * 128, TS])
        self.krope_s = scr("krope_s", [64, TS])
        self.knope_s = scr("knope_s", [H * 128, TS])
        self.vs_s = scr("vs_s", [TS, 2048])
        self.qs_s = scr("qs_s", [H * 128, TQ])
        self.qrope_s = scr("qrope_s", [H * 64, TQ])
        self.x2_s = scr("x2_s", [D, TQ])
        self.z_s = scr("z_s", [IC * 128, TQ])

        with ExitStack() as ctx:
            E = ctx.enter_context
            self.T = E(nc.semaphore("Tm"))
            self.V = E(nc.semaphore("Vm"))
            self.S = E(nc.semaphore("Sm"))
            self.GB = E(nc.semaphore("Gm"))
            self.pool = [E(nc.semaphore(f"dq{i}")) for i in range(56)]
            self._si = 0
            self.all_sems = [self.T, self.V, self.S, self.GB] + self.pool

            def sems(n):
                r = self.pool[self._si:self._si + n]
                assert len(r) == n, "sem pool exhausted"
                self._si += n
                return r if n > 1 else r[0]

            self.sems = sems
            self.banks = [Bank(E(nc.psum_tensor(f"pb{i}", [128, 512], F32)))
                          for i in range(8)]
            self.ones = E(nc.sbuf_tensor("ones_sb", [128, 128], F32))
            self.h128 = E(nc.sbuf_tensor("h128_sb", [128, 128], F32))
            self.mdiag = E(nc.sbuf_tensor("mdiag_sb", [128, 4 * 512], F32))
            self.dead = E(nc.sbuf_tensor("dead_sb", [128, 13], F32))
            self.gqs = E(nc.sbuf_tensor("gqs_sb", [128, QC], F32))
            self.gkv = E(nc.sbuf_tensor("gkv_sb", [128, KC], F32))
            self.suug = E(nc.sbuf_tensor("suug_sb", [128, FC], F32))
            self.sudp = E(nc.sbuf_tensor("sudp_sb", [128, IC], F32))
            self.scup = E(nc.sbuf_tensor("scup_sb", [128, IC], F32))
            self.scgt = E(nc.sbuf_tensor("scgt_sb", [128, IC], F32))
            self.scdp = E(nc.sbuf_tensor("scdp_sb", [128, FC], F32))
            self.csts = E(nc.sbuf_tensor("csts_sb", [128, 4], F32))
            nc.const_aps.register(F32, EPS_LN, self.csts[:, 0:1])
            nc.const_aps.register(F32, EPS_RMS, self.csts[:, 1:2])
            nc.const_aps.register(F32, 0.0, self.csts[:, 2:3])
            nc.const_aps.register(F32, 1.0, self.csts[:, 3:4])
            self.qn2m = E(nc.sbuf_tensor("qn2m", [128, 1], F32))
            self.qr2m = E(nc.sbuf_tensor("qr2m", [128, 1], F32))
            self.kn2m = E(nc.sbuf_tensor("kn2m", [128, 1], F32))
            self.kr2m = E(nc.sbuf_tensor("kr2m", [128, 1], F32))
            self.negb = E(nc.sbuf_tensor("negb", [128, 1], F32))
            self.biasc = E(nc.sbuf_tensor("biasc", [128, 13], F32))

            self.phase_init()
            with ExitStack() as ph:
                self.phase_kv_down(ph)
            self.barrier()
            with ExitStack() as ph:
                self.phase_kv_up(ph)
            self.barrier()
            with ExitStack() as ph:
                self.phase_q(ph)
            self.barrier()
            self.phase_bias()
            with ExitStack() as ph:
                self.phase_attn(ph)
                self.barrier()
                self.phase_oproj(ph)
            self.barrier()
            with ExitStack() as ph:
                self.phase_ffn1(ph)
            self.barrier()
            with ExitStack() as ph:
                self.phase_ffn2(ph)
            for sm in self.all_sems:
                if p.cur(sm) > 0:
                    p.wait("sync", sm, p.cur(sm))

            with nc.Block() as block:
                @block.sync
                def _(e):
                    p.run("sync", e)

                @block.tensor
                def _(e):
                    p.run("tensor", e)

                @block.vector
                def _(e):
                    p.run("vector", e)

                @block.scalar
                def _(e):
                    p.run("scalar", e)

                @block.gpsimd
                def _(e):
                    p.run("gpsimd", e)
        return nc

    def phase_init(self):
        p, S = self.p, self.S
        LD = self.sems(1)
        self.LD = LD
        iv = None
        for dst, src in [(self.csts, self.constsd), (self.ones, self.onesd),
                         (self.h128, self.h128d), (self.mdiag, self.mdiagd),
                         (self.dead, self.deadt), (self.gqs, self.gqd),
                         (self.gkv, self.gkvd), (self.suug, self.suugd),
                         (self.sudp, self.sudpd), (self.scup, self.scupd),
                         (self.scgt, self.scgtd), (self.scdp, self.scdpd)]:
            iv = p.dma("sync", LD, dst[:, :], src[:, :])
        self._init_done = False

    def init_compute(self, ld_tok):
        p, S = self.p, self.S
        p.waitt("scalar", ld_tok)
        p.op("scalar", S, lambda e: e.mul(self.gqs[:, :], self.gqs[:, :], SCALE))
        p.op("scalar", S, lambda e: e.mul(self.scup[:, :], self.scup[:, :], 64.0))
        p.op("scalar", S, lambda e: e.mul(self.scgt[:, :], self.scgt[:, :], 64.0))
        p.op("scalar", S, lambda e: e.mul(self.scdp[:, :], self.scdp[:, :], FS2))
        p.op("vector", self.V, lambda e: e.memset(self.qn2m[:, :], 0.0))
        p.op("vector", self.V, lambda e: e.memset(self.qr2m[:, :], 0.0))
        p.op("vector", self.V, lambda e: e.memset(self.kn2m[:, :], 0.0))
        p.op("vector", self.V, lambda e: e.memset(self.kr2m[:, :], 0.0))

    def phase_kv_down(self, ph):
        nc, p = self.nc, self.p
        E = ph.enter_context
        T, V, S, LD = self.T, self.V, self.S, self.LD
        XT = self.sems(1)
        CK = self.sems(1)
        CKS = self.sems(1)
        KRS = self.sems(1)
        wkd_sb = E(nc.sbuf_tensor("wkd_sb", [128, FC * 576], F32))
        xt = E(nc.sbuf_tensor("xt", [128, FC * 512], F32))
        xn = E(nc.sbuf_tensor("xn", [128, FC * 512], F32))
        ckvn = E(nc.sbuf_tensor("ckvn", [128, KC * 512], F32))
        stm = E(nc.sbuf_tensor("stm", [128, 512], F32))
        strr = E(nc.sbuf_tensor("strr", [128, 512], F32))
        st1 = E(nc.sbuf_tensor("st1", [128, 512], F32))
        st2 = E(nc.sbuf_tensor("st2", [128, 512], F32))
        nsc = E(nc.sbuf_tensor("nsc", [128, 512], F32))
        coskt = E(nc.sbuf_tensor("coskt", [32, 512], F32))
        sinkt = E(nc.sbuf_tensor("sinkt", [32, 512], F32))
        kt1 = E(nc.sbuf_tensor("kt1", [32, 512], F32))
        kt2 = E(nc.sbuf_tensor("kt2", [32, 512], F32))
        kE = E(nc.sbuf_tensor("kE", [32, 512], F32))
        kO = E(nc.sbuf_tensor("kO", [32, 512], F32))
        ksq = E(nc.sbuf_tensor("ksq", [32, 512], F32))

        wkd_tok = None
        for fc in range(FC):
            wkd_tok = p.dma("sync", LD, wkd_sb[:, fc * 576:(fc + 1) * 576],
                            self.wkd[fc * 128:(fc + 1) * 128, :])
        self.init_compute(wkd_tok)
        p.waitt("tensor", wkd_tok)
        p.waitt("vector", wkd_tok)

        xt_free = None
        xn_free = None
        ck_free = None
        cks_tok = None
        krs_tok = None
        for m in range(NM):
            cols = slice(m * 512, (m + 1) * 512)
            lv = p.dma("sync", XT, xt[:, :].rearrange("p (a t) -> p a t", a=FC),
                       self.xseq[:, cols].rearrange("(a p) t -> p a t", p=128),
                       wait=[xt_free])
            p.dma("sync", CK, coskt[:, :], self.cosk[:, cols], wait=[ck_free])
            ck = p.dma("sync", CK, sinkt[:, :], self.sink[:, cols])
            if xn_free is not None:
                p.wait("scalar", *xn_free)
            rv = self.ln_stats(xt, FC, 512, stm[:, :], strr[:, :], xn,
                               st1[:, :], st2[:, :], D, EPS_LN, lv)
            vv = None
            for f in range(FC):
                p.op("vector", V, lambda e, f=f: e.tensor_sub(
                    xn[:, f * 512:(f + 1) * 512], xt[:, f * 512:(f + 1) * 512],
                    stm[:, :]), wait=[(V, rv)])
                vv = p.op("vector", V, lambda e, f=f: e.tensor_mul(
                    xn[:, f * 512:(f + 1) * 512], xn[:, f * 512:(f + 1) * 512],
                    strr[:, :]))
            xt_free = (V, vv)
            p.wait("tensor", V, vv)
            p.waitt("tensor", wkd_tok)
            p.waitt("tensor", cks_tok)
            for oc in range(KC):
                bk = self.banks[2 + oc]
                self.acq(bk)
                for f in range(FC):
                    p.op("tensor", None, lambda e, f=f, o=oc, b=bk: e.matmul(
                        b.t[:, :],
                        self._mm(wkd_sb[:, f * 576 + o * 128:f * 576 + (o + 1) * 128]),
                        self._mm(xn[:, f * 512:(f + 1) * 512]),
                        start=(f == 0), stop=(f == FC - 1)))
            bE, bO = self.banks[6], self.banks[7]
            self.acq(bE)
            self.acq(bO)
            for f in range(FC):
                p.op("tensor", None, lambda e, f=f: e.matmul(
                    bE.t[0:32, :], self._mm(wkd_sb[:, f * 576 + 512:f * 576 + 544]),
                    self._mm(xn[:, f * 512:(f + 1) * 512]),
                    start=(f == 0), stop=(f == FC - 1)))
            tv = None
            for f in range(FC):
                tv = p.op("tensor", T if f == FC - 1 else None, lambda e, f=f: e.matmul(
                    bO.t[0:32, :], self._mm(wkd_sb[:, f * 576 + 544:f * 576 + 576]),
                    self._mm(xn[:, f * 512:(f + 1) * 512]),
                    start=(f == 0), stop=(f == FC - 1)))
            xn_free = (T, tv)
            sv = None
            for oc in range(KC):
                sv = p.op("scalar", S, lambda e, o=oc: e.copy(
                    ckvn[:, o * 512:(o + 1) * 512], self.banks[2 + o].t[:, :]),
                    wait=[(T, tv)])
            rv = self.rms_stats(
                lambda f: ckvn[:, f * 512:(f + 1) * 512], KC, 512, strr[:, :],
                st1[:, :], st2[:, :], nsc[:, :], CKV, EPS_RMS, bank_idx=0)
            for oc in range(KC):
                cv = p.op("vector", V, lambda e, o=oc: e.scalar_tensor_tensor(
                    ckvn[:, o * 512:(o + 1) * 512], ckvn[:, o * 512:(o + 1) * 512],
                    self.gkv[:, o:o + 1], strr[:, :], ALU.mult, ALU.mult),
                    wait=[(V, rv)])
                self.banks[2 + oc].free = (V, cv)
                cks_tok = p.dma("gpsimd", CKS,
                                self.ckvn_s[oc * 128:(oc + 1) * 128, cols],
                                ckvn[:, oc * 512:(oc + 1) * 512], wait=[(V, cv)])
            p.wait("vector", T, tv)
            p.waitt("vector", ck)
            p.waitt("vector", krs_tok)
            p.op("vector", V, lambda e: e.tensor_mul(kt1[:, :], bE.t[0:32, :], coskt[:, :]))
            p.op("vector", V, lambda e: e.tensor_mul(kt2[:, :], bO.t[0:32, :], sinkt[:, :]))
            v3 = p.op("vector", V, lambda e: e.tensor_sub(kE[:, :], kt1[:, :], kt2[:, :]))
            v4 = p.op("vector", V, lambda e: e.tensor_mul(kt1[:, :], bO.t[0:32, :], coskt[:, :]))
            v5 = p.op("vector", V, lambda e: e.tensor_mul(kt2[:, :], bE.t[0:32, :], sinkt[:, :]))
            v6 = p.op("vector", V, lambda e: e.tensor_add(kO[:, :], kt1[:, :], kt2[:, :]))
            bE.free = (V, v5)
            bO.free = (V, v4)
            ck_free = (V, v5)
            p.dma("gpsimd", KRS, self.krope_s[0:32, cols], kE[:, :], wait=[(V, v3)])
            krs_tok = p.dma("gpsimd", KRS, self.krope_s[32:64, cols], kO[:, :],
                            wait=[(V, v6)])
            s1 = p.op("scalar", S, lambda e: e.square(ksq[:, :], kE[:, :]),
                      wait=[(V, v3)])
            b2 = self.banks[1]
            self.acq(b2)
            p.wait("tensor", S, s1)
            tmid = p.op("tensor", T, lambda e: e.matmul(
                b2.t[:, :], self._mm(self.ones[0:32, :]), self._mm(ksq[:, :]),
                start=True, stop=False))
            s1 = p.op("scalar", S, lambda e: e.square(ksq[:, :], kO[:, :]),
                      wait=[(V, v6), (T, tmid)])
            p.wait("tensor", S, s1)
            tv = p.op("tensor", T, lambda e: e.matmul(
                b2.t[:, :], self._mm(self.ones[0:32, :]), self._mm(ksq[:, :]),
                start=False, stop=True))
            vv = p.op("vector", V, lambda e: e.reduce_max(nsc[:, 0:1], b2.t[:, :],
                                                          axis=AXX), wait=[(T, tv)])
            b2.free = (V, vv)
            p.op("vector", V, lambda e: e.tensor_max(self.kr2m[:, :], self.kr2m[:, :],
                                                     nsc[:, 0:1]))

    def phase_kv_up(self, ph):
        nc, p = self.nc, self.p
        E = ph.enter_context
        T, V, S, LD = self.T, self.V, self.S, self.LD
        WKs = self.sems(2)
        KSs = self.sems(2)
        VSs = self.sems(2)
        ckvr = E(nc.sbuf_tensor("ckvr", [128, KC * TS], F32))
        wkuv_sb = E(nc.sbuf_tensor("wkuv_sb", [128, KC * 2048], F32))
        wkb = [E(nc.sbuf_tensor(f"wkb{i}", [128, 512], F32)) for i in range(2)]
        ktile = [E(nc.sbuf_tensor(f"ktile{i}", [128, 512], F32)) for i in range(2)]
        ksq2 = E(nc.sbuf_tensor("ksq2", [128, 512], F32))
        vev = [E(nc.sbuf_tensor(f"vev{i}", [128, 2048], F32)) for i in range(2)]
        red = E(nc.sbuf_tensor("red", [128, 1], F32))

        loads = None
        for fc in range(KC):
            p.dma("sync", LD, ckvr[:, fc * TS:(fc + 1) * TS],
                  self.ckvn_s[fc * 128:(fc + 1) * 128, :])
        for fc in range(KC):
            loads = p.dma("sync", LD, wkuv_sb[:, fc * 2048:(fc + 1) * 2048],
                          self.wkuv[fc * 128:(fc + 1) * 128, :])

        wk_use = {}
        kspill = {}
        knorm_tv = None
        for h in range(H):
            wb = wkb[h % 2]
            wv = p.dma("sync", WKs[h % 2], wb[:, :],
                       self.wkuk[h * 128:(h + 1) * 128, :],
                       wait=[(T, wk_use.get(h - 2))] if h >= 2 else [])
            for m in range(NM):
                idx = h * NM + m
                cols = slice(m * 512, (m + 1) * 512)
                bk = self.banks[m % 2]
                self.acq(bk)
                p.waitt("tensor", wv)
                p.waitt("tensor", loads)
                tv = None
                for f in range(KC):
                    tv = p.op("tensor", T if f == KC - 1 else None,
                              lambda e, f=f, b=bk, w=wb, m=m: e.matmul(
                        b.t[:, :], self._mm(w[:, f * 128:(f + 1) * 128]),
                        self._mm(ckvr[:, f * TS + m * 512:f * TS + (m + 1) * 512]),
                        start=(f == 0), stop=(f == KC - 1)))
                wk_use[h] = tv
                kt = ktile[idx % 2]
                sv = p.op("scalar", S, lambda e, b=bk, k=kt: e.copy(k[:, :], b.t[:, :]),
                          wait=[(T, tv), kspill.get(idx - 2)])
                bk.free = (S, sv)
                kspill[idx] = p.dma("gpsimd", KSs[idx % 2],
                                    self.knope_s[h * 128:(h + 1) * 128, cols],
                                    kt[:, :], wait=[(S, sv)])
                s2 = p.op("scalar", S, lambda e, k=kt: e.square(ksq2[:, :], k[:, :]),
                          wait=[(T, knorm_tv)] if knorm_tv else [])
                bn = self.banks[2]
                self.acq(bn)
                p.wait("tensor", S, s2)
                knorm_tv = p.op("tensor", T, lambda e, b=bn: e.matmul(
                    b.t[:, :], self._mm(self.ones[:, :]), self._mm(ksq2[:, :]),
                    start=True, stop=True))
                vv = p.op("vector", V, lambda e, b=bn: e.reduce_max(
                    red[:, :], b.t[:, :], axis=AXX), wait=[(T, knorm_tv)])
                bn.free = (V, vv)
                p.op("vector", V, lambda e: e.tensor_max(self.kn2m[:, :],
                                                         self.kn2m[:, :], red[:, :]))

        vspill = {}
        for m in range(NM):
            for ts in range(4):
                idx = m * 4 + ts
                tok0 = m * 512 + ts * 128
                vb = vev[idx % 2]
                evs = None
                for seg in range(4):
                    bk = self.banks[3 + (idx * 4 + seg) % 2]
                    self.acq(bk)
                    p.waitt("tensor", loads)
                    tv = None
                    for f in range(KC):
                        tv = p.op("tensor", T if f == KC - 1 else None,
                                  lambda e, f=f, b=bk, t0=tok0, sg=seg: e.matmul(
                            b.t[:, :], self._mm(ckvr[:, f * TS + t0:f * TS + t0 + 128]),
                            self._mm(wkuv_sb[:, f * 2048 + sg * 512:
                                             f * 2048 + (sg + 1) * 512]),
                            start=(f == 0), stop=(f == KC - 1)))
                    w_ev = [(T, tv)]
                    if seg == 0 and idx >= 2:
                        w_ev.append(vspill.get(idx - 2))
                    evs = p.op("scalar", S, lambda e, b=bk, v=vb, sg=seg: e.copy(
                        v[:, sg * 512:(sg + 1) * 512], b.t[:, :]), wait=w_ev)
                    bk.free = (S, evs)
                vspill[idx] = p.dma("gpsimd", VSs[idx % 2],
                                    self.vs_s[tok0:tok0 + 128, :],
                                    vb[:, :], wait=[(S, evs)])

    def phase_q(self, ph):
        nc, p = self.nc, self.p
        E = ph.enter_context
        T, V, S, LD = self.T, self.V, self.S, self.LD
        XO = self.sems(1)
        WQs = self.sems(2)
        WUs = self.sems(2)
        QS1 = self.sems(1)
        QS2 = self.sems(1)
        xo = E(nc.sbuf_tensor("xo", [128, FC * 512], F32))
        xn = E(nc.sbuf_tensor("xnq", [128, FC * 512], F32))
        cq = E(nc.sbuf_tensor("cq", [128, QC * TQ], F32))
        stm = E(nc.sbuf_tensor("stmq", [128, 512], F32))
        strr = E(nc.sbuf_tensor("strq", [128, 512], F32))
        st1 = E(nc.sbuf_tensor("st1q", [128, 512], F32))
        st2 = E(nc.sbuf_tensor("st2q", [128, 512], F32))
        nsc = E(nc.sbuf_tensor("nscq", [128, 512], F32))
        wbuf = [E(nc.sbuf_tensor(f"wqb{i}", [128, 6 * 128], F32)) for i in range(2)]
        cosq_sb = E(nc.sbuf_tensor("cosq_sb", [128, TQ], F32))
        sinq_sb = E(nc.sbuf_tensor("sinq_sb", [128, TQ], F32))
        qE = E(nc.sbuf_tensor("qE", [128, 4 * 512], F32))
        qO = E(nc.sbuf_tensor("qO", [128, 512], F32))
        qt1 = E(nc.sbuf_tensor("qt1", [128, 512], F32))
        qt2 = E(nc.sbuf_tensor("qt2", [128, 512], F32))
        qsqt = E(nc.sbuf_tensor("qsqt", [128, 512], F32))
        qevac = E(nc.sbuf_tensor("qevac", [128, 512], F32))

        p.dma("sync", LD, cosq_sb[:, :], self.cosq[:, :])
        cv_q = p.dma("sync", LD, sinq_sb[:, :], self.sinq[:, :])

        xo_free = None
        xn_free = None
        wq_free = {}
        for th in range(2):
            colq = slice(th * 512, (th + 1) * 512)
            lv = p.dma("sync", XO, xo[:, :].rearrange("p (a t) -> p a t", a=FC),
                       self.xown[:, colq].rearrange("(a p) t -> p a t", p=128),
                       wait=[xo_free])
            if xn_free is not None:
                p.wait("scalar", *xn_free)
            rv = self.ln_stats(xo, FC, 512, stm[:, :], strr[:, :], xn,
                               st1[:, :], st2[:, :], D, EPS_LN, lv)
            vv = None
            for f in range(FC):
                p.op("vector", V, lambda e, f=f: e.tensor_sub(
                    xn[:, f * 512:(f + 1) * 512], xo[:, f * 512:(f + 1) * 512],
                    stm[:, :]), wait=[(V, rv)])
                vv = p.op("vector", V, lambda e, f=f: e.tensor_mul(
                    xn[:, f * 512:(f + 1) * 512], xn[:, f * 512:(f + 1) * 512],
                    strr[:, :]))
            xo_free = (V, vv)
            p.wait("tensor", V, vv)
            tv = None
            for og in range(2):
                for f in range(FC):
                    wv = p.dma("sync", WQs[f % 2], wbuf[f % 2][:, :],
                               self.wqd[f * 128:(f + 1) * 128, og * 768:(og + 1) * 768],
                               wait=[(T, wq_free.get(f % 2))]
                               if wq_free.get(f % 2) else [])
                    p.waitt("tensor", wv)
                    for i in range(6):
                        bk = self.banks[i]
                        if f == 0:
                            self.acq(bk)
                        tv = p.op(
                            "tensor", T if i == 5 else None,
                            lambda e, f=f, i=i, b=bk, w=wbuf[f % 2]: e.matmul(
                                b.t[:, :], self._mm(w[:, i * 128:(i + 1) * 128]),
                                self._mm(xn[:, f * 512:(f + 1) * 512]),
                                start=(f == 0), stop=(f == FC - 1)))
                    wq_free[f % 2] = tv
                for i in range(6):
                    oc = og * 6 + i
                    sv = p.op("scalar", S, lambda e, o=oc, b=self.banks[i], t=th:
                              e.copy(cq[:, o * TQ + t * 512:o * TQ + t * 512 + 512],
                                     b.t[:, :]), wait=[(T, tv)])
                    self.banks[i].free = (S, sv)
            xn_free = (T, tv)
            rv = self.rms_stats(
                lambda f, t=th: cq[:, f * TQ + t * 512:f * TQ + t * 512 + 512],
                QC, 512,
                strr[:, :], st1[:, :], st2[:, :], nsc[:, :], QR, EPS_RMS, bank_idx=6)
            for oc in range(QC):
                p.op("vector", V, lambda e, o=oc, t=th: e.scalar_tensor_tensor(
                    cq[:, o * TQ + t * 512:o * TQ + t * 512 + 512],
                    cq[:, o * TQ + t * 512:o * TQ + t * 512 + 512],
                    self.gqs[:, o:o + 1], strr[:, :], ALU.mult, ALU.mult),
                    wait=[(V, rv)])
        cq_ready = p.cur(V)
        p.wait("tensor", V, cq_ready)

        wub = [E(nc.sbuf_tensor(f"wub{i}", [128, 6 * 128], F32)) for i in range(2)]
        wu_free = {}
        qs1_tok = None
        qs2_tok = None
        norm_tv = None
        mix_done = None
        for th in range(2):
            ropeE_done = {}
            for og in range(4):
                tv = None
                for f in range(QC):
                    wv = p.dma("sync", WUs[f % 2], wub[f % 2][:, :],
                               self.wqu[f * 128:(f + 1) * 128, og * 768:(og + 1) * 768],
                               wait=[(T, wu_free.get(f % 2))]
                               if wu_free.get(f % 2) else [])
                    p.waitt("tensor", wv)
                    for i in range(6):
                        bk = self.banks[i]
                        if f == 0:
                            self.acq(bk)
                        tv = p.op(
                            "tensor", T if i == 5 else None,
                            lambda e, f=f, i=i, b=bk, w=wub[f % 2], t=th: e.matmul(
                                b.t[:, :], self._mm(w[:, i * 128:(i + 1) * 128]),
                                self._mm(cq[:, f * TQ + t * 512:f * TQ + t * 512 + 512]),
                                start=(f == 0), stop=(f == QC - 1)))
                    wu_free[f % 2] = tv
                for i in range(6):
                    oc = og * 6 + i
                    bk = self.banks[i]
                    if oc < 16:
                        sv = p.op("scalar", S, lambda e, b=bk: e.copy(
                            qevac[:, :], b.t[:, :]), wait=[(T, tv), qs1_tok])
                        qs1_tok = p.dma("gpsimd", QS1,
                                        self.qs_s[oc * 128:(oc + 1) * 128,
                                                  th * 512:(th + 1) * 512],
                                        qevac[:, :], wait=[(S, sv)])
                        bk.free = (S, sv)
                        s2 = p.op("scalar", S, lambda e: e.square(qsqt[:, :], qevac[:, :]),
                                  wait=([(T, norm_tv)] if norm_tv else []) +
                                       ([(V, mix_done)] if mix_done else []))
                        bn = self.banks[7]
                        self.acq(bn)
                        p.wait("tensor", S, s2)
                        norm_tv = p.op("tensor", T, lambda e, b=bn: e.matmul(
                            b.t[:, :], self._mm(self.ones[:, :]), self._mm(qsqt[:, :]),
                            start=True, stop=True))
                        vv = p.op("vector", V, lambda e, b=bn: e.reduce_max(
                            nsc[:, 0:1], b.t[:, :], axis=AXX), wait=[(T, norm_tv)])
                        bn.free = (V, vv)
                        mix_done = p.op("vector", V, lambda e: e.tensor_max(
                            self.qn2m[:, :], self.qn2m[:, :], nsc[:, 0:1]))
                    elif oc < 20:
                        g = oc - 16
                        sv = p.op("scalar", S, lambda e, b=bk, g=g: e.copy(
                            qE[:, g * 512:(g + 1) * 512], b.t[:, :]), wait=[(T, tv)])
                        bk.free = (S, sv)
                        ropeE_done[g] = sv
                    else:
                        g = oc - 20
                        sv = p.op("scalar", S, lambda e, b=bk: e.copy(qO[:, :], b.t[:, :]),
                                  wait=[(T, tv)])
                        bk.free = (S, sv)
                        cs = slice(th * 512, (th + 1) * 512)
                        v1 = p.op("vector", V, lambda e, g=g, cs=cs: e.tensor_mul(
                            qt1[:, :], qE[:, g * 512:(g + 1) * 512], cosq_sb[:, cs]),
                            wait=[(S, sv), (S, ropeE_done[g]), cv_q, qs2_tok] +
                                 ([(T, norm_tv)] if norm_tv else []))
                        p.op("vector", V, lambda e, cs=cs: e.tensor_mul(
                            qt2[:, :], qO[:, :], sinq_sb[:, cs]))
                        v3 = p.op("vector", V, lambda e: e.tensor_sub(
                            qt1[:, :], qt1[:, :], qt2[:, :]))
                        p.op("vector", V, lambda e, cs=cs: e.tensor_mul(
                            qt2[:, :], qO[:, :], cosq_sb[:, cs]))
                        p.op("vector", V, lambda e, g=g, cs=cs: e.tensor_mul(
                            qsqt[:, :], qE[:, g * 512:(g + 1) * 512], sinq_sb[:, cs]))
                        v6 = p.op("vector", V, lambda e: e.tensor_add(
                            qt2[:, :], qt2[:, :], qsqt[:, :]))
                        for hh in range(4):
                            hd = g * 4 + hh
                            p.dma("gpsimd", QS2,
                                  self.qrope_s[hd * 64:hd * 64 + 32, cs],
                                  qt1[hh * 32:(hh + 1) * 32, :], wait=[(V, v3)])
                            qs2_tok = p.dma(
                                "gpsimd", QS2,
                                self.qrope_s[hd * 64 + 32:(hd + 1) * 64, cs],
                                qt2[hh * 32:(hh + 1) * 32, :], wait=[(V, v6)])
                        s2 = p.op("scalar", S, lambda e: e.square(qsqt[:, :], qt1[:, :]),
                                  wait=[(V, v6)])
                        bn = self.banks[7]
                        self.acq(bn)
                        p.wait("tensor", S, s2)
                        tmid = p.op("tensor", T, lambda e, b=bn: e.matmul(
                            b.t[:, :], self._mm(self.ones[:, :]), self._mm(qsqt[:, :]),
                            start=True, stop=False))
                        s2 = p.op("scalar", S, lambda e: e.square(qsqt[:, :], qt2[:, :]),
                                  wait=[(T, tmid)])
                        p.wait("tensor", S, s2)
                        norm_tv = p.op("tensor", T, lambda e, b=bn: e.matmul(
                            b.t[:, :], self._mm(self.ones[:, :]), self._mm(qsqt[:, :]),
                            start=False, stop=True))
                        vv = p.op("vector", V, lambda e, b=bn: e.reduce_max(
                            nsc[:, 0:1], b.t[:, :], axis=AXX), wait=[(T, norm_tv)])
                        bn.free = (V, vv)
                        mix_done = p.op("vector", V, lambda e: e.tensor_max(
                            self.qr2m[:, :], self.qr2m[:, :], nsc[:, 0:1]))

    def phase_bias(self):
        p, S, V = self.p, self.S, self.V
        v2 = p.op("vector", V, lambda e: e.tensor_scalar(
            self.qr2m[:, :], self.qr2m[:, :], 4.0, None, ALU.mult))
        v2 = p.op("vector", V, lambda e: e.tensor_add(
            self.qr2m[:, :], self.qr2m[:, :], self.qn2m[:, :]))
        s1 = p.op("scalar", S, lambda e: e.sqrt(self.qr2m[:, :], self.qr2m[:, :]),
                  wait=[(V, v2)])
        v3 = p.op("vector", V, lambda e: e.tensor_add(
            self.kr2m[:, :], self.kr2m[:, :], self.kn2m[:, :]))
        s2 = p.op("scalar", S, lambda e: e.sqrt(self.kr2m[:, :], self.kr2m[:, :]),
                  wait=[(V, v3)])
        p.op("vector", V, lambda e: e.tensor_mul(
            self.negb[:, :], self.qr2m[:, :], self.kr2m[:, :]), wait=[(S, s1), (S, s2)])
        p.op("vector", V, lambda e: e.tensor_scalar(
            self.negb[:, :], self.negb[:, :], -1.0, None, ALU.mult))
        self.bias_ready = p.op("vector", V, lambda e: e.tensor_scalar(
            self.biasc[:, :], self.dead[:, :], self.negb[:, 0:1], None, ALU.add))

    def phase_attn(self, ph):
        nc, p = self.nc, self.p
        E = ph.enter_context
        T, V, S = self.T, self.V, self.S
        KVs = self.sems(2)
        QL = self.sems(1)
        attn = E(nc.sbuf_tensor("attn", [128, H * TQ], F32))
        self.attn_sb = attn
        kbuf = [E(nc.sbuf_tensor(f"kbuf{i}", [128, 512], F32)) for i in range(2)]
        krb = [E(nc.sbuf_tensor(f"krb{i}", [64, 512], F32)) for i in range(2)]
        vbuf = [E(nc.sbuf_tensor(f"vbuf{i}", [128, 512], F32)) for i in range(2)]
        qnb = [E(nc.sbuf_tensor(f"qnb{i}", [128, 512], F32)) for i in range(2)]
        qrb = [E(nc.sbuf_tensor(f"qrb{i}", [64, 512], F32)) for i in range(2)]
        pT = [E(nc.sbuf_tensor(f"pT{i}", [128, 512], F32)) for i in range(4)]
        sumsb = E(nc.sbuf_tensor("sumsb", [128, 512], F32))
        invb = E(nc.sbuf_tensor("invb", [128, 512], F32))

        p.wait("scalar", V, self.bias_ready)
        bS = [self.banks[0], self.banks[1], self.banks[6], self.banks[7]]
        bOut = {0: self.banks[2], 1: self.banks[3]}
        bSum = {0: self.banks[4], 1: self.banks[5]}

        kv_use = {}
        q_use = None
        exp_done = {}
        pt_free = {}
        sum_fin = None

        for h in range(H):
            qw = [(T, q_use)] if q_use else []
            ql = None
            for s in range(2):
                p.dma("sync", QL, qnb[s][:, :],
                      self.qs_s[h * 128:(h + 1) * 128, s * 512:(s + 1) * 512],
                      wait=qw)
                ql = p.dma("sync", QL, qrb[s][:, :],
                           self.qrope_s[h * 64:(h + 1) * 64, s * 512:(s + 1) * 512])
            out_open = {0: False, 1: False}
            for sl in range(NM):
                ki = sl % 2
                kvw = [(T, kv_use.get((sl - 2) % NM))] if (h > 0 or sl >= 2) else []
                p.dma("sync", KVs[ki], kbuf[ki][:, :],
                      self.knope_s[h * 128:(h + 1) * 128, sl * 512:(sl + 1) * 512],
                      wait=kvw)
                p.dma("sync", KVs[ki], krb[ki][:, :],
                      self.krope_s[:, sl * 512:(sl + 1) * 512])
                kl = p.dma("sync", KVs[ki],
                           vbuf[ki][:, :].rearrange("p (a v) -> p a v", a=4),
                           self.vs_s[sl * 512:(sl + 1) * 512,
                                     h * 128:(h + 1) * 128].rearrange(
                                         "(a p) v -> p a v", p=128))
                iters = ([0] if sl < 4 else []) + [1]
                for s in iters:
                    bias_col = sl if s == 0 else 4 + sl
                    diag = (s == 0 and sl == 3) or (s == 1 and sl == 8)
                    pv_queue = []

                    def emit_pv(job, ki=ki):
                        (s_, ks_, pt_, s1_, st_, sp_) = job
                        bo, bs_ = bOut[s_], bSum[s_]
                        p.wait("tensor", S, s1_)
                        p.op("tensor", None, lambda e, b=bo, ki=ki, ks=ks_, o=pt_,
                             st=st_, sp=sp_: e.matmul(
                            b.t[:, :], self._mm(vbuf[ki][:, ks * 128:(ks + 1) * 128]),
                            self._mm(o[:, :]), start=st, stop=sp))
                        t2 = p.op("tensor", T, lambda e, b=bs_, o=pt_,
                                  st=st_, sp=sp_: e.matmul(
                            b.t[:, :], self._mm(self.ones[:, :]), self._mm(o[:, :]),
                            start=st, stop=sp))
                        pt_free[pT.index(pt_)] = t2

                    for ks in range(4):
                        sb = bS[ks]
                        self.acq(sb)
                        p.wait("tensor", S, exp_done.get(id(sb)))
                        p.waitt("tensor", kl)
                        p.waitt("tensor", ql)
                        p.op("tensor", None, lambda e, b=sb, ki=ki, ks=ks, s=s: e.matmul(
                            b.t[:, :], self._mm(kbuf[ki][:, ks * 128:(ks + 1) * 128]),
                            self._mm(qnb[s][:, :]), start=True, stop=False))
                        t1 = p.op("tensor", T, lambda e, b=sb, ki=ki, ks=ks, s=s: e.matmul(
                            b.t[:, :], self._mm(krb[ki][:, ks * 128:(ks + 1) * 128]),
                            self._mm(qrb[s][:, :]), start=False, stop=True))
                        wait_exp = [(T, t1)]
                        if diag:
                            v1 = p.op("vector", V, lambda e, b=sb, ks=ks: e.tensor_add(
                                b.t[:, :], b.t[:, :],
                                self.mdiag[:, ks * 512:(ks + 1) * 512]),
                                wait=[(T, t1)])
                            wait_exp = [(V, v1)]
                        pt = pT[ks]
                        p.wait("scalar", T, pt_free.get(ks))
                        s1 = p.op("scalar", S, lambda e, b=sb, o=pt, bc=bias_col:
                                  e.activation(o[:, :], b.t[:, :], AF.Exp,
                                               bias=self.biasc[:, bc:bc + 1],
                                               scale=1.0), wait=wait_exp)
                        exp_done[id(sb)] = s1
                        if not out_open[s]:
                            self.acq(bOut[s])
                            self.acq(bSum[s])
                        stop_now = diag and (ks == 3)
                        pv_queue.append((s, ks, pt, s1, not out_open[s], stop_now))
                        out_open[s] = True
                        if len(pv_queue) >= 2:
                            emit_pv(pv_queue.pop(0))
                    while pv_queue:
                        emit_pv(pv_queue.pop(0))
                kv_use[sl] = p.cur(T)
            q_use = p.cur(T)
            for s in range(2):
                bo, bs_ = bOut[s], bSum[s]
                tv = p.cur(T)
                sv = p.op("scalar", S, lambda e, b=bs_: e.copy(sumsb[:, :], b.t[:, :]),
                          wait=[(T, tv)] + ([(V, sum_fin)] if sum_fin else []))
                bs_.free = (S, sv)
                v1 = p.op("vector", V, lambda e: e.reciprocal(invb[:, :], sumsb[:, :]),
                          wait=[(S, sv)])
                v2 = p.op("vector", V, lambda e, b=bo, h=h, s=s: e.tensor_mul(
                    attn[:, h * TQ + s * 512:h * TQ + (s + 1) * 512], b.t[:, :],
                    invb[:, :]), wait=[(V, v1)])
                bo.free = (V, v2)
                sum_fin = v2

    def phase_oproj(self, ph):
        nc, p = self.nc, self.p
        E = ph.enter_context
        T, V, S = self.T, self.V, self.S
        WOs = self.sems(2)
        XLs = self.sems(2)
        X2Ss = self.sems(2)
        attn = self.attn_sb
        wob = [E(nc.sbuf_tensor(f"wob{i}", [128, 2048], F32)) for i in range(2)]
        xot = [E(nc.sbuf_tensor(f"xot{i}", [128, 512], F32)) for i in range(2)]
        x2t = [E(nc.sbuf_tensor(f"x2t{i}", [128, 512], F32)) for i in range(2)]

        last_mm = {}
        x2_spill = {}
        vadd = {}
        for oc in range(16):
            wv = p.dma("sync", WOs[oc % 2], wob[oc % 2][:, :],
                       self.wod[oc * 128:(oc + 1) * 128, :],
                       wait=[(T, last_mm.get(oc - 2))] if oc >= 2 else [])
            for th in range(2):
                idx = oc * 2 + th
                bk = self.banks[idx % 2]
                self.acq(bk)
                lx = p.dma("sync", XLs[th], xot[th][:, :],
                           self.xown[oc * 128:(oc + 1) * 128, th * 512:(th + 1) * 512],
                           wait=[(V, vadd.get(idx - 2))] if idx >= 2 else [])
                p.waitt("tensor", wv)
                tv = None
                for vc in range(16):
                    tv = p.op("tensor", T if vc == 15 else None,
                              lambda e, b=bk, v=vc, th=th, w=wob[oc % 2]: e.matmul(
                        b.t[:, :], self._mm(w[:, v * 128:(v + 1) * 128]),
                        self._mm(attn[:, v * TQ + th * 512:v * TQ + th * 512 + 512]),
                        start=(v == 0), stop=(v == 15)))
                last_mm[oc] = tv
                v1 = p.op("vector", V, lambda e, b=bk, th=th, x=x2t[idx % 2]:
                          e.tensor_add(x[:, :], b.t[:, :], xot[th][:, :]),
                          wait=[(T, tv), lx, x2_spill.get(idx - 2)])
                vadd[idx] = v1
                bk.free = (V, v1)
                x2_spill[idx] = p.dma(
                    "gpsimd", X2Ss[idx % 2],
                    self.x2_s[oc * 128:(oc + 1) * 128, th * 512:(th + 1) * 512],
                    x2t[idx % 2][:, :], wait=[(V, v1)])

    def phase_ffn1(self, ph):
        nc, p = self.nc, self.p
        E = ph.enter_context
        T, V, S = self.T, self.V, self.S
        X2Ls = self.sems(2)
        WGs = self.sems(2)
        WU5s = self.sems(2)
        ZSPs = self.sems(2)
        x2b = [E(nc.sbuf_tensor(f"x2b{i}", [128, TQ], F32)) for i in range(2)]
        xsq = E(nc.sbuf_tensor("xsqf", [128, TQ], F32))
        m2 = E(nc.sbuf_tensor("m2", [128, TQ], F32))
        r2 = E(nc.sbuf_tensor("r2", [128, TQ], F32))
        f1 = E(nc.sbuf_tensor("f1", [128, TQ], F32))
        f2 = E(nc.sbuf_tensor("f2", [128, TQ], F32))
        y1 = E(nc.sbuf_tensor("y1", [128, 18 * TQ], F32))
        hsu = [E(nc.sbuf_tensor(f"hsu{i}", [128, 128], F32)) for i in range(2)]
        wupb = [E(nc.sbuf_tensor(f"wupb{i}", [128, 2048], F32)) for i in range(2)]
        wgb = [E(nc.sbuf_tensor(f"wgb{i}", [128, 2048], F32)) for i in range(2)]
        gt = [E(nc.sbuf_tensor(f"gt{i}", [128, 512], F32)) for i in range(2)]
        zc = [E(nc.sbuf_tensor(f"zc{i}", [128, TQ], F32)) for i in range(2)]

        bS0, bQ0, bS1, bQ1 = (self.banks[i] for i in range(4))
        for b in (bS0, bQ0, bS1, bQ1):
            self.acq(b)
        sq_done = {}
        tvv = None
        for fc in range(FC):
            lv = p.dma("sync", X2Ls[fc % 2], x2b[fc % 2][:, :],
                       self.x2_s[fc * 128:(fc + 1) * 128, :],
                       wait=[(T, sq_done.get(fc - 2))] if fc >= 2 else [])
            p.waitt("scalar", lv)
            sv = p.op("scalar", S, lambda e, x=x2b[fc % 2]: e.square(xsq[:, :], x[:, :]),
                      wait=[(T, sq_done.get(fc - 1))] if fc >= 1 else [])
            p.waitt("tensor", lv)
            p.wait("tensor", S, sv)
            for th in range(2):
                cs = slice(th * 512, (th + 1) * 512)
                p.op("tensor", None, lambda e, x=x2b[fc % 2], f=fc,
                     b=(bS0 if th == 0 else bS1), cs=cs: e.matmul(
                    b.t[:, :], self._mm(self.ones[:, :]), self._mm(x[:, cs]),
                    start=(f == 0), stop=(f == FC - 1)))
                tvv = p.op("tensor", T if th == 1 else None, lambda e, f=fc,
                           b=(bQ0 if th == 0 else bQ1), cs=cs: e.matmul(
                    b.t[:, :], self._mm(self.ones[:, :]), self._mm(xsq[:, cs]),
                    start=(f == 0), stop=(f == FC - 1)))
            sq_done[fc] = tvv
        tv = sq_done[FC - 1]
        nv = None
        for th in range(2):
            cs = slice(th * 512, (th + 1) * 512)
            bs_, bq = (bS0, bQ0) if th == 0 else (bS1, bQ1)
            sv = p.op("scalar", S, lambda e, b=bs_, cs=cs: e.mul(
                m2[:, cs], b.t[:, :], 1.0 / D), wait=[(T, tv)])
            p.op("scalar", S, lambda e, b=bq, cs=cs: e.mul(
                f1[:, cs], b.t[:, :], 1.0 / D))
            sv2 = p.cur(S)
            bs_.free = (S, sv)
            bq.free = (S, sv2)
            p.op("scalar", S, lambda e, cs=cs: e.square(f2[:, cs], m2[:, cs]))
            sv3 = p.cur(S)
            vsub = p.op("vector", V, lambda e, cs=cs: e.tensor_sub(
                f1[:, cs], f1[:, cs], f2[:, cs]), wait=[(S, sv3)])
            nv = self.newton_rsqrt(r2[:, cs], f1[:, cs], EPS_LN, f2[:, cs],
                                   xsq[:, cs], wait=[(V, vsub)])
        rv = p.op("scalar", S, lambda e: e.mul(r2[:, :], r2[:, :], FS1),
                  wait=[(V, nv)])

        slot = list(range(16))
        mm_done = {}
        for fc in range(FC):
            lv = p.dma("sync", X2Ls[fc % 2], x2b[fc % 2][:, :],
                       self.x2_s[fc * 128:(fc + 1) * 128, :],
                       wait=[(T, mm_done.get(fc - 2, tv))])
            v1 = p.op("vector", V, lambda e, x=x2b[fc % 2]: e.tensor_sub(
                f1[:, :], x[:, :], m2[:, :]),
                wait=[lv, (S, rv)] +
                     ([(T, mm_done.get(fc - 1))] if fc >= 1 else []))
            v2 = p.op("vector", V, lambda e: e.tensor_mul(f1[:, :], f1[:, :], r2[:, :]))
            hvv = p.op("vector", V, lambda e, f=fc, hb=hsu[fc % 2]: e.tensor_scalar_mul(
                hb[:, :], self.h128[:, :], self.suug[:, f:f + 1]),
                wait=[(T, mm_done.get(fc - 2))] if fc >= 2 else [])
            for th in range(2):
                bk = self.banks[4 + (fc * 2 + th) % 2]
                self.acq(bk)
                p.wait("tensor", V, max(v2, hvv))
                tvv = p.op("tensor", T, lambda e, b=bk, hb=hsu[fc % 2], th=th: e.matmul(
                    b.t[:, :], self._mm(hb[:, :]),
                    self._mm(f1[:, th * 512:(th + 1) * 512]), start=True, stop=True))
                sv = p.op("scalar", S, lambda e, b=bk, f=fc, th=th: e.copy(
                    y1[:, f * TQ + th * 512:f * TQ + th * 512 + 512], b.t[:, :]),
                    wait=[(T, tvv)])
                bk.free = (S, sv)
            mm_done[fc] = tvv
        p.wait("vector", S, p.cur(S))
        spare = [16, 17]
        for st in range(4):
            bit = 1 << st
            newslot = list(slot)
            for c in range(FC):
                if c & bit:
                    continue
                u, vtx = c, c | bit
                su_, sv_ = slot[u], slot[vtx]
                du, dv = spare[0], spare[1]
                p.op("vector", V, lambda e, a=su_, b=sv_, d=du: e.tensor_add(
                    y1[:, d * TQ:(d + 1) * TQ], y1[:, a * TQ:(a + 1) * TQ],
                    y1[:, b * TQ:(b + 1) * TQ]))
                p.op("vector", V, lambda e, a=su_, b=sv_, d=dv: e.tensor_sub(
                    y1[:, d * TQ:(d + 1) * TQ], y1[:, a * TQ:(a + 1) * TQ],
                    y1[:, b * TQ:(b + 1) * TQ]))
                newslot[u], newslot[vtx] = du, dv
                spare = [su_, sv_]
            slot = newslot
        bfly_done = p.cur(V)
        p.wait("tensor", V, bfly_done)

        w_use = {}
        z_spill = {}
        v1 = None
        for c in range(IC):
            wgv = p.dma("sync", WGs[c % 2], wgb[c % 2][:, :],
                        self.wgate[c * 128:(c + 1) * 128, :],
                        wait=[(T, w_use.get(c - 2))] if c >= 2 else [])
            wvu = p.dma("sync", WU5s[c % 2], wupb[c % 2][:, :],
                        self.wup[c * 128:(c + 1) * 128, :])
            for th in range(2):
                bg = self.banks[(c * 2 + th) % 4]
                self.acq(bg)
                p.waitt("tensor", wgv)
                p.waitt("tensor", wvu)
                for f in range(FC):
                    p.op("tensor", None, lambda e, b=bg, f=f, th=th, w=wgb[c % 2],
                         sf=slot[f]: e.matmul(
                        b.t[:, :], self._mm(w[:, f * 128:(f + 1) * 128]),
                        self._mm(y1[:, sf * TQ + th * 512:sf * TQ + th * 512 + 512]),
                        start=(f == 0), stop=(f == FC - 1)))
                bu = self.banks[4 + (c * 2 + th) % 4]
                self.acq(bu)
                tvu = None
                for f in range(FC):
                    tvu = p.op("tensor", T if f == FC - 1 else None,
                               lambda e, b=bu, f=f, th=th, w=wupb[c % 2],
                               sf=slot[f]: e.matmul(
                        b.t[:, :], self._mm(w[:, f * 128:(f + 1) * 128]),
                        self._mm(y1[:, sf * TQ + th * 512:sf * TQ + th * 512 + 512]),
                        start=(f == 0), stop=(f == FC - 1)))
                w_use[c] = tvu
                sv = p.op("scalar", S, lambda e, b=bg, c=c, g=gt[th]:
                          e.activation(g[:, :], b.t[:, :], AF.Silu, bias=0.0,
                                       scale=self.scgt[:, c:c + 1]),
                          wait=[(T, tvu)] + ([(V, v1)] if v1 else []))
                bg.free = (S, sv)
                v1 = p.op("vector", V, lambda e, b=bu, th=th, c=c, g=gt[th],
                          z=zc[c % 2]: e.scalar_tensor_tensor(
                    z[:, th * 512:(th + 1) * 512], b.t[:, :], self.scup[:, c:c + 1],
                    g[:, :], ALU.mult, ALU.mult),
                    wait=[(S, sv)] +
                         ([z_spill.get(c - 2)] if (th == 0 and c >= 2) else []))
                bu.free = (V, v1)
            z_spill[c] = p.dma("gpsimd", ZSPs[c % 2],
                               self.z_s[c * 128:(c + 1) * 128, :],
                               zc[c % 2][:, :], wait=[(V, v1)])

    def phase_ffn2(self, ph):
        nc, p = self.nc, self.p
        E = ph.enter_context
        T, V, S = self.T, self.V, self.S
        ZLs = self.sems(4)
        WDs = self.sems(2)
        XT6s = self.sems(2)
        OUTs = self.sems(2)
        zwin = E(nc.sbuf_tensor("zwin", [128, 68 * 512], F32))
        hsu = [E(nc.sbuf_tensor(f"hs2{i}", [128, 128], F32)) for i in range(2)]
        wdb = [E(nc.sbuf_tensor(f"wdb{i}", [128, 6 * 128], F32)) for i in range(2)]
        x2t = [E(nc.sbuf_tensor(f"x2f{i}", [128, 512], F32)) for i in range(2)]
        ob = [E(nc.sbuf_tensor(f"ob{i}", [128, 512], F32)) for i in range(2)]

        og_list = [(0, 6), (6, 6), (12, 4)]
        wd_free = {}
        for th in range(2):
            cs = slice(th * 512, (th + 1) * 512)
            p.wait("sync", T, p.cur(T))
            p.wait("gpsimd", T, p.cur(T))
            slot = list(range(IC))
            hv = {}
            for c in range(IC):
                lv = p.dma("sync", ZLs[c % 4], zwin[:, c * 512:(c + 1) * 512],
                           self.z_s[c * 128:(c + 1) * 128, cs],
                           wait=[(T, hv.get(c - 4))] if hv.get(c - 4) else [])
                hvv = p.op("vector", V, lambda e, c=c, hb=hsu[c % 2]: e.tensor_scalar_mul(
                    hb[:, :], self.h128[:, :], self.sudp[:, c:c + 1]),
                    wait=[(T, hv.get(c - 2))] if hv.get(c - 2) else [])
                bk = self.banks[6 + c % 2]
                self.acq(bk)
                p.waitt("tensor", lv)
                p.wait("tensor", V, hvv)
                tvv = p.op("tensor", T, lambda e, b=bk, c=c, hb=hsu[c % 2]: e.matmul(
                    b.t[:, :], self._mm(hb[:, :]),
                    self._mm(zwin[:, c * 512:(c + 1) * 512]), start=True, stop=True))
                hv[c] = tvv
                sv = p.op("scalar", S, lambda e, b=bk, c=c: e.copy(
                    zwin[:, c * 512:(c + 1) * 512], b.t[:, :]), wait=[(T, tvv)])
                bk.free = (S, sv)
            p.wait("vector", S, p.cur(S))
            p.wait("gpsimd", S, p.cur(S))
            spares = {0: [64, 65], 1: [66, 67]}
            G = self.GB
            for st in range(6):
                bit = 1 << st
                newslot = list(slot)
                k = 0
                for c in range(IC):
                    if c & bit:
                        continue
                    u, vtx = c, c | bit
                    su_, sv_ = slot[u], slot[vtx]
                    eng = "vector" if (k % 2 == 0) else "gpsimd"
                    sem = V if (k % 2 == 0) else G
                    du, dv = spares[k % 2][0], spares[k % 2][1]
                    p.op(eng, sem, lambda e, a=su_, b=sv_, d=du: e.tensor_add(
                        zwin[:, d * 512:(d + 1) * 512], zwin[:, a * 512:(a + 1) * 512],
                        zwin[:, b * 512:(b + 1) * 512]))
                    p.op(eng, sem, lambda e, a=su_, b=sv_, d=dv: e.tensor_sub(
                        zwin[:, d * 512:(d + 1) * 512], zwin[:, a * 512:(a + 1) * 512],
                        zwin[:, b * 512:(b + 1) * 512]))
                    newslot[u], newslot[vtx] = du, dv
                    spares[k % 2] = [su_, sv_]
                    k += 1
                slot = newslot
                p.wait("vector", G, p.cur(G))
                p.wait("gpsimd", V, p.cur(V))
            y_done = p.cur(V)
            y_done_g = p.cur(G)
            p.wait("tensor", V, y_done)
            p.wait("tensor", G, y_done_g)
            for (oc0, ogn) in og_list:
                tv = None
                for c in range(IC):
                    wvd = p.dma("sync", WDs[c % 2], wdb[c % 2][:, 0:ogn * 128],
                                self.wdown[c * 128:(c + 1) * 128,
                                           oc0 * 128:(oc0 + ogn) * 128],
                                wait=[(T, wd_free.get(c % 2))]
                                if wd_free.get(c % 2) else [])
                    p.waitt("tensor", wvd)
                    for i in range(ogn):
                        bk = self.banks[i]
                        if c == 0:
                            self.acq(bk)
                        tv = p.op("tensor", T if i == ogn - 1 else None,
                                  lambda e, b=bk, c=c, i=i, w=wdb[c % 2],
                                  sc=slot[c]: e.matmul(
                            b.t[:, :], self._mm(w[:, i * 128:(i + 1) * 128]),
                            self._mm(zwin[:, sc * 512:(sc + 1) * 512]),
                            start=(c == 0), stop=(c == IC - 1)))
                    wd_free[c % 2] = tv
                odma = {}
                vprev = {}
                for i in range(ogn):
                    oc = oc0 + i
                    lx = p.dma("sync", XT6s[i % 2], x2t[i % 2][:, :],
                               self.x2_s[oc * 128:(oc + 1) * 128, cs],
                               wait=[(V, vprev.get(i - 2))] if i >= 2 else [])
                    v1 = p.op("vector", V, lambda e, b=self.banks[i], i=i, oc=oc:
                              e.scalar_tensor_tensor(
                        ob[i % 2][:, :], b.t[:, :], self.scdp[:, oc:oc + 1],
                        x2t[i % 2][:, :], ALU.mult, ALU.add),
                        wait=[(T, tv), lx, odma.get(i - 2)])
                    vprev[i] = v1
                    self.banks[i].free = (V, v1)
                    odma[i] = p.dma("sync", OUTs[i % 2],
                                    self.outs[oc * 128:(oc + 1) * 128, cs],
                                    ob[i % 2][:, :], wait=[(V, v1)])


def _hadamard(n):
    h = np.array([[1.0]], dtype=np.float64)
    while h.shape[0] < n:
        h = np.block([[h, h], [h, -h]])
    return np.ascontiguousarray(h.astype(np.float32))


def _rope_tables(pos):
    inv_freq = 1.0 / (500000.0 ** (np.arange(0, 64, 2, dtype=np.float64) / 64.0))
    ang = np.outer(pos.astype(np.float64), inv_freq)
    return (np.ascontiguousarray(np.cos(ang).astype(np.float32).T),
            np.ascontiguousarray(np.sin(ang).astype(np.float32).T))


def prepare_in_maps(L0_attn_x, L0_attn_w_q_down, L0_attn_q_norm_w, L0_attn_w_q_up,
                    L0_attn_w_kv_down, L0_attn_kv_norm_w, L0_attn_w_kv_up, L0_attn_w_o,
                    L0_ffn_SU_ug, L0_ffn_SU_dp, L0_ffn_Wscale_ug, L0_ffn_Wscale_dp,
                    L0_ffn_W_up, L0_ffn_W_gate, L0_ffn_W_down):
    x = np.asarray(L0_attn_x, dtype=np.float32)

    def f32(a):
        return np.ascontiguousarray(np.asarray(a, dtype=np.float32))

    wqd_b = f32(np.asarray(L0_attn_w_q_down, np.float32)
                .reshape(QC, 128, FC, 128).transpose(2, 3, 0, 1)
                .reshape(FC * 128, QC * 128))
    perm = []
    for h in range(H):
        perm.extend(range(h * 192, h * 192 + 128))
    for g in range(4):
        for hh in range(4):
            hd = g * 4 + hh
            perm.extend((hd * 192 + 128 + 2 * np.arange(32)).tolist())
    for g in range(4):
        for hh in range(4):
            hd = g * 4 + hh
            perm.extend((hd * 192 + 129 + 2 * np.arange(32)).tolist())
    wqu_b = f32(np.asarray(L0_attn_w_q_up, np.float32)[np.asarray(perm)]
                .reshape(24, 128, QC, 128).transpose(2, 3, 0, 1)
                .reshape(QC * 128, 24 * 128))
    kperm = np.concatenate([np.arange(512), 512 + 2 * np.arange(32),
                            513 + 2 * np.arange(32)])
    wkd_b = f32(np.asarray(L0_attn_w_kv_down, np.float32)[kperm].T
                .reshape(FC * 128, 576))
    wku = np.asarray(L0_attn_w_kv_up, np.float32).reshape(H, 2, 128, KC, 128)
    wkuk_b = f32(wku[:, 0].transpose(0, 3, 2, 1).reshape(H * 128, KC * 128))
    wkuv_b = f32(wku[:, 1].transpose(2, 3, 0, 1).reshape(KC * 128, 2048))
    wo_b = f32(np.asarray(L0_attn_w_o, np.float32)
               .reshape(16, 128, 16, 128).transpose(0, 3, 2, 1)
               .reshape(16 * 128, 16 * 128))
    wup_b = f32(np.asarray(L0_ffn_W_up, np.float32)
                .reshape(IC, 128, FC, 128).transpose(0, 3, 2, 1)
                .reshape(IC * 128, FC * 128))
    wgt_b = f32(np.asarray(L0_ffn_W_gate, np.float32)
                .reshape(IC, 128, FC, 128).transpose(0, 3, 2, 1)
                .reshape(IC * 128, FC * 128))
    wdn_b = f32(np.asarray(L0_ffn_W_down, np.float32)
                .reshape(16, 128, IC, 128).transpose(2, 3, 0, 1)
                .reshape(IC * 128, 16 * 128))
    gq_b = f32(np.asarray(L0_attn_q_norm_w, np.float32).reshape(QC, 128).T)
    gkv_b = f32(np.asarray(L0_attn_kv_norm_w, np.float32).reshape(KC, 128).T)
    suug_b = f32(np.asarray(L0_ffn_SU_ug, np.float32).reshape(FC, 128).T)
    sudp_b = f32(np.asarray(L0_ffn_SU_dp, np.float32).reshape(IC, 128).T)
    wsc = np.asarray(L0_ffn_Wscale_ug, np.float32)
    scup_b = f32(wsc[:INTER].reshape(IC, 128).T)
    scgt_b = f32(wsc[INTER:].reshape(IC, 128).T)
    scdp_b = f32(np.asarray(L0_ffn_Wscale_dp, np.float32).reshape(FC, 128).T)

    onesd = np.ones((128, 128), np.float32)
    constsd = np.tile(np.array([EPS_LN, EPS_RMS, 0.0, 1.0], np.float32), (128, 1))
    h128d = _hadamard(128)
    kk = np.arange(128)[:, None]
    qq = np.arange(512)[None, :]
    md = np.concatenate(
        [np.where(ks * 128 + kk > qq, np.float32(NEGM), np.float32(0.0))
         for ks in range(4)], axis=1).astype(np.float32)

    in_maps = []
    core_meta = []
    for core in range(8):
        sq, j = core // 4, core % 4
        A, B = j, 7 - j
        xs = x[sq].T
        remaining = [g for g in range(8) if g != j]
        sperm = remaining[0:3] + [j] + remaining[3:7] + [7 - j]
        xseq_b = f32(np.concatenate([xs[:, g * 512:(g + 1) * 512] for g in sperm],
                                    axis=1))
        xown_b = f32(np.concatenate([xs[:, A * 512:(A + 1) * 512],
                                     xs[:, B * 512:(B + 1) * 512]], axis=1))
        pos_own = np.concatenate([np.arange(A * 512, (A + 1) * 512),
                                  np.arange(B * 512, (B + 1) * 512)])
        cq_t, sq_t = _rope_tables(pos_own)
        cosq_b = f32(np.tile(cq_t, (4, 1)))
        sinq_b = f32(np.tile(sq_t, (4, 1)))
        pos_seq = np.concatenate([np.arange(g * 512, (g + 1) * 512) for g in sperm])
        ck_t, sk_t = _rope_tables(pos_seq)
        deadt = np.zeros((128, 13), np.float32)
        for sl in range(4):
            g = sperm[sl]
            if not ((g < j) or (sl == 3)):
                deadt[:, sl] = NEGM
        for sl in range(9):
            g = sperm[sl]
            if not ((sl == 8) or (g < B and sl < 8)):
                deadt[:, 4 + sl] = NEGM
        in_maps.append({
            "xseq": xseq_b, "xown": xown_b, "cosq": cosq_b, "sinq": sinq_b,
            "cosk": f32(ck_t), "sink": f32(sk_t), "deadt": deadt, "mdiagd": md,
            "onesd": onesd, "h128d": h128d, "gqd": gq_b, "gkvd": gkv_b,
            "suugd": suug_b, "sudpd": sudp_b, "scupd": scup_b, "scgtd": scgt_b,
            "scdpd": scdp_b, "constsd": constsd, "wqd": wqd_b, "wqu": wqu_b,
            "wkd": wkd_b, "wkuk": wkuk_b, "wkuv": wkuv_b, "wod": wo_b,
            "wup": wup_b, "wgate": wgt_b, "wdown": wdn_b,
        })
        core_meta.append((sq, A, B))
    return in_maps, core_meta


def kernel(**inputs):
    in_maps, core_meta = prepare_in_maps(**inputs)
    nc = Builder(bass.Bass("TRN2", target_bir_lowering=False,
                       detect_race_conditions=False)).build()
    res = run_bass_kernel_spmd(nc, in_maps, core_ids=list(range(8)))

    out = np.empty((2, 4096, D), np.float32)
    for core in range(8):
        sq, A, B = core_meta[core]
        o = np.asarray(res.results[core]["outs"])
        out[sq, A * 512:(A + 1) * 512] = o[:, 0:512].T
        out[sq, B * 512:(B + 1) * 512] = o[:, 512:1024].T
    return out


# version 09
# speedup vs baseline: 1.0045x; 1.0045x over previous
import numpy as np
from contextlib import ExitStack

import concourse.bass as bass
import concourse.mybir as mybir
from concourse.bass_utils import run_bass_kernel_spmd

F32 = mybir.dt.float32
AF = mybir.ActivationFunctionType
ALU = mybir.AluOpType
AXX = mybir.AxisListType.X

ENGS = ("sync", "tensor", "vector", "scalar", "gpsimd")

MM_DT = F32

NEGM = -1.0e30

D, FC = 2048, 16
QR, QC = 1536, 12
CKV, KC = 512, 4
H = 16
INTER, IC = 8192, 64
NM = 9
TS = NM * 512
TQ = 1024
EPS_LN, EPS_RMS = 1e-5, 1e-6
SCALE = float(1.0 / np.sqrt(192.0))
FS1 = float(1.0 / (64.0 * np.sqrt(2048.0)))
FS2 = float(1.0 / np.sqrt(8192.0))


class Prog:
    def __init__(self):
        self.q = {e: [] for e in ENGS}
        self.count = {}
        self.last_wait = {e: {} for e in ENGS}

    def inc(self, sem, n=1):
        c = self.count.get(sem.name, 0) + n
        self.count[sem.name] = c
        return c

    def cur(self, sem):
        return self.count.get(sem.name, 0)

    def wait(self, eng, sem, val):
        if val is None or val <= 0:
            return
        if self.last_wait[eng].get(sem.name, 0) >= val:
            return
        self.last_wait[eng][sem.name] = val
        self.q[eng].append(lambda e, s=sem, v=val: e.wait_ge(s, v))

    def waitt(self, eng, tok):
        if tok is not None:
            self.wait(eng, tok[0], tok[1])

    def op(self, eng, sem, fn, wait=()):
        for w in wait:
            if w is None:
                continue
            self.wait(eng, w[0], w[1])
        if sem is None:
            self.q[eng].append(fn)
            return None
        v = self.inc(sem)
        self.q[eng].append(lambda e, f=fn, s=sem: f(e).then_inc(s, 1))
        return v

    def dma(self, eng, sem, out, in_, wait=()):
        for w in wait:
            if w is None:
                continue
            self.wait(eng, w[0], w[1])
        v = self.inc(sem, 16)
        self.q[eng].append(
            lambda e, o=out, i=in_, s=sem: e.dma_start(out=o, in_=i).then_inc(s, 16))
        return (sem, v)

    def run(self, eng_name, engine):
        for fn in self.q[eng_name]:
            fn(engine)


class Bank:
    __slots__ = ("t", "free")

    def __init__(self, t):
        self.t = t
        self.free = None


class Builder:
    def __init__(self, nc):
        self.nc = nc
        self.p = Prog()

    def _mm(self, ap):
        return ap if MM_DT is F32 else ap.bitcast(MM_DT)

    def acq(self, bank):
        if bank.free is not None:
            self.p.wait("tensor", *bank.free)

    def barrier(self):
        snap = [(sm, self.p.cur(sm)) for sm in self.all_sems if self.p.cur(sm) > 0]
        for e in ENGS:
            for sm, vv in snap:
                self.p.wait(e, sm, vv)

    def newton_rsqrt(self, r_out, var_ap, eps, t1, t2, wait=()):
        p, S, V = self.p, self.S, self.V
        sv = p.op("scalar", S, lambda e: e.add(t1, var_ap, eps), wait=wait)
        vv = p.op("vector", V, lambda e: e.reciprocal(t2, t1), wait=[(S, sv)])
        sv = p.op("scalar", S, lambda e: e.sqrt(r_out, t2), wait=[(V, vv), (S, sv)])
        last = None
        for it in range(2):
            s2 = p.op("scalar", S, lambda e: e.square(t2, r_out),
                      wait=[(S, sv)] if it == 0 else [(V, last)])
            p.op("vector", V, lambda e: e.tensor_mul(t2, t1, t2), wait=[(S, s2)])
            p.op("vector", V, lambda e: e.tensor_scalar(t2, t2, -0.5, 1.5,
                                                        ALU.mult, ALU.add))
            last = p.op("vector", V, lambda e: e.tensor_mul(r_out, r_out, t2))
        return last

    def ln_stats(self, x_sb, nfc, width, mean_out, rstd_out, sq_scratch, t1, t2,
                 denom, eps, lv):
        p = self.p
        S, T = self.S, self.T
        b_sum, b_sq = self.banks[0], self.banks[1]
        self.acq(b_sum)
        self.acq(b_sq)
        p.waitt("tensor", lv)
        for f in range(nfc):
            p.op("tensor", None, lambda e, f=f: e.matmul(
                b_sum.t[:, :width], self._mm(self.ones[:, :]),
                self._mm(x_sb[:, f * width:(f + 1) * width]),
                start=(f == 0), stop=(f == nfc - 1)))
        p.waitt("scalar", lv)
        sv = None
        for f in range(nfc):
            sv = p.op("scalar", S, lambda e, f=f: e.square(
                sq_scratch[:, f * width:(f + 1) * width],
                x_sb[:, f * width:(f + 1) * width]))
        p.wait("tensor", S, sv)
        tv = None
        for f in range(nfc):
            tv = p.op("tensor", T if f == nfc - 1 else None, lambda e, f=f: e.matmul(
                b_sq.t[:, :width], self._mm(self.ones[:, :]),
                self._mm(sq_scratch[:, f * width:(f + 1) * width]),
                start=(f == 0), stop=(f == nfc - 1)))
        sv = p.op("scalar", S, lambda e: e.mul(mean_out, b_sum.t[:, :width], 1.0 / denom),
                  wait=[(T, tv)])
        sv2 = p.op("scalar", S, lambda e: e.mul(t1, b_sq.t[:, :width], 1.0 / denom))
        b_sum.free = (S, sv)
        b_sq.free = (S, sv2)
        sv3 = p.op("scalar", S, lambda e: e.square(t2, mean_out), wait=[(S, sv)])
        vsub = p.op("vector", self.V, lambda e: e.tensor_sub(t1, t1, t2),
                    wait=[(S, sv3)])
        return self.newton_rsqrt(rstd_out, t1, eps, t2, sq_scratch[:, 0:width],
                                 wait=[(V := self.V, vsub)])

    def rms_stats(self, get_chunk, nfc, width, rstd_out, t1, t2, t3, denom, eps,
                  bank_idx=0):
        p = self.p
        S, T = self.S, self.T
        b_r = self.banks[bank_idx]
        self.acq(b_r)
        tv = None
        for f in range(nfc):
            sqv = p.op("scalar", S, lambda e, f=f: e.square(t3, get_chunk(f)),
                       wait=[(T, tv)] if tv else [])
            p.wait("tensor", S, sqv)
            tv = p.op("tensor", T, lambda e, f=f: e.matmul(
                b_r.t[:, :width], self._mm(self.ones[:, :]), self._mm(t3),
                start=(f == 0), stop=(f == nfc - 1)))
        sv = p.op("scalar", S, lambda e: e.mul(t1, b_r.t[:, :width], 1.0 / denom),
                  wait=[(T, tv)])
        b_r.free = (S, sv)
        return self.newton_rsqrt(rstd_out, t1, eps, t2, t3, wait=[(S, sv)])

    def build(self):
        nc, p = self.nc, self.p

        def din(name, shape):
            return nc.dram_tensor(name, shape, F32, kind="ExternalInput").ap()

        self.xseq = din("xseq", [D, TS])
        self.xown = din("xown", [D, TQ])
        self.cosq = din("cosq", [128, TQ])
        self.sinq = din("sinq", [128, TQ])
        self.cosql = din("cosql", [128, TQ])
        self.sinql = din("sinql", [128, TQ])
        self.cosk = din("cosk", [32, TS])
        self.sink = din("sink", [32, TS])
        self.coskl = din("coskl", [32, TS])
        self.sinkl = din("sinkl", [32, TS])
        self.deadt = din("deadt", [128, 13])
        self.mdiagd = din("mdiagd", [128, 4 * 512])
        self.onesd = din("onesd", [128, 128])
        self.h128d = din("h128d", [128, 128])
        self.gqd = din("gqd", [128, QC])
        self.gkvd = din("gkvd", [128, KC])
        self.suugd = din("suugd", [128, FC])
        self.sudpd = din("sudpd", [128, IC])
        self.scupd = din("scupd", [128, IC])
        self.scgtd = din("scgtd", [128, IC])
        self.scdpd = din("scdpd", [128, FC])
        self.constsd = din("constsd", [128, 4])
        self.wqd = din("wqd", [FC * 128, QC * 128])
        self.wqu = din("wqu", [QC * 128, 24 * 128])
        self.wkd = din("wkd", [FC * 128, 576])
        self.wkuk = din("wkuk", [H * 128, KC * 128])
        self.wkuv = din("wkuv", [KC * 128, 2048])
        self.wod = din("wod", [16 * 128, 16 * 128])
        self.wup = din("wup", [IC * 128, FC * 128])
        self.wgate = din("wgate", [IC * 128, FC * 128])
        self.wdown = din("wdown", [IC * 128, 16 * 128])
        self.outs = nc.dram_tensor("outs", [D, TQ], F32, kind="ExternalOutput").ap()

        def scr(name, shape):
            return nc.dram_tensor(name, shape, F32).ap()

        self.ckvn_s = scr("ckvn_s", [KC * 128, TS])
        self.krope_s = scr("krope_s", [64, TS])
        self.knope_s = scr("knope_s", [H * 128, TS])
        self.vs_s = scr("vs_s", [TS, 2048])
        self.qs_s = scr("qs_s", [H * 128, TQ])
        self.qrope_s = scr("qrope_s", [H * 64, TQ])
        self.x2_s = scr("x2_s", [D, TQ])
        self.z_s = scr("z_s", [IC * 128, TQ])

        with ExitStack() as ctx:
            E = ctx.enter_context
            self.T = E(nc.semaphore("Tm"))
            self.V = E(nc.semaphore("Vm"))
            self.S = E(nc.semaphore("Sm"))
            self.pool = [E(nc.semaphore(f"dq{i}")) for i in range(56)]
            self._si = 0
            self.all_sems = [self.T, self.V, self.S] + self.pool

            def sems(n):
                r = self.pool[self._si:self._si + n]
                assert len(r) == n, "sem pool exhausted"
                self._si += n
                return r if n > 1 else r[0]

            self.sems = sems
            self.banks = [Bank(E(nc.psum_tensor(f"pb{i}", [128, 512], F32)))
                          for i in range(8)]
            self.ones = E(nc.sbuf_tensor("ones_sb", [128, 128], F32))
            self.h128 = E(nc.sbuf_tensor("h128_sb", [128, 128], F32))
            self.mdiag = E(nc.sbuf_tensor("mdiag_sb", [128, 4 * 512], F32))
            self.dead = E(nc.sbuf_tensor("dead_sb", [128, 13], F32))
            self.gqs = E(nc.sbuf_tensor("gqs_sb", [128, QC], F32))
            self.gkv = E(nc.sbuf_tensor("gkv_sb", [128, KC], F32))
            self.suug = E(nc.sbuf_tensor("suug_sb", [128, FC], F32))
            self.sudp = E(nc.sbuf_tensor("sudp_sb", [128, IC], F32))
            self.scup = E(nc.sbuf_tensor("scup_sb", [128, IC], F32))
            self.scgt = E(nc.sbuf_tensor("scgt_sb", [128, IC], F32))
            self.scdp = E(nc.sbuf_tensor("scdp_sb", [128, FC], F32))
            self.csts = E(nc.sbuf_tensor("csts_sb", [128, 4], F32))
            nc.const_aps.register(F32, EPS_LN, self.csts[:, 0:1])
            nc.const_aps.register(F32, EPS_RMS, self.csts[:, 1:2])
            nc.const_aps.register(F32, 0.0, self.csts[:, 2:3])
            nc.const_aps.register(F32, 1.0, self.csts[:, 3:4])
            self.qn2m = E(nc.sbuf_tensor("qn2m", [128, 1], F32))
            self.qr2m = E(nc.sbuf_tensor("qr2m", [128, 1], F32))
            self.kn2m = E(nc.sbuf_tensor("kn2m", [128, 1], F32))
            self.kr2m = E(nc.sbuf_tensor("kr2m", [128, 1], F32))
            self.negb = E(nc.sbuf_tensor("negb", [128, 1], F32))
            self.biasc = E(nc.sbuf_tensor("biasc", [128, 13], F32))

            self.phase_init()
            with ExitStack() as ph:
                self.phase_kv_down(ph)
            self.barrier()
            with ExitStack() as ph:
                self.phase_kv_up(ph)
            self.barrier()
            with ExitStack() as ph:
                self.phase_q(ph)
            self.barrier()
            self.phase_bias()
            with ExitStack() as ph:
                self.phase_attn(ph)
                self.barrier()
                self.phase_oproj(ph)
            self.barrier()
            with ExitStack() as ph:
                self.phase_ffn1(ph)
            self.barrier()
            with ExitStack() as ph:
                self.phase_ffn2(ph)
            for sm in self.all_sems:
                if p.cur(sm) > 0:
                    p.wait("sync", sm, p.cur(sm))

            with nc.Block() as block:
                @block.sync
                def _(e):
                    p.run("sync", e)

                @block.tensor
                def _(e):
                    p.run("tensor", e)

                @block.vector
                def _(e):
                    p.run("vector", e)

                @block.scalar
                def _(e):
                    p.run("scalar", e)

                @block.gpsimd
                def _(e):
                    p.run("gpsimd", e)
        return nc

    def phase_init(self):
        p, S = self.p, self.S
        LD = self.sems(1)
        self.LD = LD
        iv = None
        for dst, src in [(self.csts, self.constsd), (self.ones, self.onesd),
                         (self.h128, self.h128d), (self.mdiag, self.mdiagd),
                         (self.dead, self.deadt), (self.gqs, self.gqd),
                         (self.gkv, self.gkvd), (self.suug, self.suugd),
                         (self.sudp, self.sudpd), (self.scup, self.scupd),
                         (self.scgt, self.scgtd), (self.scdp, self.scdpd)]:
            iv = p.dma("sync", LD, dst[:, :], src[:, :])
        self._init_done = False

    def init_compute(self, ld_tok):
        p, S = self.p, self.S
        p.waitt("scalar", ld_tok)
        p.op("scalar", S, lambda e: e.mul(self.gqs[:, :], self.gqs[:, :], SCALE))
        p.op("scalar", S, lambda e: e.mul(self.scup[:, :], self.scup[:, :], 64.0))
        p.op("scalar", S, lambda e: e.mul(self.scgt[:, :], self.scgt[:, :], 64.0))
        p.op("scalar", S, lambda e: e.mul(self.scdp[:, :], self.scdp[:, :], FS2))
        p.op("vector", self.V, lambda e: e.memset(self.qn2m[:, :], 0.0))
        p.op("vector", self.V, lambda e: e.memset(self.qr2m[:, :], 0.0))
        p.op("vector", self.V, lambda e: e.memset(self.kn2m[:, :], 0.0))
        p.op("vector", self.V, lambda e: e.memset(self.kr2m[:, :], 0.0))

    def phase_kv_down(self, ph):
        nc, p = self.nc, self.p
        E = ph.enter_context
        T, V, S, LD = self.T, self.V, self.S, self.LD
        XT = self.sems(1)
        CK = self.sems(1)
        CKS = self.sems(1)
        KRS = self.sems(1)
        wkd_sb = E(nc.sbuf_tensor("wkd_sb", [128, FC * 576], F32))
        xt = E(nc.sbuf_tensor("xt", [128, FC * 512], F32))
        xn = E(nc.sbuf_tensor("xn", [128, FC * 512], F32))
        ckvn = E(nc.sbuf_tensor("ckvn", [128, KC * 512], F32))
        stm = E(nc.sbuf_tensor("stm", [128, 512], F32))
        strr = E(nc.sbuf_tensor("strr", [128, 512], F32))
        st1 = E(nc.sbuf_tensor("st1", [128, 512], F32))
        st2 = E(nc.sbuf_tensor("st2", [128, 512], F32))
        nsc = E(nc.sbuf_tensor("nsc", [128, 512], F32))
        coskt = E(nc.sbuf_tensor("coskt", [32, 512], F32))
        sinkt = E(nc.sbuf_tensor("sinkt", [32, 512], F32))
        ckl = E(nc.sbuf_tensor("ckl", [32, 512], F32))
        skl = E(nc.sbuf_tensor("skl", [32, 512], F32))
        kt1 = E(nc.sbuf_tensor("kt1", [32, 512], F32))
        kt2 = E(nc.sbuf_tensor("kt2", [32, 512], F32))
        kE = E(nc.sbuf_tensor("kE", [32, 512], F32))
        kO = E(nc.sbuf_tensor("kO", [32, 512], F32))
        ksq = E(nc.sbuf_tensor("ksq", [32, 512], F32))

        wkd_tok = None
        for fc in range(FC):
            wkd_tok = p.dma("sync", LD, wkd_sb[:, fc * 576:(fc + 1) * 576],
                            self.wkd[fc * 128:(fc + 1) * 128, :])
        self.init_compute(wkd_tok)
        p.waitt("tensor", wkd_tok)
        p.waitt("vector", wkd_tok)

        xt_free = None
        xn_free = None
        ck_free = None
        cks_tok = None
        krs_tok = None
        for m in range(NM):
            cols = slice(m * 512, (m + 1) * 512)
            lv = p.dma("sync", XT, xt[:, :].rearrange("p (a t) -> p a t", a=FC),
                       self.xseq[:, cols].rearrange("(a p) t -> p a t", p=128),
                       wait=[xt_free])
            p.dma("sync", CK, coskt[:, :], self.cosk[:, cols], wait=[ck_free])
            p.dma("sync", CK, sinkt[:, :], self.sink[:, cols])
            p.dma("sync", CK, ckl[:, :], self.coskl[:, cols])
            ck0 = p.dma("sync", CK, skl[:, :], self.sinkl[:, cols])
            p.waitt("vector", ck0)
            p.op("vector", V, lambda e: e.tensor_add(coskt[:, :], coskt[:, :],
                                                     ckl[:, :]))
            ck = p.op("vector", V, lambda e: e.tensor_add(sinkt[:, :], sinkt[:, :],
                                                          skl[:, :]))
            ck = (V, ck)
            if xn_free is not None:
                p.wait("scalar", *xn_free)
            rv = self.ln_stats(xt, FC, 512, stm[:, :], strr[:, :], xn,
                               st1[:, :], st2[:, :], D, EPS_LN, lv)
            vv = None
            for f in range(FC):
                p.op("vector", V, lambda e, f=f: e.tensor_sub(
                    xn[:, f * 512:(f + 1) * 512], xt[:, f * 512:(f + 1) * 512],
                    stm[:, :]), wait=[(V, rv)])
                vv = p.op("vector", V, lambda e, f=f: e.tensor_mul(
                    xn[:, f * 512:(f + 1) * 512], xn[:, f * 512:(f + 1) * 512],
                    strr[:, :]))
            xt_free = (V, vv)
            p.wait("tensor", V, vv)
            p.waitt("tensor", wkd_tok)
            p.waitt("tensor", cks_tok)
            for oc in range(KC):
                bk = self.banks[2 + oc]
                self.acq(bk)
                for f in range(FC):
                    p.op("tensor", None, lambda e, f=f, o=oc, b=bk: e.matmul(
                        b.t[:, :],
                        self._mm(wkd_sb[:, f * 576 + o * 128:f * 576 + (o + 1) * 128]),
                        self._mm(xn[:, f * 512:(f + 1) * 512]),
                        start=(f == 0), stop=(f == FC - 1)))
            bE, bO = self.banks[6], self.banks[7]
            self.acq(bE)
            self.acq(bO)
            for f in range(FC):
                p.op("tensor", None, lambda e, f=f: e.matmul(
                    bE.t[0:32, :], self._mm(wkd_sb[:, f * 576 + 512:f * 576 + 544]),
                    self._mm(xn[:, f * 512:(f + 1) * 512]),
                    start=(f == 0), stop=(f == FC - 1)))
            tv = None
            for f in range(FC):
                tv = p.op("tensor", T if f == FC - 1 else None, lambda e, f=f: e.matmul(
                    bO.t[0:32, :], self._mm(wkd_sb[:, f * 576 + 544:f * 576 + 576]),
                    self._mm(xn[:, f * 512:(f + 1) * 512]),
                    start=(f == 0), stop=(f == FC - 1)))
            xn_free = (T, tv)
            sv = None
            for oc in range(KC):
                sv = p.op("scalar", S, lambda e, o=oc: e.copy(
                    ckvn[:, o * 512:(o + 1) * 512], self.banks[2 + o].t[:, :]),
                    wait=[(T, tv)])
            rv = self.rms_stats(
                lambda f: ckvn[:, f * 512:(f + 1) * 512], KC, 512, strr[:, :],
                st1[:, :], st2[:, :], nsc[:, :], CKV, EPS_RMS, bank_idx=0)
            for oc in range(KC):
                cv = p.op("vector", V, lambda e, o=oc: e.scalar_tensor_tensor(
                    ckvn[:, o * 512:(o + 1) * 512], ckvn[:, o * 512:(o + 1) * 512],
                    self.gkv[:, o:o + 1], strr[:, :], ALU.mult, ALU.mult),
                    wait=[(V, rv)])
                self.banks[2 + oc].free = (V, cv)
                cks_tok = p.dma("gpsimd", CKS,
                                self.ckvn_s[oc * 128:(oc + 1) * 128, cols],
                                ckvn[:, oc * 512:(oc + 1) * 512], wait=[(V, cv)])
            p.wait("vector", T, tv)
            p.waitt("vector", ck)
            p.waitt("vector", krs_tok)
            p.op("vector", V, lambda e: e.tensor_mul(kt1[:, :], bE.t[0:32, :], coskt[:, :]))
            p.op("vector", V, lambda e: e.tensor_mul(kt2[:, :], bO.t[0:32, :], sinkt[:, :]))
            v3 = p.op("vector", V, lambda e: e.tensor_sub(kE[:, :], kt1[:, :], kt2[:, :]))
            v4 = p.op("vector", V, lambda e: e.tensor_mul(kt1[:, :], bO.t[0:32, :], coskt[:, :]))
            v5 = p.op("vector", V, lambda e: e.tensor_mul(kt2[:, :], bE.t[0:32, :], sinkt[:, :]))
            v6 = p.op("vector", V, lambda e: e.tensor_add(kO[:, :], kt1[:, :], kt2[:, :]))
            bE.free = (V, v5)
            bO.free = (V, v4)
            ck_free = (V, v5)
            p.dma("gpsimd", KRS, self.krope_s[0:32, cols], kE[:, :], wait=[(V, v3)])
            krs_tok = p.dma("gpsimd", KRS, self.krope_s[32:64, cols], kO[:, :],
                            wait=[(V, v6)])
            s1 = p.op("scalar", S, lambda e: e.square(ksq[:, :], kE[:, :]),
                      wait=[(V, v3)])
            b2 = self.banks[1]
            self.acq(b2)
            p.wait("tensor", S, s1)
            tmid = p.op("tensor", T, lambda e: e.matmul(
                b2.t[:, :], self._mm(self.ones[0:32, :]), self._mm(ksq[:, :]),
                start=True, stop=False))
            s1 = p.op("scalar", S, lambda e: e.square(ksq[:, :], kO[:, :]),
                      wait=[(V, v6), (T, tmid)])
            p.wait("tensor", S, s1)
            tv = p.op("tensor", T, lambda e: e.matmul(
                b2.t[:, :], self._mm(self.ones[0:32, :]), self._mm(ksq[:, :]),
                start=False, stop=True))
            vv = p.op("vector", V, lambda e: e.reduce_max(nsc[:, 0:1], b2.t[:, :],
                                                          axis=AXX), wait=[(T, tv)])
            b2.free = (V, vv)
            p.op("vector", V, lambda e: e.tensor_max(self.kr2m[:, :], self.kr2m[:, :],
                                                     nsc[:, 0:1]))

    def phase_kv_up(self, ph):
        nc, p = self.nc, self.p
        E = ph.enter_context
        T, V, S, LD = self.T, self.V, self.S, self.LD
        WKs = self.sems(2)
        KSs = self.sems(2)
        VSs = self.sems(2)
        ckvr = E(nc.sbuf_tensor("ckvr", [128, KC * TS], F32))
        wkuv_sb = E(nc.sbuf_tensor("wkuv_sb", [128, KC * 2048], F32))
        wkb = [E(nc.sbuf_tensor(f"wkb{i}", [128, 512], F32)) for i in range(2)]
        ktile = [E(nc.sbuf_tensor(f"ktile{i}", [128, 512], F32)) for i in range(2)]
        ksq2 = E(nc.sbuf_tensor("ksq2", [128, 512], F32))
        vev = [E(nc.sbuf_tensor(f"vev{i}", [128, 2048], F32)) for i in range(2)]
        red = E(nc.sbuf_tensor("red", [128, 1], F32))

        loads = None
        for fc in range(KC):
            p.dma("sync", LD, ckvr[:, fc * TS:(fc + 1) * TS],
                  self.ckvn_s[fc * 128:(fc + 1) * 128, :])
        for fc in range(KC):
            loads = p.dma("sync", LD, wkuv_sb[:, fc * 2048:(fc + 1) * 2048],
                          self.wkuv[fc * 128:(fc + 1) * 128, :])

        wk_use = {}
        kspill = {}
        knorm_tv = None
        for h in range(H):
            wb = wkb[h % 2]
            wv = p.dma("sync", WKs[h % 2], wb[:, :],
                       self.wkuk[h * 128:(h + 1) * 128, :],
                       wait=[(T, wk_use.get(h - 2))] if h >= 2 else [])
            for m in range(NM):
                idx = h * NM + m
                cols = slice(m * 512, (m + 1) * 512)
                bk = self.banks[m % 2]
                self.acq(bk)
                p.waitt("tensor", wv)
                p.waitt("tensor", loads)
                tv = None
                for f in range(KC):
                    tv = p.op("tensor", T if f == KC - 1 else None,
                              lambda e, f=f, b=bk, w=wb, m=m: e.matmul(
                        b.t[:, :], self._mm(w[:, f * 128:(f + 1) * 128]),
                        self._mm(ckvr[:, f * TS + m * 512:f * TS + (m + 1) * 512]),
                        start=(f == 0), stop=(f == KC - 1)))
                wk_use[h] = tv
                kt = ktile[idx % 2]
                sv = p.op("scalar", S, lambda e, b=bk, k=kt: e.copy(k[:, :], b.t[:, :]),
                          wait=[(T, tv), kspill.get(idx - 2)])
                bk.free = (S, sv)
                kspill[idx] = p.dma("gpsimd", KSs[idx % 2],
                                    self.knope_s[h * 128:(h + 1) * 128, cols],
                                    kt[:, :], wait=[(S, sv)])
                s2 = p.op("scalar", S, lambda e, k=kt: e.square(ksq2[:, :], k[:, :]),
                          wait=[(T, knorm_tv)] if knorm_tv else [])
                bn = self.banks[2]
                self.acq(bn)
                p.wait("tensor", S, s2)
                knorm_tv = p.op("tensor", T, lambda e, b=bn: e.matmul(
                    b.t[:, :], self._mm(self.ones[:, :]), self._mm(ksq2[:, :]),
                    start=True, stop=True))
                vv = p.op("vector", V, lambda e, b=bn: e.reduce_max(
                    red[:, :], b.t[:, :], axis=AXX), wait=[(T, knorm_tv)])
                bn.free = (V, vv)
                p.op("vector", V, lambda e: e.tensor_max(self.kn2m[:, :],
                                                         self.kn2m[:, :], red[:, :]))

        vspill = {}
        for m in range(NM):
            for ts in range(4):
                idx = m * 4 + ts
                tok0 = m * 512 + ts * 128
                vb = vev[idx % 2]
                evs = None
                for seg in range(4):
                    bk = self.banks[3 + (idx * 4 + seg) % 2]
                    self.acq(bk)
                    p.waitt("tensor", loads)
                    tv = None
                    for f in range(KC):
                        tv = p.op("tensor", T if f == KC - 1 else None,
                                  lambda e, f=f, b=bk, t0=tok0, sg=seg: e.matmul(
                            b.t[:, :], self._mm(ckvr[:, f * TS + t0:f * TS + t0 + 128]),
                            self._mm(wkuv_sb[:, f * 2048 + sg * 512:
                                             f * 2048 + (sg + 1) * 512]),
                            start=(f == 0), stop=(f == KC - 1)))
                    w_ev = [(T, tv)]
                    if seg == 0 and idx >= 2:
                        w_ev.append(vspill.get(idx - 2))
                    evs = p.op("scalar", S, lambda e, b=bk, v=vb, sg=seg: e.copy(
                        v[:, sg * 512:(sg + 1) * 512], b.t[:, :]), wait=w_ev)
                    bk.free = (S, evs)
                vspill[idx] = p.dma("gpsimd", VSs[idx % 2],
                                    self.vs_s[tok0:tok0 + 128, :],
                                    vb[:, :], wait=[(S, evs)])

    def phase_q(self, ph):
        nc, p = self.nc, self.p
        E = ph.enter_context
        T, V, S, LD = self.T, self.V, self.S, self.LD
        XO = self.sems(1)
        WQs = self.sems(2)
        WUs = self.sems(2)
        QS1 = self.sems(1)
        QS2 = self.sems(1)
        xo = E(nc.sbuf_tensor("xo", [128, FC * 512], F32))
        xn = E(nc.sbuf_tensor("xnq", [128, FC * 512], F32))
        cq = E(nc.sbuf_tensor("cq", [128, QC * TQ], F32))
        stm = E(nc.sbuf_tensor("stmq", [128, 512], F32))
        strr = E(nc.sbuf_tensor("strq", [128, 512], F32))
        st1 = E(nc.sbuf_tensor("st1q", [128, 512], F32))
        st2 = E(nc.sbuf_tensor("st2q", [128, 512], F32))
        nsc = E(nc.sbuf_tensor("nscq", [128, 512], F32))
        wbuf = [E(nc.sbuf_tensor(f"wqb{i}", [128, 6 * 128], F32)) for i in range(2)]
        cosq_sb = E(nc.sbuf_tensor("cosq_sb", [128, TQ], F32))
        sinq_sb = E(nc.sbuf_tensor("sinq_sb", [128, TQ], F32))
        qE = E(nc.sbuf_tensor("qE", [128, 4 * 512], F32))
        qO = E(nc.sbuf_tensor("qO", [128, 512], F32))
        qt1 = E(nc.sbuf_tensor("qt1", [128, 512], F32))
        qt2 = E(nc.sbuf_tensor("qt2", [128, 512], F32))
        qsqt = E(nc.sbuf_tensor("qsqt", [128, 512], F32))
        qevac = E(nc.sbuf_tensor("qevac", [128, 512], F32))

        cql_sb = E(nc.sbuf_tensor("cql_sb", [128, TQ], F32))
        p.dma("sync", LD, cosq_sb[:, :], self.cosq[:, :])
        p.dma("sync", LD, sinq_sb[:, :], self.sinq[:, :])
        cvl = p.dma("sync", LD, cql_sb[:, :], self.cosql[:, :])
        p.waitt("vector", cvl)
        p.op("vector", V, lambda e: e.tensor_add(cosq_sb[:, :], cosq_sb[:, :],
                                                 cql_sb[:, :]))
        cvl = p.dma("sync", LD, cql_sb[:, :], self.sinql[:, :],
                    wait=[(V, p.cur(V))])
        p.waitt("vector", cvl)
        cv_q0 = p.op("vector", V, lambda e: e.tensor_add(sinq_sb[:, :], sinq_sb[:, :],
                                                         cql_sb[:, :]))
        cv_q = (V, cv_q0)

        xo_free = None
        xn_free = None
        wq_free = {}
        for th in range(2):
            colq = slice(th * 512, (th + 1) * 512)
            lv = p.dma("sync", XO, xo[:, :].rearrange("p (a t) -> p a t", a=FC),
                       self.xown[:, colq].rearrange("(a p) t -> p a t", p=128),
                       wait=[xo_free])
            if xn_free is not None:
                p.wait("scalar", *xn_free)
            rv = self.ln_stats(xo, FC, 512, stm[:, :], strr[:, :], xn,
                               st1[:, :], st2[:, :], D, EPS_LN, lv)
            vv = None
            for f in range(FC):
                p.op("vector", V, lambda e, f=f: e.tensor_sub(
                    xn[:, f * 512:(f + 1) * 512], xo[:, f * 512:(f + 1) * 512],
                    stm[:, :]), wait=[(V, rv)])
                vv = p.op("vector", V, lambda e, f=f: e.tensor_mul(
                    xn[:, f * 512:(f + 1) * 512], xn[:, f * 512:(f + 1) * 512],
                    strr[:, :]))
            xo_free = (V, vv)
            p.wait("tensor", V, vv)
            tv = None
            for og in range(2):
                for f in range(FC):
                    wv = p.dma("sync", WQs[f % 2], wbuf[f % 2][:, :],
                               self.wqd[f * 128:(f + 1) * 128, og * 768:(og + 1) * 768],
                               wait=[(T, wq_free.get(f % 2))]
                               if wq_free.get(f % 2) else [])
                    p.waitt("tensor", wv)
                    for i in range(6):
                        bk = self.banks[i]
                        if f == 0:
                            self.acq(bk)
                        tv = p.op(
                            "tensor", T if i == 5 else None,
                            lambda e, f=f, i=i, b=bk, w=wbuf[f % 2]: e.matmul(
                                b.t[:, :], self._mm(w[:, i * 128:(i + 1) * 128]),
                                self._mm(xn[:, f * 512:(f + 1) * 512]),
                                start=(f == 0), stop=(f == FC - 1)))
                    wq_free[f % 2] = tv
                for i in range(6):
                    oc = og * 6 + i
                    sv = p.op("scalar", S, lambda e, o=oc, b=self.banks[i], t=th:
                              e.copy(cq[:, o * TQ + t * 512:o * TQ + t * 512 + 512],
                                     b.t[:, :]), wait=[(T, tv)])
                    self.banks[i].free = (S, sv)
            xn_free = (T, tv)
            rv = self.rms_stats(
                lambda f, t=th: cq[:, f * TQ + t * 512:f * TQ + t * 512 + 512],
                QC, 512,
                strr[:, :], st1[:, :], st2[:, :], nsc[:, :], QR, EPS_RMS, bank_idx=6)
            for oc in range(QC):
                p.op("vector", V, lambda e, o=oc, t=th: e.scalar_tensor_tensor(
                    cq[:, o * TQ + t * 512:o * TQ + t * 512 + 512],
                    cq[:, o * TQ + t * 512:o * TQ + t * 512 + 512],
                    self.gqs[:, o:o + 1], strr[:, :], ALU.mult, ALU.mult),
                    wait=[(V, rv)])
        cq_ready = p.cur(V)
        p.wait("tensor", V, cq_ready)

        wub = [E(nc.sbuf_tensor(f"wub{i}", [128, 6 * 128], F32)) for i in range(2)]
        wu_free = {}
        qs1_tok = None
        qs2_tok = None
        norm_tv = None
        mix_done = None
        for th in range(2):
            ropeE_done = {}
            for og in range(4):
                tv = None
                for f in range(QC):
                    wv = p.dma("sync", WUs[f % 2], wub[f % 2][:, :],
                               self.wqu[f * 128:(f + 1) * 128, og * 768:(og + 1) * 768],
                               wait=[(T, wu_free.get(f % 2))]
                               if wu_free.get(f % 2) else [])
                    p.waitt("tensor", wv)
                    for i in range(6):
                        bk = self.banks[i]
                        if f == 0:
                            self.acq(bk)
                        tv = p.op(
                            "tensor", T if i == 5 else None,
                            lambda e, f=f, i=i, b=bk, w=wub[f % 2], t=th: e.matmul(
                                b.t[:, :], self._mm(w[:, i * 128:(i + 1) * 128]),
                                self._mm(cq[:, f * TQ + t * 512:f * TQ + t * 512 + 512]),
                                start=(f == 0), stop=(f == QC - 1)))
                    wu_free[f % 2] = tv
                for i in range(6):
                    oc = og * 6 + i
                    bk = self.banks[i]
                    if oc < 16:
                        sv = p.op("scalar", S, lambda e, b=bk: e.copy(
                            qevac[:, :], b.t[:, :]), wait=[(T, tv), qs1_tok])
                        qs1_tok = p.dma("gpsimd", QS1,
                                        self.qs_s[oc * 128:(oc + 1) * 128,
                                                  th * 512:(th + 1) * 512],
                                        qevac[:, :], wait=[(S, sv)])
                        bk.free = (S, sv)
                        s2 = p.op("scalar", S, lambda e: e.square(qsqt[:, :], qevac[:, :]),
                                  wait=([(T, norm_tv)] if norm_tv else []) +
                                       ([(V, mix_done)] if mix_done else []))
                        bn = self.banks[7]
                        self.acq(bn)
                        p.wait("tensor", S, s2)
                        norm_tv = p.op("tensor", T, lambda e, b=bn: e.matmul(
                            b.t[:, :], self._mm(self.ones[:, :]), self._mm(qsqt[:, :]),
                            start=True, stop=True))
                        vv = p.op("vector", V, lambda e, b=bn: e.reduce_max(
                            nsc[:, 0:1], b.t[:, :], axis=AXX), wait=[(T, norm_tv)])
                        bn.free = (V, vv)
                        mix_done = p.op("vector", V, lambda e: e.tensor_max(
                            self.qn2m[:, :], self.qn2m[:, :], nsc[:, 0:1]))
                    elif oc < 20:
                        g = oc - 16
                        sv = p.op("scalar", S, lambda e, b=bk, g=g: e.copy(
                            qE[:, g * 512:(g + 1) * 512], b.t[:, :]), wait=[(T, tv)])
                        bk.free = (S, sv)
                        ropeE_done[g] = sv
                    else:
                        g = oc - 20
                        sv = p.op("scalar", S, lambda e, b=bk: e.copy(qO[:, :], b.t[:, :]),
                                  wait=[(T, tv)])
                        bk.free = (S, sv)
                        cs = slice(th * 512, (th + 1) * 512)
                        v1 = p.op("vector", V, lambda e, g=g, cs=cs: e.tensor_mul(
                            qt1[:, :], qE[:, g * 512:(g + 1) * 512], cosq_sb[:, cs]),
                            wait=[(S, sv), (S, ropeE_done[g]), cv_q, qs2_tok] +
                                 ([(T, norm_tv)] if norm_tv else []))
                        p.op("vector", V, lambda e, cs=cs: e.tensor_mul(
                            qt2[:, :], qO[:, :], sinq_sb[:, cs]))
                        v3 = p.op("vector", V, lambda e: e.tensor_sub(
                            qt1[:, :], qt1[:, :], qt2[:, :]))
                        p.op("vector", V, lambda e, cs=cs: e.tensor_mul(
                            qt2[:, :], qO[:, :], cosq_sb[:, cs]))
                        p.op("vector", V, lambda e, g=g, cs=cs: e.tensor_mul(
                            qsqt[:, :], qE[:, g * 512:(g + 1) * 512], sinq_sb[:, cs]))
                        v6 = p.op("vector", V, lambda e: e.tensor_add(
                            qt2[:, :], qt2[:, :], qsqt[:, :]))
                        for hh in range(4):
                            hd = g * 4 + hh
                            p.dma("gpsimd", QS2,
                                  self.qrope_s[hd * 64:hd * 64 + 32, cs],
                                  qt1[hh * 32:(hh + 1) * 32, :], wait=[(V, v3)])
                            qs2_tok = p.dma(
                                "gpsimd", QS2,
                                self.qrope_s[hd * 64 + 32:(hd + 1) * 64, cs],
                                qt2[hh * 32:(hh + 1) * 32, :], wait=[(V, v6)])
                        s2 = p.op("scalar", S, lambda e: e.square(qsqt[:, :], qt1[:, :]),
                                  wait=[(V, v6)])
                        bn = self.banks[7]
                        self.acq(bn)
                        p.wait("tensor", S, s2)
                        tmid = p.op("tensor", T, lambda e, b=bn: e.matmul(
                            b.t[:, :], self._mm(self.ones[:, :]), self._mm(qsqt[:, :]),
                            start=True, stop=False))
                        s2 = p.op("scalar", S, lambda e: e.square(qsqt[:, :], qt2[:, :]),
                                  wait=[(T, tmid)])
                        p.wait("tensor", S, s2)
                        norm_tv = p.op("tensor", T, lambda e, b=bn: e.matmul(
                            b.t[:, :], self._mm(self.ones[:, :]), self._mm(qsqt[:, :]),
                            start=False, stop=True))
                        vv = p.op("vector", V, lambda e, b=bn: e.reduce_max(
                            nsc[:, 0:1], b.t[:, :], axis=AXX), wait=[(T, norm_tv)])
                        bn.free = (V, vv)
                        mix_done = p.op("vector", V, lambda e: e.tensor_max(
                            self.qr2m[:, :], self.qr2m[:, :], nsc[:, 0:1]))

    def phase_bias(self):
        p, S, V = self.p, self.S, self.V
        v2 = p.op("vector", V, lambda e: e.tensor_scalar(
            self.qr2m[:, :], self.qr2m[:, :], 4.0, None, ALU.mult))
        v2 = p.op("vector", V, lambda e: e.tensor_add(
            self.qr2m[:, :], self.qr2m[:, :], self.qn2m[:, :]))
        s1 = p.op("scalar", S, lambda e: e.sqrt(self.qr2m[:, :], self.qr2m[:, :]),
                  wait=[(V, v2)])
        v3 = p.op("vector", V, lambda e: e.tensor_add(
            self.kr2m[:, :], self.kr2m[:, :], self.kn2m[:, :]))
        s2 = p.op("scalar", S, lambda e: e.sqrt(self.kr2m[:, :], self.kr2m[:, :]),
                  wait=[(V, v3)])
        p.op("vector", V, lambda e: e.tensor_mul(
            self.negb[:, :], self.qr2m[:, :], self.kr2m[:, :]), wait=[(S, s1), (S, s2)])
        p.op("vector", V, lambda e: e.tensor_scalar(
            self.negb[:, :], self.negb[:, :], -1.0, None, ALU.mult))
        self.bias_ready = p.op("vector", V, lambda e: e.tensor_scalar(
            self.biasc[:, :], self.dead[:, :], self.negb[:, 0:1], None, ALU.add))

    def phase_attn(self, ph):
        nc, p = self.nc, self.p
        E = ph.enter_context
        T, V, S = self.T, self.V, self.S
        KVs = self.sems(2)
        QL = self.sems(1)
        attn = E(nc.sbuf_tensor("attn", [128, H * TQ], F32))
        self.attn_sb = attn
        kbuf = [E(nc.sbuf_tensor(f"kbuf{i}", [128, 512], F32)) for i in range(2)]
        krb = [E(nc.sbuf_tensor(f"krb{i}", [64, 512], F32)) for i in range(2)]
        vbuf = [E(nc.sbuf_tensor(f"vbuf{i}", [128, 512], F32)) for i in range(2)]
        qnb = [E(nc.sbuf_tensor(f"qnb{i}", [128, 512], F32)) for i in range(2)]
        qrb = [E(nc.sbuf_tensor(f"qrb{i}", [64, 512], F32)) for i in range(2)]
        pT = [E(nc.sbuf_tensor(f"pT{i}", [128, 512], F32)) for i in range(4)]
        sumsb = E(nc.sbuf_tensor("sumsb", [128, 512], F32))
        invb = E(nc.sbuf_tensor("invb", [128, 512], F32))
        rfn = E(nc.sbuf_tensor("rfn", [128, 512], F32))

        p.wait("scalar", V, self.bias_ready)
        bS = [self.banks[0], self.banks[1], self.banks[6], self.banks[7]]
        bOut = {0: self.banks[2], 1: self.banks[3]}
        bSum = {0: self.banks[4], 1: self.banks[5]}

        kv_use = {}
        q_use = None
        exp_done = {}
        pt_free = {}
        sum_fin = None

        for h in range(H):
            qw = [(T, q_use)] if q_use else []
            ql = None
            for s in range(2):
                p.dma("sync", QL, qnb[s][:, :],
                      self.qs_s[h * 128:(h + 1) * 128, s * 512:(s + 1) * 512],
                      wait=qw)
                ql = p.dma("sync", QL, qrb[s][:, :],
                           self.qrope_s[h * 64:(h + 1) * 64, s * 512:(s + 1) * 512])
            out_open = {0: False, 1: False}
            for sl in range(NM):
                ki = sl % 2
                kvw = [(T, kv_use.get((sl - 2) % NM))] if (h > 0 or sl >= 2) else []
                p.dma("sync", KVs[ki], kbuf[ki][:, :],
                      self.knope_s[h * 128:(h + 1) * 128, sl * 512:(sl + 1) * 512],
                      wait=kvw)
                p.dma("sync", KVs[ki], krb[ki][:, :],
                      self.krope_s[:, sl * 512:(sl + 1) * 512])
                kl = p.dma("sync", KVs[ki],
                           vbuf[ki][:, :].rearrange("p (a v) -> p a v", a=4),
                           self.vs_s[sl * 512:(sl + 1) * 512,
                                     h * 128:(h + 1) * 128].rearrange(
                                         "(a p) v -> p a v", p=128))
                iters = ([0] if sl < 4 else []) + [1]
                for s in iters:
                    bias_col = sl if s == 0 else 4 + sl
                    diag = (s == 0 and sl == 3) or (s == 1 and sl == 8)
                    pv_queue = []

                    def emit_pv(job, ki=ki):
                        (s_, ks_, pt_, s1_, st_, sp_) = job
                        bo, bs_ = bOut[s_], bSum[s_]
                        p.wait("tensor", S, s1_)
                        p.op("tensor", None, lambda e, b=bo, ki=ki, ks=ks_, o=pt_,
                             st=st_, sp=sp_: e.matmul(
                            b.t[:, :], self._mm(vbuf[ki][:, ks * 128:(ks + 1) * 128]),
                            self._mm(o[:, :]), start=st, stop=sp))
                        t2 = p.op("tensor", T, lambda e, b=bs_, o=pt_,
                                  st=st_, sp=sp_: e.matmul(
                            b.t[:, :], self._mm(self.ones[:, :]), self._mm(o[:, :]),
                            start=st, stop=sp))
                        pt_free[pT.index(pt_)] = t2

                    for ks in range(4):
                        sb = bS[ks]
                        self.acq(sb)
                        p.wait("tensor", S, exp_done.get(id(sb)))
                        p.waitt("tensor", kl)
                        p.waitt("tensor", ql)
                        p.op("tensor", None, lambda e, b=sb, ki=ki, ks=ks, s=s: e.matmul(
                            b.t[:, :], self._mm(kbuf[ki][:, ks * 128:(ks + 1) * 128]),
                            self._mm(qnb[s][:, :]), start=True, stop=False))
                        t1 = p.op("tensor", T, lambda e, b=sb, ki=ki, ks=ks, s=s: e.matmul(
                            b.t[:, :], self._mm(krb[ki][:, ks * 128:(ks + 1) * 128]),
                            self._mm(qrb[s][:, :]), start=False, stop=True))
                        wait_exp = [(T, t1)]
                        if diag:
                            v1 = p.op("vector", V, lambda e, b=sb, ks=ks: e.tensor_add(
                                b.t[:, :], b.t[:, :],
                                self.mdiag[:, ks * 512:(ks + 1) * 512]),
                                wait=[(T, t1)])
                            wait_exp = [(V, v1)]
                        pt = pT[ks]
                        p.wait("scalar", T, pt_free.get(ks))
                        s1 = p.op("scalar", S, lambda e, b=sb, o=pt, bc=bias_col:
                                  e.activation(o[:, :], b.t[:, :], AF.Exp,
                                               bias=self.biasc[:, bc:bc + 1],
                                               scale=1.0), wait=wait_exp)
                        exp_done[id(sb)] = s1
                        if not out_open[s]:
                            self.acq(bOut[s])
                            self.acq(bSum[s])
                        stop_now = diag and (ks == 3)
                        pv_queue.append((s, ks, pt, s1, not out_open[s], stop_now))
                        out_open[s] = True
                        if len(pv_queue) >= 2:
                            emit_pv(pv_queue.pop(0))
                    while pv_queue:
                        emit_pv(pv_queue.pop(0))
                kv_use[sl] = p.cur(T)
            q_use = p.cur(T)
            for s in range(2):
                bo, bs_ = bOut[s], bSum[s]
                tv = p.cur(T)
                sv = p.op("scalar", S, lambda e, b=bs_: e.copy(sumsb[:, :], b.t[:, :]),
                          wait=[(T, tv)] + ([(V, sum_fin)] if sum_fin else []))
                bs_.free = (S, sv)
                v1 = p.op("vector", V, lambda e: e.reciprocal(invb[:, :], sumsb[:, :]),
                          wait=[(S, sv)])
                p.op("vector", V, lambda e: e.tensor_mul(rfn[:, :], sumsb[:, :],
                                                         invb[:, :]))
                p.op("vector", V, lambda e: e.tensor_scalar(
                    rfn[:, :], rfn[:, :], -1.0, 2.0, ALU.mult, ALU.add))
                v1 = p.op("vector", V, lambda e: e.tensor_mul(invb[:, :], invb[:, :],
                                                              rfn[:, :]))
                v2 = p.op("vector", V, lambda e, b=bo, h=h, s=s: e.tensor_mul(
                    attn[:, h * TQ + s * 512:h * TQ + (s + 1) * 512], b.t[:, :],
                    invb[:, :]), wait=[(V, v1)])
                bo.free = (V, v2)
                sum_fin = v2

    def phase_oproj(self, ph):
        nc, p = self.nc, self.p
        E = ph.enter_context
        T, V, S = self.T, self.V, self.S
        WOs = self.sems(2)
        XLs = self.sems(2)
        X2Ss = self.sems(2)
        attn = self.attn_sb
        wob = [E(nc.sbuf_tensor(f"wob{i}", [128, 2048], F32)) for i in range(2)]
        xot = [E(nc.sbuf_tensor(f"xot{i}", [128, 512], F32)) for i in range(2)]
        x2t = [E(nc.sbuf_tensor(f"x2t{i}", [128, 512], F32)) for i in range(2)]

        last_mm = {}
        x2_spill = {}
        vadd = {}
        for oc in range(16):
            wv = p.dma("sync", WOs[oc % 2], wob[oc % 2][:, :],
                       self.wod[oc * 128:(oc + 1) * 128, :],
                       wait=[(T, last_mm.get(oc - 2))] if oc >= 2 else [])
            for th in range(2):
                idx = oc * 2 + th
                bk = self.banks[idx % 2]
                self.acq(bk)
                lx = p.dma("sync", XLs[th], xot[th][:, :],
                           self.xown[oc * 128:(oc + 1) * 128, th * 512:(th + 1) * 512],
                           wait=[(V, vadd.get(idx - 2))] if idx >= 2 else [])
                p.waitt("tensor", wv)
                tv = None
                for vc in range(16):
                    tv = p.op("tensor", T if vc == 15 else None,
                              lambda e, b=bk, v=vc, th=th, w=wob[oc % 2]: e.matmul(
                        b.t[:, :], self._mm(w[:, v * 128:(v + 1) * 128]),
                        self._mm(attn[:, v * TQ + th * 512:v * TQ + th * 512 + 512]),
                        start=(v == 0), stop=(v == 15)))
                last_mm[oc] = tv
                v1 = p.op("vector", V, lambda e, b=bk, th=th, x=x2t[idx % 2]:
                          e.tensor_add(x[:, :], b.t[:, :], xot[th][:, :]),
                          wait=[(T, tv), lx, x2_spill.get(idx - 2)])
                vadd[idx] = v1
                bk.free = (V, v1)
                x2_spill[idx] = p.dma(
                    "gpsimd", X2Ss[idx % 2],
                    self.x2_s[oc * 128:(oc + 1) * 128, th * 512:(th + 1) * 512],
                    x2t[idx % 2][:, :], wait=[(V, v1)])

    def phase_ffn1(self, ph):
        nc, p = self.nc, self.p
        E = ph.enter_context
        T, V, S = self.T, self.V, self.S
        X2Ls = self.sems(2)
        WGs = self.sems(2)
        WU5s = self.sems(2)
        ZSPs = self.sems(2)
        x2b = [E(nc.sbuf_tensor(f"x2b{i}", [128, TQ], F32)) for i in range(2)]
        xsq = E(nc.sbuf_tensor("xsqf", [128, TQ], F32))
        m2 = E(nc.sbuf_tensor("m2", [128, TQ], F32))
        r2 = E(nc.sbuf_tensor("r2", [128, TQ], F32))
        f1 = E(nc.sbuf_tensor("f1", [128, TQ], F32))
        f2 = E(nc.sbuf_tensor("f2", [128, TQ], F32))
        y1 = E(nc.sbuf_tensor("y1", [128, 18 * TQ], F32))
        hsu = [E(nc.sbuf_tensor(f"hsu{i}", [128, 128], F32)) for i in range(2)]
        wupb = [E(nc.sbuf_tensor(f"wupb{i}", [128, 2048], F32)) for i in range(2)]
        wgb = [E(nc.sbuf_tensor(f"wgb{i}", [128, 2048], F32)) for i in range(2)]
        gt = [E(nc.sbuf_tensor(f"gt{i}", [128, 512], F32)) for i in range(2)]
        zc = [E(nc.sbuf_tensor(f"zc{i}", [128, TQ], F32)) for i in range(2)]

        bS0, bQ0, bS1, bQ1 = (self.banks[i] for i in range(4))
        for b in (bS0, bQ0, bS1, bQ1):
            self.acq(b)
        sq_done = {}
        tvv = None
        for fc in range(FC):
            lv = p.dma("sync", X2Ls[fc % 2], x2b[fc % 2][:, :],
                       self.x2_s[fc * 128:(fc + 1) * 128, :],
                       wait=[(T, sq_done.get(fc - 2))] if fc >= 2 else [])
            p.waitt("scalar", lv)
            sv = p.op("scalar", S, lambda e, x=x2b[fc % 2]: e.square(xsq[:, :], x[:, :]),
                      wait=[(T, sq_done.get(fc - 1))] if fc >= 1 else [])
            p.waitt("tensor", lv)
            p.wait("tensor", S, sv)
            for th in range(2):
                cs = slice(th * 512, (th + 1) * 512)
                p.op("tensor", None, lambda e, x=x2b[fc % 2], f=fc,
                     b=(bS0 if th == 0 else bS1), cs=cs: e.matmul(
                    b.t[:, :], self._mm(self.ones[:, :]), self._mm(x[:, cs]),
                    start=(f == 0), stop=(f == FC - 1)))
                tvv = p.op("tensor", T if th == 1 else None, lambda e, f=fc,
                           b=(bQ0 if th == 0 else bQ1), cs=cs: e.matmul(
                    b.t[:, :], self._mm(self.ones[:, :]), self._mm(xsq[:, cs]),
                    start=(f == 0), stop=(f == FC - 1)))
            sq_done[fc] = tvv
        tv = sq_done[FC - 1]
        nv = None
        for th in range(2):
            cs = slice(th * 512, (th + 1) * 512)
            bs_, bq = (bS0, bQ0) if th == 0 else (bS1, bQ1)
            sv = p.op("scalar", S, lambda e, b=bs_, cs=cs: e.mul(
                m2[:, cs], b.t[:, :], 1.0 / D), wait=[(T, tv)])
            p.op("scalar", S, lambda e, b=bq, cs=cs: e.mul(
                f1[:, cs], b.t[:, :], 1.0 / D))
            sv2 = p.cur(S)
            bs_.free = (S, sv)
            bq.free = (S, sv2)
            p.op("scalar", S, lambda e, cs=cs: e.square(f2[:, cs], m2[:, cs]))
            sv3 = p.cur(S)
            vsub = p.op("vector", V, lambda e, cs=cs: e.tensor_sub(
                f1[:, cs], f1[:, cs], f2[:, cs]), wait=[(S, sv3)])
            nv = self.newton_rsqrt(r2[:, cs], f1[:, cs], EPS_LN, f2[:, cs],
                                   xsq[:, cs], wait=[(V, vsub)])
        rv = p.op("scalar", S, lambda e: e.mul(r2[:, :], r2[:, :], FS1),
                  wait=[(V, nv)])

        slot = list(range(16))
        mm_done = {}
        for fc in range(FC):
            lv = p.dma("sync", X2Ls[fc % 2], x2b[fc % 2][:, :],
                       self.x2_s[fc * 128:(fc + 1) * 128, :],
                       wait=[(T, mm_done.get(fc - 2, tv))])
            v1 = p.op("vector", V, lambda e, x=x2b[fc % 2]: e.tensor_sub(
                f1[:, :], x[:, :], m2[:, :]),
                wait=[lv, (S, rv)] +
                     ([(T, mm_done.get(fc - 1))] if fc >= 1 else []))
            v2 = p.op("vector", V, lambda e: e.tensor_mul(f1[:, :], f1[:, :], r2[:, :]))
            hvv = p.op("vector", V, lambda e, f=fc, hb=hsu[fc % 2]: e.tensor_scalar_mul(
                hb[:, :], self.h128[:, :], self.suug[:, f:f + 1]),
                wait=[(T, mm_done.get(fc - 2))] if fc >= 2 else [])
            for th in range(2):
                bk = self.banks[4 + (fc * 2 + th) % 2]
                self.acq(bk)
                p.wait("tensor", V, max(v2, hvv))
                tvv = p.op("tensor", T, lambda e, b=bk, hb=hsu[fc % 2], th=th: e.matmul(
                    b.t[:, :], self._mm(hb[:, :]),
                    self._mm(f1[:, th * 512:(th + 1) * 512]), start=True, stop=True))
                sv = p.op("scalar", S, lambda e, b=bk, f=fc, th=th: e.copy(
                    y1[:, f * TQ + th * 512:f * TQ + th * 512 + 512], b.t[:, :]),
                    wait=[(T, tvv)])
                bk.free = (S, sv)
            mm_done[fc] = tvv
        p.wait("vector", S, p.cur(S))
        spare = [16, 17]
        for st in range(4):
            bit = 1 << st
            newslot = list(slot)
            for c in range(FC):
                if c & bit:
                    continue
                u, vtx = c, c | bit
                su_, sv_ = slot[u], slot[vtx]
                du, dv = spare[0], spare[1]
                p.op("vector", V, lambda e, a=su_, b=sv_, d=du: e.tensor_add(
                    y1[:, d * TQ:(d + 1) * TQ], y1[:, a * TQ:(a + 1) * TQ],
                    y1[:, b * TQ:(b + 1) * TQ]))
                p.op("vector", V, lambda e, a=su_, b=sv_, d=dv: e.tensor_sub(
                    y1[:, d * TQ:(d + 1) * TQ], y1[:, a * TQ:(a + 1) * TQ],
                    y1[:, b * TQ:(b + 1) * TQ]))
                newslot[u], newslot[vtx] = du, dv
                spare = [su_, sv_]
            slot = newslot
        bfly_done = p.cur(V)
        p.wait("tensor", V, bfly_done)

        w_use = {}
        z_spill = {}
        v1 = None
        for c in range(IC):
            wgv = p.dma("sync", WGs[c % 2], wgb[c % 2][:, :],
                        self.wgate[c * 128:(c + 1) * 128, :],
                        wait=[(T, w_use.get(c - 2))] if c >= 2 else [])
            wvu = p.dma("sync", WU5s[c % 2], wupb[c % 2][:, :],
                        self.wup[c * 128:(c + 1) * 128, :])
            for th in range(2):
                bg = self.banks[(c * 2 + th) % 4]
                self.acq(bg)
                p.waitt("tensor", wgv)
                p.waitt("tensor", wvu)
                for f in range(FC):
                    p.op("tensor", None, lambda e, b=bg, f=f, th=th, w=wgb[c % 2],
                         sf=slot[f]: e.matmul(
                        b.t[:, :], self._mm(w[:, f * 128:(f + 1) * 128]),
                        self._mm(y1[:, sf * TQ + th * 512:sf * TQ + th * 512 + 512]),
                        start=(f == 0), stop=(f == FC - 1)))
                bu = self.banks[4 + (c * 2 + th) % 4]
                self.acq(bu)
                tvu = None
                for f in range(FC):
                    tvu = p.op("tensor", T if f == FC - 1 else None,
                               lambda e, b=bu, f=f, th=th, w=wupb[c % 2],
                               sf=slot[f]: e.matmul(
                        b.t[:, :], self._mm(w[:, f * 128:(f + 1) * 128]),
                        self._mm(y1[:, sf * TQ + th * 512:sf * TQ + th * 512 + 512]),
                        start=(f == 0), stop=(f == FC - 1)))
                w_use[c] = tvu
                sv = p.op("scalar", S, lambda e, b=bg, c=c, g=gt[th]:
                          e.activation(g[:, :], b.t[:, :], AF.Silu, bias=0.0,
                                       scale=self.scgt[:, c:c + 1]),
                          wait=[(T, tvu)] + ([(V, v1)] if v1 else []))
                bg.free = (S, sv)
                v1 = p.op("vector", V, lambda e, b=bu, th=th, c=c, g=gt[th],
                          z=zc[c % 2]: e.scalar_tensor_tensor(
                    z[:, th * 512:(th + 1) * 512], b.t[:, :], self.scup[:, c:c + 1],
                    g[:, :], ALU.mult, ALU.mult),
                    wait=[(S, sv)] +
                         ([z_spill.get(c - 2)] if (th == 0 and c >= 2) else []))
                bu.free = (V, v1)
            z_spill[c] = p.dma("gpsimd", ZSPs[c % 2],
                               self.z_s[c * 128:(c + 1) * 128, :],
                               zc[c % 2][:, :], wait=[(V, v1)])

    def phase_ffn2(self, ph):
        nc, p = self.nc, self.p
        E = ph.enter_context
        T, V, S = self.T, self.V, self.S
        ZLs = self.sems(4)
        WDs = self.sems(2)
        XT6s = self.sems(2)
        OUTs = self.sems(2)
        zwin = E(nc.sbuf_tensor("zwin", [128, 66 * 512], F32))
        hsu = [E(nc.sbuf_tensor(f"hs2{i}", [128, 128], F32)) for i in range(2)]
        wdb = [E(nc.sbuf_tensor(f"wdb{i}", [128, 6 * 128], F32)) for i in range(2)]
        x2t = [E(nc.sbuf_tensor(f"x2f{i}", [128, 512], F32)) for i in range(2)]
        ob = [E(nc.sbuf_tensor(f"ob{i}", [128, 512], F32)) for i in range(2)]

        og_list = [(0, 6), (6, 6), (12, 4)]
        wd_free = {}
        for th in range(2):
            cs = slice(th * 512, (th + 1) * 512)
            p.wait("sync", T, p.cur(T))
            p.wait("gpsimd", T, p.cur(T))
            slot = list(range(IC))
            hv = {}
            for c in range(IC):
                lv = p.dma("sync", ZLs[c % 4], zwin[:, c * 512:(c + 1) * 512],
                           self.z_s[c * 128:(c + 1) * 128, cs],
                           wait=[(T, hv.get(c - 4))] if hv.get(c - 4) else [])
                hvv = p.op("vector", V, lambda e, c=c, hb=hsu[c % 2]: e.tensor_scalar_mul(
                    hb[:, :], self.h128[:, :], self.sudp[:, c:c + 1]),
                    wait=[(T, hv.get(c - 2))] if hv.get(c - 2) else [])
                bk = self.banks[6 + c % 2]
                self.acq(bk)
                p.waitt("tensor", lv)
                p.wait("tensor", V, hvv)
                tvv = p.op("tensor", T, lambda e, b=bk, c=c, hb=hsu[c % 2]: e.matmul(
                    b.t[:, :], self._mm(hb[:, :]),
                    self._mm(zwin[:, c * 512:(c + 1) * 512]), start=True, stop=True))
                hv[c] = tvv
                sv = p.op("scalar", S, lambda e, b=bk, c=c: e.copy(
                    zwin[:, c * 512:(c + 1) * 512], b.t[:, :]), wait=[(T, tvv)])
                bk.free = (S, sv)
            p.wait("vector", S, p.cur(S))
            spare = [64, 65]
            for st in range(6):
                bit = 1 << st
                newslot = list(slot)
                for c in range(IC):
                    if c & bit:
                        continue
                    u, vtx = c, c | bit
                    su_, sv_ = slot[u], slot[vtx]
                    du, dv = spare[0], spare[1]
                    p.op("vector", V, lambda e, a=su_, b=sv_, d=du: e.tensor_add(
                        zwin[:, d * 512:(d + 1) * 512], zwin[:, a * 512:(a + 1) * 512],
                        zwin[:, b * 512:(b + 1) * 512]))
                    p.op("vector", V, lambda e, a=su_, b=sv_, d=dv: e.tensor_sub(
                        zwin[:, d * 512:(d + 1) * 512], zwin[:, a * 512:(a + 1) * 512],
                        zwin[:, b * 512:(b + 1) * 512]))
                    newslot[u], newslot[vtx] = du, dv
                    spare = [su_, sv_]
                slot = newslot
            y_done = p.cur(V)
            p.wait("tensor", V, y_done)
            for (oc0, ogn) in og_list:
                tv = None
                for c in range(IC):
                    wvd = p.dma("sync", WDs[c % 2], wdb[c % 2][:, 0:ogn * 128],
                                self.wdown[c * 128:(c + 1) * 128,
                                           oc0 * 128:(oc0 + ogn) * 128],
                                wait=[(T, wd_free.get(c % 2))]
                                if wd_free.get(c % 2) else [])
                    p.waitt("tensor", wvd)
                    for i in range(ogn):
                        bk = self.banks[i]
                        if c == 0:
                            self.acq(bk)
                        tv = p.op("tensor", T if i == ogn - 1 else None,
                                  lambda e, b=bk, c=c, i=i, w=wdb[c % 2],
                                  sc=slot[c]: e.matmul(
                            b.t[:, :], self._mm(w[:, i * 128:(i + 1) * 128]),
                            self._mm(zwin[:, sc * 512:(sc + 1) * 512]),
                            start=(c == 0), stop=(c == IC - 1)))
                    wd_free[c % 2] = tv
                odma = {}
                vprev = {}
                for i in range(ogn):
                    oc = oc0 + i
                    lx = p.dma("sync", XT6s[i % 2], x2t[i % 2][:, :],
                               self.x2_s[oc * 128:(oc + 1) * 128, cs],
                               wait=[(V, vprev.get(i - 2))] if i >= 2 else [])
                    v1 = p.op("vector", V, lambda e, b=self.banks[i], i=i, oc=oc:
                              e.scalar_tensor_tensor(
                        ob[i % 2][:, :], b.t[:, :], self.scdp[:, oc:oc + 1],
                        x2t[i % 2][:, :], ALU.mult, ALU.add),
                        wait=[(T, tv), lx, odma.get(i - 2)])
                    vprev[i] = v1
                    self.banks[i].free = (V, v1)
                    odma[i] = p.dma("sync", OUTs[i % 2],
                                    self.outs[oc * 128:(oc + 1) * 128, cs],
                                    ob[i % 2][:, :], wait=[(V, v1)])


def _hadamard(n):
    h = np.array([[1.0]], dtype=np.float64)
    while h.shape[0] < n:
        h = np.block([[h, h], [h, -h]])
    return np.ascontiguousarray(h.astype(np.float32))


def _rope_tables(pos):
    inv_freq = 1.0 / (500000.0 ** (np.arange(0, 64, 2, dtype=np.float64) / 64.0))
    ang = np.outer(pos.astype(np.float64), inv_freq)
    c64, s64 = np.cos(ang).T, np.sin(ang).T
    ch = c64.astype(np.float32)
    sh = s64.astype(np.float32)
    cl = (c64 - ch).astype(np.float32)
    sl = (s64 - sh).astype(np.float32)
    asc = np.ascontiguousarray
    return asc(ch), asc(sh), asc(cl), asc(sl)


def prepare_in_maps(L0_attn_x, L0_attn_w_q_down, L0_attn_q_norm_w, L0_attn_w_q_up,
                    L0_attn_w_kv_down, L0_attn_kv_norm_w, L0_attn_w_kv_up, L0_attn_w_o,
                    L0_ffn_SU_ug, L0_ffn_SU_dp, L0_ffn_Wscale_ug, L0_ffn_Wscale_dp,
                    L0_ffn_W_up, L0_ffn_W_gate, L0_ffn_W_down):
    x = np.asarray(L0_attn_x, dtype=np.float32)

    def f32(a):
        return np.ascontiguousarray(np.asarray(a, dtype=np.float32))

    wqd_b = f32(np.asarray(L0_attn_w_q_down, np.float32)
                .reshape(QC, 128, FC, 128).transpose(2, 3, 0, 1)
                .reshape(FC * 128, QC * 128))
    perm = []
    for h in range(H):
        perm.extend(range(h * 192, h * 192 + 128))
    for g in range(4):
        for hh in range(4):
            hd = g * 4 + hh
            perm.extend((hd * 192 + 128 + 2 * np.arange(32)).tolist())
    for g in range(4):
        for hh in range(4):
            hd = g * 4 + hh
            perm.extend((hd * 192 + 129 + 2 * np.arange(32)).tolist())
    wqu_b = f32(np.asarray(L0_attn_w_q_up, np.float32)[np.asarray(perm)]
                .reshape(24, 128, QC, 128).transpose(2, 3, 0, 1)
                .reshape(QC * 128, 24 * 128))
    kperm = np.concatenate([np.arange(512), 512 + 2 * np.arange(32),
                            513 + 2 * np.arange(32)])
    wkd_b = f32(np.asarray(L0_attn_w_kv_down, np.float32)[kperm].T
                .reshape(FC * 128, 576))
    wku = np.asarray(L0_attn_w_kv_up, np.float32).reshape(H, 2, 128, KC, 128)
    wkuk_b = f32(wku[:, 0].transpose(0, 3, 2, 1).reshape(H * 128, KC * 128))
    wkuv_b = f32(wku[:, 1].transpose(2, 3, 0, 1).reshape(KC * 128, 2048))
    wo_b = f32(np.asarray(L0_attn_w_o, np.float32)
               .reshape(16, 128, 16, 128).transpose(0, 3, 2, 1)
               .reshape(16 * 128, 16 * 128))
    wup_b = f32(np.asarray(L0_ffn_W_up, np.float32)
                .reshape(IC, 128, FC, 128).transpose(0, 3, 2, 1)
                .reshape(IC * 128, FC * 128))
    wgt_b = f32(np.asarray(L0_ffn_W_gate, np.float32)
                .reshape(IC, 128, FC, 128).transpose(0, 3, 2, 1)
                .reshape(IC * 128, FC * 128))
    wdn_b = f32(np.asarray(L0_ffn_W_down, np.float32)
                .reshape(16, 128, IC, 128).transpose(2, 3, 0, 1)
                .reshape(IC * 128, 16 * 128))
    gq_b = f32(np.asarray(L0_attn_q_norm_w, np.float32).reshape(QC, 128).T)
    gkv_b = f32(np.asarray(L0_attn_kv_norm_w, np.float32).reshape(KC, 128).T)
    suug_b = f32(np.asarray(L0_ffn_SU_ug, np.float32).reshape(FC, 128).T)
    sudp_b = f32(np.asarray(L0_ffn_SU_dp, np.float32).reshape(IC, 128).T)
    wsc = np.asarray(L0_ffn_Wscale_ug, np.float32)
    scup_b = f32(wsc[:INTER].reshape(IC, 128).T)
    scgt_b = f32(wsc[INTER:].reshape(IC, 128).T)
    scdp_b = f32(np.asarray(L0_ffn_Wscale_dp, np.float32).reshape(FC, 128).T)

    onesd = np.ones((128, 128), np.float32)
    constsd = np.tile(np.array([EPS_LN, EPS_RMS, 0.0, 1.0], np.float32), (128, 1))
    h128d = _hadamard(128)
    kk = np.arange(128)[:, None]
    qq = np.arange(512)[None, :]
    md = np.concatenate(
        [np.where(ks * 128 + kk > qq, np.float32(NEGM), np.float32(0.0))
         for ks in range(4)], axis=1).astype(np.float32)

    in_maps = []
    core_meta = []
    for core in range(8):
        sq, j = core // 4, core % 4
        A, B = j, 7 - j
        xs = x[sq].T
        remaining = [g for g in range(8) if g != j]
        sperm = remaining[0:3] + [j] + remaining[3:7] + [7 - j]
        xseq_b = f32(np.concatenate([xs[:, g * 512:(g + 1) * 512] for g in sperm],
                                    axis=1))
        xown_b = f32(np.concatenate([xs[:, A * 512:(A + 1) * 512],
                                     xs[:, B * 512:(B + 1) * 512]], axis=1))
        pos_own = np.concatenate([np.arange(A * 512, (A + 1) * 512),
                                  np.arange(B * 512, (B + 1) * 512)])
        cq_t, sq_t, cq_l, sq_l = _rope_tables(pos_own)
        cosq_b = f32(np.tile(cq_t, (4, 1)))
        sinq_b = f32(np.tile(sq_t, (4, 1)))
        cosql_b = f32(np.tile(cq_l, (4, 1)))
        sinql_b = f32(np.tile(sq_l, (4, 1)))
        pos_seq = np.concatenate([np.arange(g * 512, (g + 1) * 512) for g in sperm])
        ck_t, sk_t, ck_l, sk_l = _rope_tables(pos_seq)
        deadt = np.zeros((128, 13), np.float32)
        for sl in range(4):
            g = sperm[sl]
            if not ((g < j) or (sl == 3)):
                deadt[:, sl] = NEGM
        for sl in range(9):
            g = sperm[sl]
            if not ((sl == 8) or (g < B and sl < 8)):
                deadt[:, 4 + sl] = NEGM
        in_maps.append({
            "xseq": xseq_b, "xown": xown_b, "cosq": cosq_b, "sinq": sinq_b,
            "cosk": f32(ck_t), "sink": f32(sk_t), "coskl": f32(ck_l),
            "sinkl": f32(sk_l), "cosql": cosql_b, "sinql": sinql_b,
            "deadt": deadt, "mdiagd": md,
            "onesd": onesd, "h128d": h128d, "gqd": gq_b, "gkvd": gkv_b,
            "suugd": suug_b, "sudpd": sudp_b, "scupd": scup_b, "scgtd": scgt_b,
            "scdpd": scdp_b, "constsd": constsd, "wqd": wqd_b, "wqu": wqu_b,
            "wkd": wkd_b, "wkuk": wkuk_b, "wkuv": wkuv_b, "wod": wo_b,
            "wup": wup_b, "wgate": wgt_b, "wdown": wdn_b,
        })
        core_meta.append((sq, A, B))
    return in_maps, core_meta


def kernel(**inputs):
    in_maps, core_meta = prepare_in_maps(**inputs)
    nc = Builder(bass.Bass("TRN2", target_bir_lowering=False,
                       detect_race_conditions=False)).build()
    res = run_bass_kernel_spmd(nc, in_maps, core_ids=list(range(8)))

    out = np.empty((2, 4096, D), np.float32)
    for core in range(8):
        sq, A, B = core_meta[core]
        o = np.asarray(res.results[core]["outs"])
        out[sq, A * 512:(A + 1) * 512] = o[:, 0:512].T
        out[sq, B * 512:(B + 1) * 512] = o[:, 512:1024].T
    return out


# version 10 (final)
# speedup vs baseline: 1.0158x; 1.0113x over previous
import numpy as np
from contextlib import ExitStack

import concourse.bass as bass
import concourse.mybir as mybir
from concourse.bass_utils import run_bass_kernel_spmd

F32 = mybir.dt.float32
AF = mybir.ActivationFunctionType
ALU = mybir.AluOpType
AXX = mybir.AxisListType.X

ENGS = ("sync", "tensor", "vector", "scalar", "gpsimd")

MM_DT = F32

NEGM = -1.0e30

D, FC = 2048, 16
QR, QC = 1536, 12
CKV, KC = 512, 4
H = 16
INTER, IC = 8192, 64
NM = 9
TS = NM * 512
TQ = 1024
EPS_LN, EPS_RMS = 1e-5, 1e-6
SCALE = float(1.0 / np.sqrt(192.0))
FS1 = float(1.0 / (64.0 * np.sqrt(2048.0)))
FS2 = float(1.0 / np.sqrt(8192.0))


class Prog:
    def __init__(self):
        self.q = {e: [] for e in ENGS}
        self.count = {}
        self.last_wait = {e: {} for e in ENGS}

    def inc(self, sem, n=1):
        c = self.count.get(sem.name, 0) + n
        self.count[sem.name] = c
        return c

    def cur(self, sem):
        return self.count.get(sem.name, 0)

    def wait(self, eng, sem, val):
        if val is None or val <= 0:
            return
        if self.last_wait[eng].get(sem.name, 0) >= val:
            return
        self.last_wait[eng][sem.name] = val
        self.q[eng].append(lambda e, s=sem, v=val: e.wait_ge(s, v))

    def waitt(self, eng, tok):
        if tok is not None:
            self.wait(eng, tok[0], tok[1])

    def op(self, eng, sem, fn, wait=()):
        for w in wait:
            if w is None:
                continue
            self.wait(eng, w[0], w[1])
        if sem is None:
            self.q[eng].append(fn)
            return None
        v = self.inc(sem)
        self.q[eng].append(lambda e, f=fn, s=sem: f(e).then_inc(s, 1))
        return v

    def dma(self, eng, sem, out, in_, wait=()):
        for w in wait:
            if w is None:
                continue
            self.wait(eng, w[0], w[1])
        v = self.inc(sem, 16)
        self.q[eng].append(
            lambda e, o=out, i=in_, s=sem: e.dma_start(out=o, in_=i).then_inc(s, 16))
        return (sem, v)

    def run(self, eng_name, engine):
        for fn in self.q[eng_name]:
            fn(engine)


class Bank:
    __slots__ = ("t", "free")

    def __init__(self, t):
        self.t = t
        self.free = None


class Builder:
    def __init__(self, nc):
        self.nc = nc
        self.p = Prog()

    def _mm(self, ap):
        return ap if MM_DT is F32 else ap.bitcast(MM_DT)

    def acq(self, bank):
        if bank.free is not None:
            self.p.wait("tensor", *bank.free)

    def barrier(self):
        snap = [(sm, self.p.cur(sm)) for sm in self.all_sems if self.p.cur(sm) > 0]
        for e in ENGS:
            for sm, vv in snap:
                self.p.wait(e, sm, vv)

    def newton_rsqrt(self, r_out, var_ap, eps, t1, t2, wait=()):
        p, S, V = self.p, self.S, self.V
        sv = p.op("scalar", S, lambda e: e.add(t1, var_ap, eps), wait=wait)
        vv = p.op("vector", V, lambda e: e.reciprocal(t2, t1), wait=[(S, sv)])
        sv = p.op("scalar", S, lambda e: e.sqrt(r_out, t2), wait=[(V, vv), (S, sv)])
        last = None
        for it in range(2):
            s2 = p.op("scalar", S, lambda e: e.square(t2, r_out),
                      wait=[(S, sv)] if it == 0 else [(V, last)])
            p.op("vector", V, lambda e: e.tensor_mul(t2, t1, t2), wait=[(S, s2)])
            p.op("vector", V, lambda e: e.tensor_scalar(t2, t2, -0.5, 1.5,
                                                        ALU.mult, ALU.add))
            last = p.op("vector", V, lambda e: e.tensor_mul(r_out, r_out, t2))
        return last

    def ln_stats(self, x_sb, nfc, width, mean_out, rstd_out, sq_scratch, t1, t2,
                 denom, eps, lv):
        p = self.p
        S, T = self.S, self.T
        b_sum, b_sq = self.banks[0], self.banks[1]
        self.acq(b_sum)
        self.acq(b_sq)
        p.waitt("tensor", lv)
        for f in range(nfc):
            p.op("tensor", None, lambda e, f=f: e.matmul(
                b_sum.t[:, :width], self._mm(self.ones[:, :]),
                self._mm(x_sb[:, f * width:(f + 1) * width]),
                start=(f == 0), stop=(f == nfc - 1)))
        p.waitt("scalar", lv)
        sv = None
        for f in range(nfc):
            sv = p.op("scalar", S, lambda e, f=f: e.square(
                sq_scratch[:, f * width:(f + 1) * width],
                x_sb[:, f * width:(f + 1) * width]))
        p.wait("tensor", S, sv)
        tv = None
        for f in range(nfc):
            tv = p.op("tensor", T if f == nfc - 1 else None, lambda e, f=f: e.matmul(
                b_sq.t[:, :width], self._mm(self.ones[:, :]),
                self._mm(sq_scratch[:, f * width:(f + 1) * width]),
                start=(f == 0), stop=(f == nfc - 1)))
        sv = p.op("scalar", S, lambda e: e.mul(mean_out, b_sum.t[:, :width], 1.0 / denom),
                  wait=[(T, tv)])
        sv2 = p.op("scalar", S, lambda e: e.mul(t1, b_sq.t[:, :width], 1.0 / denom))
        b_sum.free = (S, sv)
        b_sq.free = (S, sv2)
        sv3 = p.op("scalar", S, lambda e: e.square(t2, mean_out), wait=[(S, sv)])
        vsub = p.op("vector", self.V, lambda e: e.tensor_sub(t1, t1, t2),
                    wait=[(S, sv3)])
        return self.newton_rsqrt(rstd_out, t1, eps, t2, sq_scratch[:, 0:width],
                                 wait=[(V := self.V, vsub)])

    def rms_stats(self, get_chunk, nfc, width, rstd_out, t1, t2, t3, denom, eps,
                  bank_idx=0):
        p = self.p
        S, T = self.S, self.T
        b_r = self.banks[bank_idx]
        self.acq(b_r)
        tv = None
        for f in range(nfc):
            sqv = p.op("scalar", S, lambda e, f=f: e.square(t3, get_chunk(f)),
                       wait=[(T, tv)] if tv else [])
            p.wait("tensor", S, sqv)
            tv = p.op("tensor", T, lambda e, f=f: e.matmul(
                b_r.t[:, :width], self._mm(self.ones[:, :]), self._mm(t3),
                start=(f == 0), stop=(f == nfc - 1)))
        sv = p.op("scalar", S, lambda e: e.mul(t1, b_r.t[:, :width], 1.0 / denom),
                  wait=[(T, tv)])
        b_r.free = (S, sv)
        return self.newton_rsqrt(rstd_out, t1, eps, t2, t3, wait=[(S, sv)])

    def build(self):
        nc, p = self.nc, self.p

        def din(name, shape):
            return nc.dram_tensor(name, shape, F32, kind="ExternalInput").ap()

        self.xseq = din("xseq", [D, TS])
        self.xown = din("xown", [D, TQ])
        self.cosq = din("cosq", [128, TQ])
        self.sinq = din("sinq", [128, TQ])
        self.cosql = din("cosql", [128, TQ])
        self.sinql = din("sinql", [128, TQ])
        self.cosk = din("cosk", [32, TS])
        self.sink = din("sink", [32, TS])
        self.coskl = din("coskl", [32, TS])
        self.sinkl = din("sinkl", [32, TS])
        self.deadt = din("deadt", [128, 13])
        self.mdiagd = din("mdiagd", [128, 4 * 512])
        self.onesd = din("onesd", [128, 128])
        self.h128d = din("h128d", [128, 128])
        self.gqd = din("gqd", [128, QC])
        self.gkvd = din("gkvd", [128, KC])
        self.suugd = din("suugd", [128, FC])
        self.sudpd = din("sudpd", [128, IC])
        self.scupd = din("scupd", [128, IC])
        self.scgtd = din("scgtd", [128, IC])
        self.scdpd = din("scdpd", [128, FC])
        self.constsd = din("constsd", [128, 4])
        self.wqd = din("wqd", [FC * 128, QC * 128])
        self.wqu = din("wqu", [QC * 128, 24 * 128])
        self.wkd = din("wkd", [FC * 128, 576])
        self.wkuk = din("wkuk", [H * 128, KC * 128])
        self.wkuv = din("wkuv", [KC * 128, 2048])
        self.wod = din("wod", [16 * 128, 16 * 128])
        self.wup = din("wup", [IC * 128, FC * 128])
        self.wgate = din("wgate", [IC * 128, FC * 128])
        self.wdown = din("wdown", [IC * 128, 16 * 128])
        self.outs = nc.dram_tensor("outs", [D, TQ], F32, kind="ExternalOutput").ap()

        def scr(name, shape):
            return nc.dram_tensor(name, shape, F32).ap()

        self.ckvn_s = scr("ckvn_s", [KC * 128, TS])
        self.krope_s = scr("krope_s", [64, TS])
        self.knope_s = scr("knope_s", [H * 128, TS])
        self.vs_s = scr("vs_s", [TS, 2048])
        self.qs_s = scr("qs_s", [H * 128, TQ])
        self.qrope_s = scr("qrope_s", [H * 64, TQ])
        self.x2_s = scr("x2_s", [D, TQ])
        self.z_s = scr("z_s", [IC * 128, TQ])

        with ExitStack() as ctx:
            E = ctx.enter_context
            self.T = E(nc.semaphore("Tm"))
            self.V = E(nc.semaphore("Vm"))
            self.S = E(nc.semaphore("Sm"))
            self.pool = [E(nc.semaphore(f"dq{i}")) for i in range(56)]
            self._si = 0
            self.all_sems = [self.T, self.V, self.S] + self.pool

            def sems(n):
                r = self.pool[self._si:self._si + n]
                assert len(r) == n, "sem pool exhausted"
                self._si += n
                return r if n > 1 else r[0]

            self.sems = sems
            self.banks = [Bank(E(nc.psum_tensor(f"pb{i}", [128, 512], F32)))
                          for i in range(8)]
            self.ones = E(nc.sbuf_tensor("ones_sb", [128, 128], F32))
            self.h128 = E(nc.sbuf_tensor("h128_sb", [128, 128], F32))
            self.mdiag = E(nc.sbuf_tensor("mdiag_sb", [128, 4 * 512], F32))
            self.dead = E(nc.sbuf_tensor("dead_sb", [128, 13], F32))
            self.gqs = E(nc.sbuf_tensor("gqs_sb", [128, QC], F32))
            self.gkv = E(nc.sbuf_tensor("gkv_sb", [128, KC], F32))
            self.suug = E(nc.sbuf_tensor("suug_sb", [128, FC], F32))
            self.sudp = E(nc.sbuf_tensor("sudp_sb", [128, IC], F32))
            self.scup = E(nc.sbuf_tensor("scup_sb", [128, IC], F32))
            self.scgt = E(nc.sbuf_tensor("scgt_sb", [128, IC], F32))
            self.scdp = E(nc.sbuf_tensor("scdp_sb", [128, FC], F32))
            self.csts = E(nc.sbuf_tensor("csts_sb", [128, 4], F32))
            nc.const_aps.register(F32, EPS_LN, self.csts[:, 0:1])
            nc.const_aps.register(F32, EPS_RMS, self.csts[:, 1:2])
            nc.const_aps.register(F32, 0.0, self.csts[:, 2:3])
            nc.const_aps.register(F32, 1.0, self.csts[:, 3:4])
            self.qn2m = E(nc.sbuf_tensor("qn2m", [128, 1], F32))
            self.qr2m = E(nc.sbuf_tensor("qr2m", [128, 1], F32))
            self.kn2m = E(nc.sbuf_tensor("kn2m", [128, 1], F32))
            self.kr2m = E(nc.sbuf_tensor("kr2m", [128, 1], F32))
            self.negb = E(nc.sbuf_tensor("negb", [128, 1], F32))
            self.biasc = E(nc.sbuf_tensor("biasc", [128, 13], F32))

            self.phase_init()
            with ExitStack() as ph:
                self.phase_kv_down(ph)
            self.barrier()
            with ExitStack() as ph:
                self.phase_kv_up(ph)
            self.barrier()
            with ExitStack() as ph:
                self.phase_q(ph)
            self.barrier()
            self.phase_bias()
            with ExitStack() as ph:
                self.phase_attn(ph)
                self.barrier()
                self.phase_oproj(ph)
            self.barrier()
            with ExitStack() as ph:
                self.phase_ffn1(ph)
            self.barrier()
            with ExitStack() as ph:
                self.phase_ffn2(ph)
            for sm in self.all_sems:
                if p.cur(sm) > 0:
                    p.wait("sync", sm, p.cur(sm))

            with nc.Block() as block:
                @block.sync
                def _(e):
                    p.run("sync", e)

                @block.tensor
                def _(e):
                    p.run("tensor", e)

                @block.vector
                def _(e):
                    p.run("vector", e)

                @block.scalar
                def _(e):
                    p.run("scalar", e)

                @block.gpsimd
                def _(e):
                    p.run("gpsimd", e)
        return nc

    def phase_init(self):
        p, S = self.p, self.S
        LD = self.sems(1)
        self.LD = LD
        iv = None
        for dst, src in [(self.csts, self.constsd), (self.ones, self.onesd),
                         (self.h128, self.h128d), (self.mdiag, self.mdiagd),
                         (self.dead, self.deadt), (self.gqs, self.gqd),
                         (self.gkv, self.gkvd), (self.suug, self.suugd),
                         (self.sudp, self.sudpd), (self.scup, self.scupd),
                         (self.scgt, self.scgtd), (self.scdp, self.scdpd)]:
            iv = p.dma("sync", LD, dst[:, :], src[:, :])
        self._init_done = False

    def init_compute(self, ld_tok):
        p, S = self.p, self.S
        p.waitt("scalar", ld_tok)
        p.op("scalar", S, lambda e: e.mul(self.gqs[:, :], self.gqs[:, :], SCALE))
        p.op("scalar", S, lambda e: e.mul(self.scup[:, :], self.scup[:, :], 64.0))
        p.op("scalar", S, lambda e: e.mul(self.scgt[:, :], self.scgt[:, :], 64.0))
        p.op("scalar", S, lambda e: e.mul(self.scdp[:, :], self.scdp[:, :], FS2))
        p.op("vector", self.V, lambda e: e.memset(self.qn2m[:, :], 0.0))
        p.op("vector", self.V, lambda e: e.memset(self.qr2m[:, :], 0.0))
        p.op("vector", self.V, lambda e: e.memset(self.kn2m[:, :], 0.0))
        p.op("vector", self.V, lambda e: e.memset(self.kr2m[:, :], 0.0))

    def phase_kv_down(self, ph):
        nc, p = self.nc, self.p
        E = ph.enter_context
        T, V, S, LD = self.T, self.V, self.S, self.LD
        XT = self.sems(1)
        CK = self.sems(1)
        CKS = self.sems(1)
        KRS = self.sems(1)
        wkd_sb = E(nc.sbuf_tensor("wkd_sb", [128, FC * 576], F32))
        xt = E(nc.sbuf_tensor("xt", [128, FC * 512], F32))
        xn = E(nc.sbuf_tensor("xn", [128, FC * 512], F32))
        ckvn = E(nc.sbuf_tensor("ckvn", [128, KC * 512], F32))
        stm = E(nc.sbuf_tensor("stm", [128, 512], F32))
        strr = E(nc.sbuf_tensor("strr", [128, 512], F32))
        st1 = E(nc.sbuf_tensor("st1", [128, 512], F32))
        st2 = E(nc.sbuf_tensor("st2", [128, 512], F32))
        nsc = E(nc.sbuf_tensor("nsc", [128, 512], F32))
        coskt = E(nc.sbuf_tensor("coskt", [32, 512], F32))
        sinkt = E(nc.sbuf_tensor("sinkt", [32, 512], F32))
        ckl = E(nc.sbuf_tensor("ckl", [32, 512], F32))
        skl = E(nc.sbuf_tensor("skl", [32, 512], F32))
        kt1 = E(nc.sbuf_tensor("kt1", [32, 512], F32))
        kt2 = E(nc.sbuf_tensor("kt2", [32, 512], F32))
        kE = E(nc.sbuf_tensor("kE", [32, 512], F32))
        kO = E(nc.sbuf_tensor("kO", [32, 512], F32))
        ksq = E(nc.sbuf_tensor("ksq", [32, 512], F32))

        wkd_tok = None
        for fc in range(FC):
            wkd_tok = p.dma("sync", LD, wkd_sb[:, fc * 576:(fc + 1) * 576],
                            self.wkd[fc * 128:(fc + 1) * 128, :])
        self.init_compute(wkd_tok)
        p.waitt("tensor", wkd_tok)
        p.waitt("vector", wkd_tok)

        xt_free = None
        xn_free = None
        ck_free = None
        cks_tok = None
        krs_tok = None
        for m in range(NM):
            cols = slice(m * 512, (m + 1) * 512)
            lv = p.dma("sync", XT, xt[:, :].rearrange("p (a t) -> p a t", a=FC),
                       self.xseq[:, cols].rearrange("(a p) t -> p a t", p=128),
                       wait=[xt_free])
            p.dma("sync", CK, coskt[:, :], self.cosk[:, cols], wait=[ck_free])
            p.dma("sync", CK, sinkt[:, :], self.sink[:, cols])
            p.dma("sync", CK, ckl[:, :], self.coskl[:, cols])
            ck0 = p.dma("sync", CK, skl[:, :], self.sinkl[:, cols])
            p.waitt("vector", ck0)
            p.op("vector", V, lambda e: e.tensor_add(coskt[:, :], coskt[:, :],
                                                     ckl[:, :]))
            ck = p.op("vector", V, lambda e: e.tensor_add(sinkt[:, :], sinkt[:, :],
                                                          skl[:, :]))
            ck = (V, ck)
            if xn_free is not None:
                p.wait("scalar", *xn_free)
            rv = self.ln_stats(xt, FC, 512, stm[:, :], strr[:, :], xn,
                               st1[:, :], st2[:, :], D, EPS_LN, lv)
            vv = None
            for f in range(FC):
                p.op("vector", V, lambda e, f=f: e.tensor_sub(
                    xn[:, f * 512:(f + 1) * 512], xt[:, f * 512:(f + 1) * 512],
                    stm[:, :]), wait=[(V, rv)])
                vv = p.op("vector", V, lambda e, f=f: e.tensor_mul(
                    xn[:, f * 512:(f + 1) * 512], xn[:, f * 512:(f + 1) * 512],
                    strr[:, :]))
            xt_free = (V, vv)
            p.wait("tensor", V, vv)
            p.waitt("tensor", wkd_tok)
            p.waitt("tensor", cks_tok)
            for oc in range(KC):
                bk = self.banks[2 + oc]
                self.acq(bk)
                for f in range(FC):
                    p.op("tensor", None, lambda e, f=f, o=oc, b=bk: e.matmul(
                        b.t[:, :],
                        self._mm(wkd_sb[:, f * 576 + o * 128:f * 576 + (o + 1) * 128]),
                        self._mm(xn[:, f * 512:(f + 1) * 512]),
                        start=(f == 0), stop=(f == FC - 1)))
            bE, bO = self.banks[6], self.banks[7]
            self.acq(bE)
            self.acq(bO)
            for f in range(FC):
                p.op("tensor", None, lambda e, f=f: e.matmul(
                    bE.t[0:32, :], self._mm(wkd_sb[:, f * 576 + 512:f * 576 + 544]),
                    self._mm(xn[:, f * 512:(f + 1) * 512]),
                    start=(f == 0), stop=(f == FC - 1)))
            tv = None
            for f in range(FC):
                tv = p.op("tensor", T if f == FC - 1 else None, lambda e, f=f: e.matmul(
                    bO.t[0:32, :], self._mm(wkd_sb[:, f * 576 + 544:f * 576 + 576]),
                    self._mm(xn[:, f * 512:(f + 1) * 512]),
                    start=(f == 0), stop=(f == FC - 1)))
            xn_free = (T, tv)
            sv = None
            for oc in range(KC):
                sv = p.op("scalar", S, lambda e, o=oc: e.copy(
                    ckvn[:, o * 512:(o + 1) * 512], self.banks[2 + o].t[:, :]),
                    wait=[(T, tv)])
            rv = self.rms_stats(
                lambda f: ckvn[:, f * 512:(f + 1) * 512], KC, 512, strr[:, :],
                st1[:, :], st2[:, :], nsc[:, :], CKV, EPS_RMS, bank_idx=0)
            for oc in range(KC):
                cv = p.op("vector", V, lambda e, o=oc: e.scalar_tensor_tensor(
                    ckvn[:, o * 512:(o + 1) * 512], ckvn[:, o * 512:(o + 1) * 512],
                    self.gkv[:, o:o + 1], strr[:, :], ALU.mult, ALU.mult),
                    wait=[(V, rv)])
                self.banks[2 + oc].free = (V, cv)
                cks_tok = p.dma("gpsimd", CKS,
                                self.ckvn_s[oc * 128:(oc + 1) * 128, cols],
                                ckvn[:, oc * 512:(oc + 1) * 512], wait=[(V, cv)])
            p.wait("vector", T, tv)
            p.waitt("vector", ck)
            p.waitt("vector", krs_tok)
            p.op("vector", V, lambda e: e.tensor_mul(kt1[:, :], bE.t[0:32, :], coskt[:, :]))
            p.op("vector", V, lambda e: e.tensor_mul(kt2[:, :], bO.t[0:32, :], sinkt[:, :]))
            v3 = p.op("vector", V, lambda e: e.tensor_sub(kE[:, :], kt1[:, :], kt2[:, :]))
            v4 = p.op("vector", V, lambda e: e.tensor_mul(kt1[:, :], bO.t[0:32, :], coskt[:, :]))
            v5 = p.op("vector", V, lambda e: e.tensor_mul(kt2[:, :], bE.t[0:32, :], sinkt[:, :]))
            v6 = p.op("vector", V, lambda e: e.tensor_add(kO[:, :], kt1[:, :], kt2[:, :]))
            bE.free = (V, v5)
            bO.free = (V, v4)
            ck_free = (V, v5)
            p.dma("gpsimd", KRS, self.krope_s[0:32, cols], kE[:, :], wait=[(V, v3)])
            krs_tok = p.dma("gpsimd", KRS, self.krope_s[32:64, cols], kO[:, :],
                            wait=[(V, v6)])
            s1 = p.op("scalar", S, lambda e: e.square(ksq[:, :], kE[:, :]),
                      wait=[(V, v3)])
            b2 = self.banks[1]
            self.acq(b2)
            p.wait("tensor", S, s1)
            tmid = p.op("tensor", T, lambda e: e.matmul(
                b2.t[:, :], self._mm(self.ones[0:32, :]), self._mm(ksq[:, :]),
                start=True, stop=False))
            s1 = p.op("scalar", S, lambda e: e.square(ksq[:, :], kO[:, :]),
                      wait=[(V, v6), (T, tmid)])
            p.wait("tensor", S, s1)
            tv = p.op("tensor", T, lambda e: e.matmul(
                b2.t[:, :], self._mm(self.ones[0:32, :]), self._mm(ksq[:, :]),
                start=False, stop=True))
            vv = p.op("vector", V, lambda e: e.reduce_max(nsc[:, 0:1], b2.t[:, :],
                                                          axis=AXX), wait=[(T, tv)])
            b2.free = (V, vv)
            p.op("vector", V, lambda e: e.tensor_max(self.kr2m[:, :], self.kr2m[:, :],
                                                     nsc[:, 0:1]))

    def phase_kv_up(self, ph):
        nc, p = self.nc, self.p
        E = ph.enter_context
        T, V, S, LD = self.T, self.V, self.S, self.LD
        WKs = self.sems(2)
        KSs = self.sems(2)
        VSs = self.sems(2)
        ckvr = E(nc.sbuf_tensor("ckvr", [128, KC * TS], F32))
        wkuv_sb = E(nc.sbuf_tensor("wkuv_sb", [128, KC * 2048], F32))
        wkb = [E(nc.sbuf_tensor(f"wkb{i}", [128, 512], F32)) for i in range(2)]
        ktile = [E(nc.sbuf_tensor(f"ktile{i}", [128, 512], F32)) for i in range(2)]
        ksq2 = E(nc.sbuf_tensor("ksq2", [128, 512], F32))
        vev = [E(nc.sbuf_tensor(f"vev{i}", [128, 2048], F32)) for i in range(2)]
        red = E(nc.sbuf_tensor("red", [128, 1], F32))

        loads = None
        for fc in range(KC):
            p.dma("sync", LD, ckvr[:, fc * TS:(fc + 1) * TS],
                  self.ckvn_s[fc * 128:(fc + 1) * 128, :])
        for fc in range(KC):
            loads = p.dma("sync", LD, wkuv_sb[:, fc * 2048:(fc + 1) * 2048],
                          self.wkuv[fc * 128:(fc + 1) * 128, :])

        wk_use = {}
        kspill = {}
        knorm_tv = None
        for h in range(H):
            wb = wkb[h % 2]
            wv = p.dma("sync", WKs[h % 2], wb[:, :],
                       self.wkuk[h * 128:(h + 1) * 128, :],
                       wait=[(T, wk_use.get(h - 2))] if h >= 2 else [])
            for m in range(NM):
                idx = h * NM + m
                cols = slice(m * 512, (m + 1) * 512)
                bk = self.banks[m % 2]
                self.acq(bk)
                p.waitt("tensor", wv)
                p.waitt("tensor", loads)
                tv = None
                for f in range(KC):
                    tv = p.op("tensor", T if f == KC - 1 else None,
                              lambda e, f=f, b=bk, w=wb, m=m: e.matmul(
                        b.t[:, :], self._mm(w[:, f * 128:(f + 1) * 128]),
                        self._mm(ckvr[:, f * TS + m * 512:f * TS + (m + 1) * 512]),
                        start=(f == 0), stop=(f == KC - 1)))
                wk_use[h] = tv
                kt = ktile[idx % 2]
                sv = p.op("scalar", S, lambda e, b=bk, k=kt: e.copy(k[:, :], b.t[:, :]),
                          wait=[(T, tv), kspill.get(idx - 2)])
                bk.free = (S, sv)
                kspill[idx] = p.dma("gpsimd", KSs[idx % 2],
                                    self.knope_s[h * 128:(h + 1) * 128, cols],
                                    kt[:, :], wait=[(S, sv)])
                s2 = p.op("scalar", S, lambda e, k=kt: e.square(ksq2[:, :], k[:, :]),
                          wait=[(T, knorm_tv)] if knorm_tv else [])
                bn = self.banks[2]
                self.acq(bn)
                p.wait("tensor", S, s2)
                knorm_tv = p.op("tensor", T, lambda e, b=bn: e.matmul(
                    b.t[:, :], self._mm(self.ones[:, :]), self._mm(ksq2[:, :]),
                    start=True, stop=True))
                vv = p.op("vector", V, lambda e, b=bn: e.reduce_max(
                    red[:, :], b.t[:, :], axis=AXX), wait=[(T, knorm_tv)])
                bn.free = (V, vv)
                p.op("vector", V, lambda e: e.tensor_max(self.kn2m[:, :],
                                                         self.kn2m[:, :], red[:, :]))

        vspill = {}
        for m in range(NM):
            for ts in range(4):
                idx = m * 4 + ts
                tok0 = m * 512 + ts * 128
                vb = vev[idx % 2]
                evs = None
                for seg in range(4):
                    bk = self.banks[3 + (idx * 4 + seg) % 2]
                    self.acq(bk)
                    p.waitt("tensor", loads)
                    tv = None
                    for f in range(KC):
                        tv = p.op("tensor", T if f == KC - 1 else None,
                                  lambda e, f=f, b=bk, t0=tok0, sg=seg: e.matmul(
                            b.t[:, :], self._mm(ckvr[:, f * TS + t0:f * TS + t0 + 128]),
                            self._mm(wkuv_sb[:, f * 2048 + sg * 512:
                                             f * 2048 + (sg + 1) * 512]),
                            start=(f == 0), stop=(f == KC - 1)))
                    w_ev = [(T, tv)]
                    if seg == 0 and idx >= 2:
                        w_ev.append(vspill.get(idx - 2))
                    evs = p.op("scalar", S, lambda e, b=bk, v=vb, sg=seg: e.copy(
                        v[:, sg * 512:(sg + 1) * 512], b.t[:, :]), wait=w_ev)
                    bk.free = (S, evs)
                vspill[idx] = p.dma("gpsimd", VSs[idx % 2],
                                    self.vs_s[tok0:tok0 + 128, :],
                                    vb[:, :], wait=[(S, evs)])

    def phase_q(self, ph):
        nc, p = self.nc, self.p
        E = ph.enter_context
        T, V, S, LD = self.T, self.V, self.S, self.LD
        XO = self.sems(1)
        WQs = self.sems(2)
        WUs = self.sems(2)
        QS1 = self.sems(1)
        QS2 = self.sems(1)
        xo = E(nc.sbuf_tensor("xo", [128, FC * 512], F32))
        xn = E(nc.sbuf_tensor("xnq", [128, FC * 512], F32))
        cq = E(nc.sbuf_tensor("cq", [128, QC * TQ], F32))
        stm = E(nc.sbuf_tensor("stmq", [128, 512], F32))
        strr = E(nc.sbuf_tensor("strq", [128, 512], F32))
        st1 = E(nc.sbuf_tensor("st1q", [128, 512], F32))
        st2 = E(nc.sbuf_tensor("st2q", [128, 512], F32))
        nsc = E(nc.sbuf_tensor("nscq", [128, 512], F32))
        wbuf = [E(nc.sbuf_tensor(f"wqb{i}", [128, 6 * 128], F32)) for i in range(2)]
        cosq_sb = E(nc.sbuf_tensor("cosq_sb", [128, TQ], F32))
        sinq_sb = E(nc.sbuf_tensor("sinq_sb", [128, TQ], F32))
        qE = E(nc.sbuf_tensor("qE", [128, 4 * 512], F32))
        qO = E(nc.sbuf_tensor("qO", [128, 512], F32))
        qt1 = E(nc.sbuf_tensor("qt1", [128, 512], F32))
        qt2 = E(nc.sbuf_tensor("qt2", [128, 512], F32))
        qsqt = E(nc.sbuf_tensor("qsqt", [128, 512], F32))
        qevac = E(nc.sbuf_tensor("qevac", [128, 512], F32))

        cql_sb = E(nc.sbuf_tensor("cql_sb", [128, TQ], F32))
        p.dma("sync", LD, cosq_sb[:, :], self.cosq[:, :])
        p.dma("sync", LD, sinq_sb[:, :], self.sinq[:, :])
        cvl = p.dma("sync", LD, cql_sb[:, :], self.cosql[:, :])
        p.waitt("vector", cvl)
        p.op("vector", V, lambda e: e.tensor_add(cosq_sb[:, :], cosq_sb[:, :],
                                                 cql_sb[:, :]))
        cvl = p.dma("sync", LD, cql_sb[:, :], self.sinql[:, :],
                    wait=[(V, p.cur(V))])
        p.waitt("vector", cvl)
        cv_q0 = p.op("vector", V, lambda e: e.tensor_add(sinq_sb[:, :], sinq_sb[:, :],
                                                         cql_sb[:, :]))
        cv_q = (V, cv_q0)

        xo_free = None
        xn_free = None
        wq_free = {}
        for th in range(2):
            colq = slice(th * 512, (th + 1) * 512)
            lv = p.dma("sync", XO, xo[:, :].rearrange("p (a t) -> p a t", a=FC),
                       self.xown[:, colq].rearrange("(a p) t -> p a t", p=128),
                       wait=[xo_free])
            if xn_free is not None:
                p.wait("scalar", *xn_free)
            rv = self.ln_stats(xo, FC, 512, stm[:, :], strr[:, :], xn,
                               st1[:, :], st2[:, :], D, EPS_LN, lv)
            vv = None
            for f in range(FC):
                p.op("vector", V, lambda e, f=f: e.tensor_sub(
                    xn[:, f * 512:(f + 1) * 512], xo[:, f * 512:(f + 1) * 512],
                    stm[:, :]), wait=[(V, rv)])
                vv = p.op("vector", V, lambda e, f=f: e.tensor_mul(
                    xn[:, f * 512:(f + 1) * 512], xn[:, f * 512:(f + 1) * 512],
                    strr[:, :]))
            xo_free = (V, vv)
            p.wait("tensor", V, vv)
            tv = None
            for og in range(2):
                for f in range(FC):
                    wv = p.dma("sync", WQs[f % 2], wbuf[f % 2][:, :],
                               self.wqd[f * 128:(f + 1) * 128, og * 768:(og + 1) * 768],
                               wait=[(T, wq_free.get(f % 2))]
                               if wq_free.get(f % 2) else [])
                    p.waitt("tensor", wv)
                    for i in range(6):
                        bk = self.banks[i]
                        if f == 0:
                            self.acq(bk)
                        tv = p.op(
                            "tensor", T if i == 5 else None,
                            lambda e, f=f, i=i, b=bk, w=wbuf[f % 2]: e.matmul(
                                b.t[:, :], self._mm(w[:, i * 128:(i + 1) * 128]),
                                self._mm(xn[:, f * 512:(f + 1) * 512]),
                                start=(f == 0), stop=(f == FC - 1)))
                    wq_free[f % 2] = tv
                for i in range(6):
                    oc = og * 6 + i
                    sv = p.op("scalar", S, lambda e, o=oc, b=self.banks[i], t=th:
                              e.copy(cq[:, o * TQ + t * 512:o * TQ + t * 512 + 512],
                                     b.t[:, :]), wait=[(T, tv)])
                    self.banks[i].free = (S, sv)
            xn_free = (T, tv)
            rv = self.rms_stats(
                lambda f, t=th: cq[:, f * TQ + t * 512:f * TQ + t * 512 + 512],
                QC, 512,
                strr[:, :], st1[:, :], st2[:, :], nsc[:, :], QR, EPS_RMS, bank_idx=6)
            for oc in range(QC):
                p.op("vector", V, lambda e, o=oc, t=th: e.scalar_tensor_tensor(
                    cq[:, o * TQ + t * 512:o * TQ + t * 512 + 512],
                    cq[:, o * TQ + t * 512:o * TQ + t * 512 + 512],
                    self.gqs[:, o:o + 1], strr[:, :], ALU.mult, ALU.mult),
                    wait=[(V, rv)])
        cq_ready = p.cur(V)
        p.wait("tensor", V, cq_ready)

        wub = [E(nc.sbuf_tensor(f"wub{i}", [128, 6 * 128], F32)) for i in range(2)]
        wu_free = {}
        qs1_tok = None
        qs2_tok = None
        norm_tv = None
        mix_done = None
        for th in range(2):
            ropeE_done = {}
            for og in range(4):
                tv = None
                for f in range(QC):
                    wv = p.dma("sync", WUs[f % 2], wub[f % 2][:, :],
                               self.wqu[f * 128:(f + 1) * 128, og * 768:(og + 1) * 768],
                               wait=[(T, wu_free.get(f % 2))]
                               if wu_free.get(f % 2) else [])
                    p.waitt("tensor", wv)
                    for i in range(6):
                        bk = self.banks[i]
                        if f == 0:
                            self.acq(bk)
                        tv = p.op(
                            "tensor", T if i == 5 else None,
                            lambda e, f=f, i=i, b=bk, w=wub[f % 2], t=th: e.matmul(
                                b.t[:, :], self._mm(w[:, i * 128:(i + 1) * 128]),
                                self._mm(cq[:, f * TQ + t * 512:f * TQ + t * 512 + 512]),
                                start=(f == 0), stop=(f == QC - 1)))
                    wu_free[f % 2] = tv
                for i in range(6):
                    oc = og * 6 + i
                    bk = self.banks[i]
                    if oc < 16:
                        sv = p.op("scalar", S, lambda e, b=bk: e.copy(
                            qevac[:, :], b.t[:, :]), wait=[(T, tv), qs1_tok])
                        qs1_tok = p.dma("gpsimd", QS1,
                                        self.qs_s[oc * 128:(oc + 1) * 128,
                                                  th * 512:(th + 1) * 512],
                                        qevac[:, :], wait=[(S, sv)])
                        bk.free = (S, sv)
                        s2 = p.op("scalar", S, lambda e: e.square(qsqt[:, :], qevac[:, :]),
                                  wait=([(T, norm_tv)] if norm_tv else []) +
                                       ([(V, mix_done)] if mix_done else []))
                        bn = self.banks[7]
                        self.acq(bn)
                        p.wait("tensor", S, s2)
                        norm_tv = p.op("tensor", T, lambda e, b=bn: e.matmul(
                            b.t[:, :], self._mm(self.ones[:, :]), self._mm(qsqt[:, :]),
                            start=True, stop=True))
                        vv = p.op("vector", V, lambda e, b=bn: e.reduce_max(
                            nsc[:, 0:1], b.t[:, :], axis=AXX), wait=[(T, norm_tv)])
                        bn.free = (V, vv)
                        mix_done = p.op("vector", V, lambda e: e.tensor_max(
                            self.qn2m[:, :], self.qn2m[:, :], nsc[:, 0:1]))
                    elif oc < 20:
                        g = oc - 16
                        sv = p.op("scalar", S, lambda e, b=bk, g=g: e.copy(
                            qE[:, g * 512:(g + 1) * 512], b.t[:, :]), wait=[(T, tv)])
                        bk.free = (S, sv)
                        ropeE_done[g] = sv
                    else:
                        g = oc - 20
                        sv = p.op("scalar", S, lambda e, b=bk: e.copy(qO[:, :], b.t[:, :]),
                                  wait=[(T, tv)])
                        bk.free = (S, sv)
                        cs = slice(th * 512, (th + 1) * 512)
                        v1 = p.op("vector", V, lambda e, g=g, cs=cs: e.tensor_mul(
                            qt1[:, :], qE[:, g * 512:(g + 1) * 512], cosq_sb[:, cs]),
                            wait=[(S, sv), (S, ropeE_done[g]), cv_q, qs2_tok] +
                                 ([(T, norm_tv)] if norm_tv else []))
                        p.op("vector", V, lambda e, cs=cs: e.tensor_mul(
                            qt2[:, :], qO[:, :], sinq_sb[:, cs]))
                        v3 = p.op("vector", V, lambda e: e.tensor_sub(
                            qt1[:, :], qt1[:, :], qt2[:, :]))
                        p.op("vector", V, lambda e, cs=cs: e.tensor_mul(
                            qt2[:, :], qO[:, :], cosq_sb[:, cs]))
                        p.op("vector", V, lambda e, g=g, cs=cs: e.tensor_mul(
                            qsqt[:, :], qE[:, g * 512:(g + 1) * 512], sinq_sb[:, cs]))
                        v6 = p.op("vector", V, lambda e: e.tensor_add(
                            qt2[:, :], qt2[:, :], qsqt[:, :]))
                        for hh in range(4):
                            hd = g * 4 + hh
                            p.dma("gpsimd", QS2,
                                  self.qrope_s[hd * 64:hd * 64 + 32, cs],
                                  qt1[hh * 32:(hh + 1) * 32, :], wait=[(V, v3)])
                            qs2_tok = p.dma(
                                "gpsimd", QS2,
                                self.qrope_s[hd * 64 + 32:(hd + 1) * 64, cs],
                                qt2[hh * 32:(hh + 1) * 32, :], wait=[(V, v6)])
                        s2 = p.op("scalar", S, lambda e: e.square(qsqt[:, :], qt1[:, :]),
                                  wait=[(V, v6)])
                        bn = self.banks[7]
                        self.acq(bn)
                        p.wait("tensor", S, s2)
                        tmid = p.op("tensor", T, lambda e, b=bn: e.matmul(
                            b.t[:, :], self._mm(self.ones[:, :]), self._mm(qsqt[:, :]),
                            start=True, stop=False))
                        s2 = p.op("scalar", S, lambda e: e.square(qsqt[:, :], qt2[:, :]),
                                  wait=[(T, tmid)])
                        p.wait("tensor", S, s2)
                        norm_tv = p.op("tensor", T, lambda e, b=bn: e.matmul(
                            b.t[:, :], self._mm(self.ones[:, :]), self._mm(qsqt[:, :]),
                            start=False, stop=True))
                        vv = p.op("vector", V, lambda e, b=bn: e.reduce_max(
                            nsc[:, 0:1], b.t[:, :], axis=AXX), wait=[(T, norm_tv)])
                        bn.free = (V, vv)
                        mix_done = p.op("vector", V, lambda e: e.tensor_max(
                            self.qr2m[:, :], self.qr2m[:, :], nsc[:, 0:1]))

    def phase_bias(self):
        p, S, V = self.p, self.S, self.V
        v2 = p.op("vector", V, lambda e: e.tensor_scalar(
            self.qr2m[:, :], self.qr2m[:, :], 4.0, None, ALU.mult))
        v2 = p.op("vector", V, lambda e: e.tensor_add(
            self.qr2m[:, :], self.qr2m[:, :], self.qn2m[:, :]))
        s1 = p.op("scalar", S, lambda e: e.sqrt(self.qr2m[:, :], self.qr2m[:, :]),
                  wait=[(V, v2)])
        v3 = p.op("vector", V, lambda e: e.tensor_add(
            self.kr2m[:, :], self.kr2m[:, :], self.kn2m[:, :]))
        s2 = p.op("scalar", S, lambda e: e.sqrt(self.kr2m[:, :], self.kr2m[:, :]),
                  wait=[(V, v3)])
        p.op("vector", V, lambda e: e.tensor_mul(
            self.negb[:, :], self.qr2m[:, :], self.kr2m[:, :]), wait=[(S, s1), (S, s2)])
        p.op("vector", V, lambda e: e.tensor_scalar(
            self.negb[:, :], self.negb[:, :], -1.0, None, ALU.mult))
        self.bias_ready = p.op("vector", V, lambda e: e.tensor_scalar(
            self.biasc[:, :], self.dead[:, :], self.negb[:, 0:1], None, ALU.add))

    def phase_attn(self, ph):
        nc, p = self.nc, self.p
        E = ph.enter_context
        T, V, S = self.T, self.V, self.S
        KVs = self.sems(2)
        QL = self.sems(1)
        attn = E(nc.sbuf_tensor("attn", [128, H * TQ], F32))
        self.attn_sb = attn
        kbuf = [E(nc.sbuf_tensor(f"kbuf{i}", [128, 512], F32)) for i in range(2)]
        krb = [E(nc.sbuf_tensor(f"krb{i}", [64, 512], F32)) for i in range(2)]
        vbuf = [E(nc.sbuf_tensor(f"vbuf{i}", [128, 512], F32)) for i in range(2)]
        qnb = [E(nc.sbuf_tensor(f"qnb{i}", [128, 512], F32)) for i in range(2)]
        qrb = [E(nc.sbuf_tensor(f"qrb{i}", [64, 512], F32)) for i in range(2)]
        pT = [E(nc.sbuf_tensor(f"pT{i}", [128, 512], F32)) for i in range(4)]
        sumsb = E(nc.sbuf_tensor("sumsb", [128, 512], F32))
        invb = E(nc.sbuf_tensor("invb", [128, 512], F32))
        rfn = E(nc.sbuf_tensor("rfn", [128, 512], F32))

        p.wait("scalar", V, self.bias_ready)
        bS = [self.banks[0], self.banks[1], self.banks[6], self.banks[7]]
        bOut = {0: self.banks[2], 1: self.banks[3]}
        bSum = {0: self.banks[4], 1: self.banks[5]}

        kv_use = {}
        q_use = None
        exp_done = {}
        pt_free = {}
        sum_fin = None

        for h in range(H):
            qw = [(T, q_use)] if q_use else []
            ql = None
            for s in range(2):
                p.dma("sync", QL, qnb[s][:, :],
                      self.qs_s[h * 128:(h + 1) * 128, s * 512:(s + 1) * 512],
                      wait=qw)
                ql = p.dma("sync", QL, qrb[s][:, :],
                           self.qrope_s[h * 64:(h + 1) * 64, s * 512:(s + 1) * 512])
            out_open = {0: False, 1: False}
            for sl in range(NM):
                ki = sl % 2
                kvw = [(T, kv_use.get((sl - 2) % NM))] if (h > 0 or sl >= 2) else []
                p.dma("sync", KVs[ki], kbuf[ki][:, :],
                      self.knope_s[h * 128:(h + 1) * 128, sl * 512:(sl + 1) * 512],
                      wait=kvw)
                p.dma("sync", KVs[ki], krb[ki][:, :],
                      self.krope_s[:, sl * 512:(sl + 1) * 512])
                kl = p.dma("sync", KVs[ki],
                           vbuf[ki][:, :].rearrange("p (a v) -> p a v", a=4),
                           self.vs_s[sl * 512:(sl + 1) * 512,
                                     h * 128:(h + 1) * 128].rearrange(
                                         "(a p) v -> p a v", p=128))
                iters = ([0] if sl < 4 else []) + [1]
                for s in iters:
                    bias_col = sl if s == 0 else 4 + sl
                    diag = (s == 0 and sl == 3) or (s == 1 and sl == 8)
                    pv_queue = []

                    def emit_pv(job, ki=ki):
                        (s_, ks_, pt_, s1_, st_, sp_) = job
                        bo, bs_ = bOut[s_], bSum[s_]
                        p.wait("tensor", S, s1_)
                        p.op("tensor", None, lambda e, b=bo, ki=ki, ks=ks_, o=pt_,
                             st=st_, sp=sp_: e.matmul(
                            b.t[:, :], self._mm(vbuf[ki][:, ks * 128:(ks + 1) * 128]),
                            self._mm(o[:, :]), start=st, stop=sp))
                        t2 = p.op("tensor", T, lambda e, b=bs_, o=pt_,
                                  st=st_, sp=sp_: e.matmul(
                            b.t[:, :], self._mm(self.ones[:, :]), self._mm(o[:, :]),
                            start=st, stop=sp))
                        pt_free[pT.index(pt_)] = t2

                    for ks in range(4):
                        sb = bS[ks]
                        self.acq(sb)
                        p.wait("tensor", S, exp_done.get(id(sb)))
                        p.waitt("tensor", kl)
                        p.waitt("tensor", ql)
                        p.op("tensor", None, lambda e, b=sb, ki=ki, ks=ks, s=s: e.matmul(
                            b.t[:, :], self._mm(kbuf[ki][:, ks * 128:(ks + 1) * 128]),
                            self._mm(qnb[s][:, :]), start=True, stop=False))
                        t1 = p.op("tensor", T, lambda e, b=sb, ki=ki, ks=ks, s=s: e.matmul(
                            b.t[:, :], self._mm(krb[ki][:, ks * 128:(ks + 1) * 128]),
                            self._mm(qrb[s][:, :]), start=False, stop=True))
                        wait_exp = [(T, t1)]
                        if diag:
                            v1 = p.op("vector", V, lambda e, b=sb, ks=ks: e.tensor_add(
                                b.t[:, :], b.t[:, :],
                                self.mdiag[:, ks * 512:(ks + 1) * 512]),
                                wait=[(T, t1)])
                            wait_exp = [(V, v1)]
                        pt = pT[ks]
                        p.wait("scalar", T, pt_free.get(ks))
                        s1 = p.op("scalar", S, lambda e, b=sb, o=pt, bc=bias_col:
                                  e.activation(o[:, :], b.t[:, :], AF.Exp,
                                               bias=self.biasc[:, bc:bc + 1],
                                               scale=1.0), wait=wait_exp)
                        exp_done[id(sb)] = s1
                        if not out_open[s]:
                            self.acq(bOut[s])
                            self.acq(bSum[s])
                        stop_now = diag and (ks == 3)
                        pv_queue.append((s, ks, pt, s1, not out_open[s], stop_now))
                        out_open[s] = True
                        if len(pv_queue) >= 2:
                            emit_pv(pv_queue.pop(0))
                    while pv_queue:
                        emit_pv(pv_queue.pop(0))
                kv_use[sl] = p.cur(T)
            q_use = p.cur(T)
            for s in range(2):
                bo, bs_ = bOut[s], bSum[s]
                tv = p.cur(T)
                sv = p.op("scalar", S, lambda e, b=bs_: e.copy(sumsb[:, :], b.t[:, :]),
                          wait=[(T, tv)] + ([(V, sum_fin)] if sum_fin else []))
                bs_.free = (S, sv)
                v1 = p.op("vector", V, lambda e: e.reciprocal(invb[:, :], sumsb[:, :]),
                          wait=[(S, sv)])
                p.op("vector", V, lambda e: e.tensor_mul(rfn[:, :], sumsb[:, :],
                                                         invb[:, :]))
                p.op("vector", V, lambda e: e.tensor_scalar(
                    rfn[:, :], rfn[:, :], -1.0, 2.0, ALU.mult, ALU.add))
                v1 = p.op("vector", V, lambda e: e.tensor_mul(invb[:, :], invb[:, :],
                                                              rfn[:, :]))
                v2 = p.op("vector", V, lambda e, b=bo, h=h, s=s: e.tensor_mul(
                    attn[:, h * TQ + s * 512:h * TQ + (s + 1) * 512], b.t[:, :],
                    invb[:, :]), wait=[(V, v1)])
                bo.free = (V, v2)
                sum_fin = v2

    def phase_oproj(self, ph):
        nc, p = self.nc, self.p
        E = ph.enter_context
        T, V, S = self.T, self.V, self.S
        WOs = self.sems(2)
        XLs = self.sems(2)
        X2Ss = self.sems(2)
        attn = self.attn_sb
        wob = [E(nc.sbuf_tensor(f"wob{i}", [128, 2048], F32)) for i in range(2)]
        xot = [E(nc.sbuf_tensor(f"xot{i}", [128, 512], F32)) for i in range(2)]
        x2t = [E(nc.sbuf_tensor(f"x2t{i}", [128, 512], F32)) for i in range(2)]

        last_mm = {}
        x2_spill = {}
        vadd = {}
        for oc in range(16):
            wv = p.dma("sync", WOs[oc % 2], wob[oc % 2][:, :],
                       self.wod[oc * 128:(oc + 1) * 128, :],
                       wait=[(T, last_mm.get(oc - 2))] if oc >= 2 else [])
            for th in range(2):
                idx = oc * 2 + th
                bk = self.banks[idx % 2]
                self.acq(bk)
                lx = p.dma("sync", XLs[th], xot[th][:, :],
                           self.xown[oc * 128:(oc + 1) * 128, th * 512:(th + 1) * 512],
                           wait=[(V, vadd.get(idx - 2))] if idx >= 2 else [])
                p.waitt("tensor", wv)
                tv = None
                for vc in range(16):
                    tv = p.op("tensor", T if vc == 15 else None,
                              lambda e, b=bk, v=vc, th=th, w=wob[oc % 2]: e.matmul(
                        b.t[:, :], self._mm(w[:, v * 128:(v + 1) * 128]),
                        self._mm(attn[:, v * TQ + th * 512:v * TQ + th * 512 + 512]),
                        start=(v == 0), stop=(v == 15)))
                last_mm[oc] = tv
                v1 = p.op("vector", V, lambda e, b=bk, th=th, x=x2t[idx % 2]:
                          e.tensor_add(x[:, :], b.t[:, :], xot[th][:, :]),
                          wait=[(T, tv), lx, x2_spill.get(idx - 2)])
                vadd[idx] = v1
                bk.free = (V, v1)
                x2_spill[idx] = p.dma(
                    "gpsimd", X2Ss[idx % 2],
                    self.x2_s[oc * 128:(oc + 1) * 128, th * 512:(th + 1) * 512],
                    x2t[idx % 2][:, :], wait=[(V, v1)])

    def phase_ffn1(self, ph):
        nc, p = self.nc, self.p
        E = ph.enter_context
        T, V, S = self.T, self.V, self.S
        X2Ls = self.sems(2)
        WGs = self.sems(2)
        WU5s = self.sems(2)
        ZSPs = self.sems(2)
        x2b = [E(nc.sbuf_tensor(f"x2b{i}", [128, TQ], F32)) for i in range(2)]
        xsq = E(nc.sbuf_tensor("xsqf", [128, TQ], F32))
        m2 = E(nc.sbuf_tensor("m2", [128, TQ], F32))
        r2 = E(nc.sbuf_tensor("r2", [128, TQ], F32))
        f1 = E(nc.sbuf_tensor("f1", [128, TQ], F32))
        f2 = E(nc.sbuf_tensor("f2", [128, TQ], F32))
        y1 = E(nc.sbuf_tensor("y1", [128, 18 * TQ], F32))
        hsu = [E(nc.sbuf_tensor(f"hsu{i}", [128, 128], F32)) for i in range(2)]
        wupb = [E(nc.sbuf_tensor(f"wupb{i}", [128, 2048], F32)) for i in range(2)]
        wgb = [E(nc.sbuf_tensor(f"wgb{i}", [128, 2048], F32)) for i in range(2)]
        gt = [E(nc.sbuf_tensor(f"gt{i}", [128, 512], F32)) for i in range(2)]
        zc = [E(nc.sbuf_tensor(f"zc{i}", [128, TQ], F32)) for i in range(2)]

        bS0, bQ0, bS1, bQ1 = (self.banks[i] for i in range(4))
        for b in (bS0, bQ0, bS1, bQ1):
            self.acq(b)
        sq_done = {}
        tvv = None
        for fc in range(FC):
            lv = p.dma("sync", X2Ls[fc % 2], x2b[fc % 2][:, :],
                       self.x2_s[fc * 128:(fc + 1) * 128, :],
                       wait=[(T, sq_done.get(fc - 2))] if fc >= 2 else [])
            p.waitt("scalar", lv)
            sv = p.op("scalar", S, lambda e, x=x2b[fc % 2]: e.square(xsq[:, :], x[:, :]),
                      wait=[(T, sq_done.get(fc - 1))] if fc >= 1 else [])
            p.waitt("tensor", lv)
            p.wait("tensor", S, sv)
            for th in range(2):
                cs = slice(th * 512, (th + 1) * 512)
                p.op("tensor", None, lambda e, x=x2b[fc % 2], f=fc,
                     b=(bS0 if th == 0 else bS1), cs=cs: e.matmul(
                    b.t[:, :], self._mm(self.ones[:, :]), self._mm(x[:, cs]),
                    start=(f == 0), stop=(f == FC - 1)))
                tvv = p.op("tensor", T if th == 1 else None, lambda e, f=fc,
                           b=(bQ0 if th == 0 else bQ1), cs=cs: e.matmul(
                    b.t[:, :], self._mm(self.ones[:, :]), self._mm(xsq[:, cs]),
                    start=(f == 0), stop=(f == FC - 1)))
            sq_done[fc] = tvv
        tv = sq_done[FC - 1]
        nv = None
        for th in range(2):
            cs = slice(th * 512, (th + 1) * 512)
            bs_, bq = (bS0, bQ0) if th == 0 else (bS1, bQ1)
            sv = p.op("scalar", S, lambda e, b=bs_, cs=cs: e.mul(
                m2[:, cs], b.t[:, :], 1.0 / D), wait=[(T, tv)])
            p.op("scalar", S, lambda e, b=bq, cs=cs: e.mul(
                f1[:, cs], b.t[:, :], 1.0 / D))
            sv2 = p.cur(S)
            bs_.free = (S, sv)
            bq.free = (S, sv2)
            p.op("scalar", S, lambda e, cs=cs: e.square(f2[:, cs], m2[:, cs]))
            sv3 = p.cur(S)
            vsub = p.op("vector", V, lambda e, cs=cs: e.tensor_sub(
                f1[:, cs], f1[:, cs], f2[:, cs]), wait=[(S, sv3)])
            nv = self.newton_rsqrt(r2[:, cs], f1[:, cs], EPS_LN, f2[:, cs],
                                   xsq[:, cs], wait=[(V, vsub)])
        rv = p.op("scalar", S, lambda e: e.mul(r2[:, :], r2[:, :], FS1),
                  wait=[(V, nv)])

        slot = list(range(16))
        mm_done = {}
        for fc in range(FC):
            lv = p.dma("sync", X2Ls[fc % 2], x2b[fc % 2][:, :],
                       self.x2_s[fc * 128:(fc + 1) * 128, :],
                       wait=[(T, mm_done.get(fc - 2, tv))])
            v1 = p.op("vector", V, lambda e, x=x2b[fc % 2]: e.tensor_sub(
                f1[:, :], x[:, :], m2[:, :]),
                wait=[lv, (S, rv)] +
                     ([(T, mm_done.get(fc - 1))] if fc >= 1 else []))
            v2 = p.op("vector", V, lambda e: e.tensor_mul(f1[:, :], f1[:, :], r2[:, :]))
            hvv = p.op("vector", V, lambda e, f=fc, hb=hsu[fc % 2]: e.tensor_scalar_mul(
                hb[:, :], self.h128[:, :], self.suug[:, f:f + 1]),
                wait=[(T, mm_done.get(fc - 2))] if fc >= 2 else [])
            for th in range(2):
                bk = self.banks[4 + (fc * 2 + th) % 2]
                self.acq(bk)
                p.wait("tensor", V, max(v2, hvv))
                tvv = p.op("tensor", T, lambda e, b=bk, hb=hsu[fc % 2], th=th: e.matmul(
                    b.t[:, :], self._mm(hb[:, :]),
                    self._mm(f1[:, th * 512:(th + 1) * 512]), start=True, stop=True))
                sv = p.op("scalar", S, lambda e, b=bk, f=fc, th=th: e.copy(
                    y1[:, f * TQ + th * 512:f * TQ + th * 512 + 512], b.t[:, :]),
                    wait=[(T, tvv)])
                bk.free = (S, sv)
            mm_done[fc] = tvv
        p.wait("vector", S, p.cur(S))
        spare = [16, 17]
        for st in range(4):
            bit = 1 << st
            newslot = list(slot)
            for c in range(FC):
                if c & bit:
                    continue
                u, vtx = c, c | bit
                su_, sv_ = slot[u], slot[vtx]
                du, dv = spare[0], spare[1]
                p.op("vector", V, lambda e, a=su_, b=sv_, d=du: e.tensor_add(
                    y1[:, d * TQ:(d + 1) * TQ], y1[:, a * TQ:(a + 1) * TQ],
                    y1[:, b * TQ:(b + 1) * TQ]))
                p.op("vector", V, lambda e, a=su_, b=sv_, d=dv: e.tensor_sub(
                    y1[:, d * TQ:(d + 1) * TQ], y1[:, a * TQ:(a + 1) * TQ],
                    y1[:, b * TQ:(b + 1) * TQ]))
                newslot[u], newslot[vtx] = du, dv
                spare = [su_, sv_]
            slot = newslot
        bfly_done = p.cur(V)
        p.wait("tensor", V, bfly_done)

        w_use = {}
        z_spill = {}
        v1 = None
        for c in range(IC):
            wgv = p.dma("sync", WGs[c % 2], wgb[c % 2][:, :],
                        self.wgate[c * 128:(c + 1) * 128, :],
                        wait=[(T, w_use.get(c - 2))] if c >= 2 else [])
            wvu = p.dma("sync", WU5s[c % 2], wupb[c % 2][:, :],
                        self.wup[c * 128:(c + 1) * 128, :])
            for th in range(2):
                bg = self.banks[(c * 2 + th) % 4]
                self.acq(bg)
                p.waitt("tensor", wgv)
                p.waitt("tensor", wvu)
                for f in range(FC):
                    p.op("tensor", None, lambda e, b=bg, f=f, th=th, w=wgb[c % 2],
                         sf=slot[f]: e.matmul(
                        b.t[:, :], self._mm(w[:, f * 128:(f + 1) * 128]),
                        self._mm(y1[:, sf * TQ + th * 512:sf * TQ + th * 512 + 512]),
                        start=(f == 0), stop=(f == FC - 1)))
                bu = self.banks[4 + (c * 2 + th) % 4]
                self.acq(bu)
                tvu = None
                for f in range(FC):
                    tvu = p.op("tensor", T if f == FC - 1 else None,
                               lambda e, b=bu, f=f, th=th, w=wupb[c % 2],
                               sf=slot[f]: e.matmul(
                        b.t[:, :], self._mm(w[:, f * 128:(f + 1) * 128]),
                        self._mm(y1[:, sf * TQ + th * 512:sf * TQ + th * 512 + 512]),
                        start=(f == 0), stop=(f == FC - 1)))
                w_use[c] = tvu
                sv = p.op("scalar", S, lambda e, b=bg, c=c, g=gt[th]:
                          e.activation(g[:, :], b.t[:, :], AF.Silu, bias=0.0,
                                       scale=self.scgt[:, c:c + 1]),
                          wait=[(T, tvu)] + ([(V, v1)] if v1 else []))
                bg.free = (S, sv)
                v1 = p.op("vector", V, lambda e, b=bu, th=th, c=c, g=gt[th],
                          z=zc[c % 2]: e.scalar_tensor_tensor(
                    z[:, th * 512:(th + 1) * 512], b.t[:, :], self.scup[:, c:c + 1],
                    g[:, :], ALU.mult, ALU.mult),
                    wait=[(S, sv)] +
                         ([z_spill.get(c - 2)] if (th == 0 and c >= 2) else []))
                bu.free = (V, v1)
            z_spill[c] = p.dma("gpsimd", ZSPs[c % 2],
                               self.z_s[c * 128:(c + 1) * 128, :],
                               zc[c % 2][:, :], wait=[(V, v1)])

    def phase_ffn2(self, ph):
        nc, p = self.nc, self.p
        E = ph.enter_context
        T, V, S = self.T, self.V, self.S
        ZLs = self.sems(4)
        WDs = self.sems(2)
        XT6s = self.sems(2)
        OUTs = self.sems(2)
        zwin = E(nc.sbuf_tensor("zwin", [128, 66 * 512], F32))
        hsu = [E(nc.sbuf_tensor(f"hs2{i}", [128, 128], F32)) for i in range(2)]
        wdb = [E(nc.sbuf_tensor(f"wdb{i}", [128, 6 * 128], F32)) for i in range(2)]
        x2t = [E(nc.sbuf_tensor(f"x2f{i}", [128, 512], F32)) for i in range(2)]
        ob = [E(nc.sbuf_tensor(f"ob{i}", [128, 512], F32)) for i in range(2)]

        og_list = [(0, 6), (6, 6), (12, 4)]
        wd_free = {}
        for th in range(2):
            cs = slice(th * 512, (th + 1) * 512)
            p.wait("sync", T, p.cur(T))
            p.wait("gpsimd", T, p.cur(T))
            slot = list(range(IC))
            hv = {}
            for c in range(IC):
                lv = p.dma("sync", ZLs[c % 4], zwin[:, c * 512:(c + 1) * 512],
                           self.z_s[c * 128:(c + 1) * 128, cs],
                           wait=[(T, hv.get(c - 4))] if hv.get(c - 4) else [])
                hvv = p.op("vector", V, lambda e, c=c, hb=hsu[c % 2]: e.tensor_scalar_mul(
                    hb[:, :], self.h128[:, :], self.sudp[:, c:c + 1]),
                    wait=[(T, hv.get(c - 2))] if hv.get(c - 2) else [])
                bk = self.banks[6 + c % 2]
                self.acq(bk)
                p.waitt("tensor", lv)
                p.wait("tensor", V, hvv)
                tvv = p.op("tensor", T, lambda e, b=bk, c=c, hb=hsu[c % 2]: e.matmul(
                    b.t[:, :], self._mm(hb[:, :]),
                    self._mm(zwin[:, c * 512:(c + 1) * 512]), start=True, stop=True))
                hv[c] = tvv
                sv = p.op("scalar", S, lambda e, b=bk, c=c: e.copy(
                    zwin[:, c * 512:(c + 1) * 512], b.t[:, :]), wait=[(T, tvv)])
                bk.free = (S, sv)
            p.wait("vector", S, p.cur(S))
            spare = [64, 65]
            for st in range(6):
                bit = 1 << st
                newslot = list(slot)
                for c in range(IC):
                    if c & bit:
                        continue
                    u, vtx = c, c | bit
                    su_, sv_ = slot[u], slot[vtx]
                    du, dv = spare[0], spare[1]
                    p.op("vector", V, lambda e, a=su_, b=sv_, d=du: e.tensor_add(
                        zwin[:, d * 512:(d + 1) * 512], zwin[:, a * 512:(a + 1) * 512],
                        zwin[:, b * 512:(b + 1) * 512]))
                    p.op("vector", V, lambda e, a=su_, b=sv_, d=dv: e.tensor_sub(
                        zwin[:, d * 512:(d + 1) * 512], zwin[:, a * 512:(a + 1) * 512],
                        zwin[:, b * 512:(b + 1) * 512]))
                    newslot[u], newslot[vtx] = du, dv
                    spare = [su_, sv_]
                slot = newslot
            y_done = p.cur(V)
            p.wait("tensor", V, y_done)
            for (oc0, ogn) in og_list:
                tv = None
                for c in range(IC):
                    wvd = p.dma("sync", WDs[c % 2], wdb[c % 2][:, 0:ogn * 128],
                                self.wdown[c * 128:(c + 1) * 128,
                                           oc0 * 128:(oc0 + ogn) * 128],
                                wait=[(T, wd_free.get(c % 2))]
                                if wd_free.get(c % 2) else [])
                    p.waitt("tensor", wvd)
                    for i in range(ogn):
                        bk = self.banks[i]
                        if c == 0:
                            self.acq(bk)
                        tv = p.op("tensor", T if i == ogn - 1 else None,
                                  lambda e, b=bk, c=c, i=i, w=wdb[c % 2],
                                  sc=slot[c]: e.matmul(
                            b.t[:, :], self._mm(w[:, i * 128:(i + 1) * 128]),
                            self._mm(zwin[:, sc * 512:(sc + 1) * 512]),
                            start=(c == 0), stop=(c == IC - 1)))
                    wd_free[c % 2] = tv
                odma = {}
                vprev = {}
                for i in range(ogn):
                    oc = oc0 + i
                    lx = p.dma("sync", XT6s[i % 2], x2t[i % 2][:, :],
                               self.x2_s[oc * 128:(oc + 1) * 128, cs],
                               wait=[(V, vprev.get(i - 2))] if i >= 2 else [])
                    v1 = p.op("vector", V, lambda e, b=self.banks[i], i=i, oc=oc:
                              e.scalar_tensor_tensor(
                        ob[i % 2][:, :], b.t[:, :], self.scdp[:, oc:oc + 1],
                        x2t[i % 2][:, :], ALU.mult, ALU.add),
                        wait=[(T, tv), lx, odma.get(i - 2)])
                    vprev[i] = v1
                    self.banks[i].free = (V, v1)
                    odma[i] = p.dma("sync", OUTs[i % 2],
                                    self.outs[oc * 128:(oc + 1) * 128, cs],
                                    ob[i % 2][:, :], wait=[(V, v1)])


def _hadamard(n):
    h = np.array([[1.0]], dtype=np.float64)
    while h.shape[0] < n:
        h = np.block([[h, h], [h, -h]])
    return np.ascontiguousarray(h.astype(np.float32))


_ROPE_CACHE = {}


def _rope_tables(pos):
    if "t" not in _ROPE_CACHE:
        import jax
        import jax.numpy as jnp
        cpu = jax.devices("cpu")[0]
        with jax.default_device(cpu):
            inv = 1.0 / (500000.0 ** (jnp.arange(0, 64, 2, dtype=jnp.float32) / 64))
            ang = jnp.outer(jnp.arange(4096, dtype=jnp.float32), inv)
            _ROPE_CACHE["t"] = (np.asarray(jnp.cos(ang), np.float32),
                                np.asarray(jnp.sin(ang), np.float32))
    cf, sf = _ROPE_CACHE["t"]
    asc = np.ascontiguousarray
    ch = asc(cf[pos].T)
    sh = asc(sf[pos].T)
    zz = np.zeros_like(ch)
    return ch, sh, zz, zz


def prepare_in_maps(L0_attn_x, L0_attn_w_q_down, L0_attn_q_norm_w, L0_attn_w_q_up,
                    L0_attn_w_kv_down, L0_attn_kv_norm_w, L0_attn_w_kv_up, L0_attn_w_o,
                    L0_ffn_SU_ug, L0_ffn_SU_dp, L0_ffn_Wscale_ug, L0_ffn_Wscale_dp,
                    L0_ffn_W_up, L0_ffn_W_gate, L0_ffn_W_down):
    x = np.asarray(L0_attn_x, dtype=np.float32)

    def f32(a):
        return np.ascontiguousarray(np.asarray(a, dtype=np.float32))

    wqd_b = f32(np.asarray(L0_attn_w_q_down, np.float32)
                .reshape(QC, 128, FC, 128).transpose(2, 3, 0, 1)
                .reshape(FC * 128, QC * 128))
    perm = []
    for h in range(H):
        perm.extend(range(h * 192, h * 192 + 128))
    for g in range(4):
        for hh in range(4):
            hd = g * 4 + hh
            perm.extend((hd * 192 + 128 + 2 * np.arange(32)).tolist())
    for g in range(4):
        for hh in range(4):
            hd = g * 4 + hh
            perm.extend((hd * 192 + 129 + 2 * np.arange(32)).tolist())
    wqu_b = f32(np.asarray(L0_attn_w_q_up, np.float32)[np.asarray(perm)]
                .reshape(24, 128, QC, 128).transpose(2, 3, 0, 1)
                .reshape(QC * 128, 24 * 128))
    kperm = np.concatenate([np.arange(512), 512 + 2 * np.arange(32),
                            513 + 2 * np.arange(32)])
    wkd_b = f32(np.asarray(L0_attn_w_kv_down, np.float32)[kperm].T
                .reshape(FC * 128, 576))
    wku = np.asarray(L0_attn_w_kv_up, np.float32).reshape(H, 2, 128, KC, 128)
    wkuk_b = f32(wku[:, 0].transpose(0, 3, 2, 1).reshape(H * 128, KC * 128))
    wkuv_b = f32(wku[:, 1].transpose(2, 3, 0, 1).reshape(KC * 128, 2048))
    wo_b = f32(np.asarray(L0_attn_w_o, np.float32)
               .reshape(16, 128, 16, 128).transpose(0, 3, 2, 1)
               .reshape(16 * 128, 16 * 128))
    wup_b = f32(np.asarray(L0_ffn_W_up, np.float32)
                .reshape(IC, 128, FC, 128).transpose(0, 3, 2, 1)
                .reshape(IC * 128, FC * 128))
    wgt_b = f32(np.asarray(L0_ffn_W_gate, np.float32)
                .reshape(IC, 128, FC, 128).transpose(0, 3, 2, 1)
                .reshape(IC * 128, FC * 128))
    wdn_b = f32(np.asarray(L0_ffn_W_down, np.float32)
                .reshape(16, 128, IC, 128).transpose(2, 3, 0, 1)
                .reshape(IC * 128, 16 * 128))
    gq_b = f32(np.asarray(L0_attn_q_norm_w, np.float32).reshape(QC, 128).T)
    gkv_b = f32(np.asarray(L0_attn_kv_norm_w, np.float32).reshape(KC, 128).T)
    suug_b = f32(np.asarray(L0_ffn_SU_ug, np.float32).reshape(FC, 128).T)
    sudp_b = f32(np.asarray(L0_ffn_SU_dp, np.float32).reshape(IC, 128).T)
    wsc = np.asarray(L0_ffn_Wscale_ug, np.float32)
    scup_b = f32(wsc[:INTER].reshape(IC, 128).T)
    scgt_b = f32(wsc[INTER:].reshape(IC, 128).T)
    scdp_b = f32(np.asarray(L0_ffn_Wscale_dp, np.float32).reshape(FC, 128).T)

    onesd = np.ones((128, 128), np.float32)
    constsd = np.tile(np.array([EPS_LN, EPS_RMS, 0.0, 1.0], np.float32), (128, 1))
    h128d = _hadamard(128)
    kk = np.arange(128)[:, None]
    qq = np.arange(512)[None, :]
    md = np.concatenate(
        [np.where(ks * 128 + kk > qq, np.float32(NEGM), np.float32(0.0))
         for ks in range(4)], axis=1).astype(np.float32)

    in_maps = []
    core_meta = []
    for core in range(8):
        sq, j = core // 4, core % 4
        A, B = j, 7 - j
        xs = x[sq].T
        remaining = [g for g in range(8) if g != j]
        sperm = remaining[0:3] + [j] + remaining[3:7] + [7 - j]
        xseq_b = f32(np.concatenate([xs[:, g * 512:(g + 1) * 512] for g in sperm],
                                    axis=1))
        xown_b = f32(np.concatenate([xs[:, A * 512:(A + 1) * 512],
                                     xs[:, B * 512:(B + 1) * 512]], axis=1))
        pos_own = np.concatenate([np.arange(A * 512, (A + 1) * 512),
                                  np.arange(B * 512, (B + 1) * 512)])
        cq_t, sq_t, cq_l, sq_l = _rope_tables(pos_own)
        cosq_b = f32(np.tile(cq_t, (4, 1)))
        sinq_b = f32(np.tile(sq_t, (4, 1)))
        cosql_b = f32(np.tile(cq_l, (4, 1)))
        sinql_b = f32(np.tile(sq_l, (4, 1)))
        pos_seq = np.concatenate([np.arange(g * 512, (g + 1) * 512) for g in sperm])
        ck_t, sk_t, ck_l, sk_l = _rope_tables(pos_seq)
        deadt = np.zeros((128, 13), np.float32)
        for sl in range(4):
            g = sperm[sl]
            if not ((g < j) or (sl == 3)):
                deadt[:, sl] = NEGM
        for sl in range(9):
            g = sperm[sl]
            if not ((sl == 8) or (g < B and sl < 8)):
                deadt[:, 4 + sl] = NEGM
        in_maps.append({
            "xseq": xseq_b, "xown": xown_b, "cosq": cosq_b, "sinq": sinq_b,
            "cosk": f32(ck_t), "sink": f32(sk_t), "coskl": f32(ck_l),
            "sinkl": f32(sk_l), "cosql": cosql_b, "sinql": sinql_b,
            "deadt": deadt, "mdiagd": md,
            "onesd": onesd, "h128d": h128d, "gqd": gq_b, "gkvd": gkv_b,
            "suugd": suug_b, "sudpd": sudp_b, "scupd": scup_b, "scgtd": scgt_b,
            "scdpd": scdp_b, "constsd": constsd, "wqd": wqd_b, "wqu": wqu_b,
            "wkd": wkd_b, "wkuk": wkuk_b, "wkuv": wkuv_b, "wod": wo_b,
            "wup": wup_b, "wgate": wgt_b, "wdown": wdn_b,
        })
        core_meta.append((sq, A, B))
    return in_maps, core_meta


def kernel(**inputs):
    in_maps, core_meta = prepare_in_maps(**inputs)
    nc = Builder(bass.Bass("TRN2", target_bir_lowering=False,
                       detect_race_conditions=False)).build()
    res = run_bass_kernel_spmd(nc, in_maps, core_ids=list(range(8)))

    out = np.empty((2, 4096, D), np.float32)
    for core in range(8):
        sq, A, B = core_meta[core]
        o = np.asarray(res.results[core]["outs"])
        out[sq, A * 512:(A + 1) * 512] = o[:, 0:512].T
        out[sq, B * 512:(B + 1) * 512] = o[:, 512:1024].T
    return out
